# version 01
# baseline (speedup 1.0000x reference)
import numpy as np
import ml_dtypes
from contextlib import ExitStack
import concourse.bass as bass
import concourse.mybir as mybir
from concourse.bass_utils import run_bass_kernel_spmd

F32 = mybir.dt.float32
BF16 = mybir.dt.bfloat16
I32 = mybir.dt.int32
AF = mybir.ActivationFunctionType
ALU = mybir.AluOpType
AX = mybir.AxisListType.X
MAGIC = 12582912.0

S, Dm = 4096, 1024
NBLK, BLK = 8, 256
NW, WIN = 8, 512
EPS = 1e-5
GROUPS = [[0, 1], [2, 3], [4, 5], [6, 7]]

PHASE_END = 99


class Sem:
    def __init__(self, h):
        self.h = h
        self.v = 0

    def bump(self, k=1):
        self.v += k
        return self.v


class KB:
    def __init__(self, nc):
        self.nc = nc
        self.ops = {e: [] for e in ("tensor", "vector", "scalar", "gpsimd", "sync")}

    def emit(self, eng, fn):
        self.ops[eng].append(fn)

    def soft_barrier(self, sems):
        for eng in ("tensor", "vector", "scalar", "sync", "gpsimd"):
            for sm in sems:
                self.wait(eng, sm)

    def wait(self, eng, sem, v=None):
        vv = sem.v if v is None else v
        if vv <= 0:
            return
        self.emit(eng, lambda e, h=sem.h, w=vv: e.wait_ge(h, w))

    def dma(self, q, out, in_, sem):
        sem.bump(16)
        self.emit(q, lambda e, o=out, i=in_, h=sem.h: e.dma_start(out=o, in_=i).then_inc(h, 16))

    def dma_slow(self, q, out, in_, sem):
        sem.bump(16)
        self.emit(q, lambda e, o=out, i=in_, h=sem.h: e.dma_start(out=o, in_=i, allow_slow_non_contiguous=True).then_inc(h, 16))

    def v(self, name, *a, sem=None, **kw):
        self._c("vector", name, a, kw, sem)

    def s(self, name, *a, sem=None, **kw):
        self._c("scalar", name, a, kw, sem)

    def t(self, name, *a, sem=None, **kw):
        self._c("tensor", name, a, kw, sem)

    def g(self, name, *a, sem=None, **kw):
        self._c("gpsimd", name, a, kw, sem)

    def _c(self, eng, name, a, kw, sem):
        if sem is not None:
            sem.bump(1)
            self.emit(eng, lambda e, n=name, aa=a, kk=kw, h=sem.h: getattr(e, n)(*aa, **kk).then_inc(h, 1))
        else:
            self.emit(eng, lambda e, n=name, aa=a, kk=kw: getattr(e, n)(*aa, **kk))


def build(role_any, debug_keys=(), ph=4):
    nc = bass.Bass("TRN2", target_bir_lowering=False, debug=False)
    kb = KB(nc)
    es = ExitStack()
    E = es.enter_context

    def sem(name):
        return Sem(E(nc.semaphore(name)))

    def inp(name, shape, dt):
        return nc.dram_tensor(name, list(shape), dt, kind="ExternalInput").ap()

    xt = inp("xt", (Dm, 2048), F32)
    wq = inp("wq", (Dm, Dm), F32)
    wk = inp("wk", (Dm, Dm), F32)
    wv = inp("wv", (Dm, Dm), F32)
    wo = inp("wo", (64, 16 * Dm), F32)
    w_inp = inp("w_inp", (Dm, 1312), F32)
    w_hp = inp("w_hp", (1056, 40), F32)
    w_outp = inp("w_outp", (512, Dm), F32)
    ffn_pT = inp("ffn_pT", (Dm, 256), I32)
    ffn_bias = inp("ffn_bias", (128, 8), F32)
    conv_wv = inp("conv_wv", (128, 24), F32)
    conv_bv = inp("conv_bv", (128, 6), F32)
    dtbias_rep = inp("dtbias_rep", (128, 1), F32)
    gamma_rep = inp("gamma_rep", (128, 12), F32)
    alog_my = inp("alog_my", (8, 1), F32)
    d_my = inp("d_my", (128, 4), F32)
    normw_my = inp("normw_my", (128, 4), F32)
    wscale = inp("wscale", (1, 1), F32)
    c_ident = inp("c_ident", (128, 128), BF16)
    c_ones = inp("c_ones", (128, 128), BF16)
    c_tri = inp("c_tri", (128, 128), BF16)
    c_diagmask = inp("c_diagmask", (128, 1024), BF16)
    c_id16 = inp("c_id16", (17, 16), F32)
    c_sel = inp("c_sel", (16, 8), F32)
    c_id8 = inp("c_id8", (8, 8), F32)
    c_identf = inp("c_identf", (128, 128), F32)
    c_onesrow = inp("c_onesrow", (1, 4096), F32)
    c_negones = inp("c_negones", (1, 4096), F32)
    c_invden = inp("c_invden", (40, S), F32)
    c_iota12 = inp("c_iota12", (128, 336), BF16)
    c_rmask = inp("c_rmask", (128, 2), F32)

    out_mine = nc.dram_tensor("out_mine", [Dm, 2048], F32, kind="ExternalOutput").ap()

    dbg_aps = {}

    def dbg(name, shape, dt=F32):
        if name in debug_keys:
            dbg_aps[name] = nc.dram_tensor("dbg_" + name, list(shape), dt, kind="ExternalOutput").ap()
            return dbg_aps[name]
        return None

    e0_in = nc.dram_tensor("e0_in", [4096, 1024], BF16, kind="Internal")
    e0_out = nc.dram_tensor("e0_out", [8192, 1024], BF16, kind="Internal")
    e1_in = nc.dram_tensor("e1_in", [1024, 2048], F32, kind="Internal")
    e1_out = nc.dram_tensor("e1_out", [2048, 2048], F32, kind="Internal")
    e2_in = nc.dram_tensor("e2_in", [2050, 2048], F32, kind="Internal")
    e2_out = nc.dram_tensor("e2_out", [4100, 2048], F32, kind="Internal")
    x1d = nc.dram_tensor("x1d", [1024, 2048], F32, kind="Internal")

    s_ld = sem("s_ld")
    s_cv = sem("s_cv")
    s_sc = sem("s_sc")
    s_pe = sem("s_pe")
    s_st = sem("s_st")
    s_cc = sem("s_cc")
    s_cc1 = sem("s_cc1")
    s_cc2 = sem("s_cc2")
    s_po = sem("s_po")
    s_out = sem("s_out")

    sb_ident = E(nc.sbuf_tensor("sb_ident", [128, 128], BF16))
    sb_ones = E(nc.sbuf_tensor("sb_ones", [128, 128], BF16))
    sb_tri = E(nc.sbuf_tensor("sb_tri", [128, 128], BF16))
    sb_dmask = E(nc.sbuf_tensor("sb_dmask", [128, 1024], BF16))
    sb_id16 = E(nc.sbuf_tensor("sb_id16", [17, 16], F32))
    sb_vecs = E(nc.sbuf_tensor("sb_vecs", [128, 64], F32))
    sb_cwv = E(nc.sbuf_tensor("sb_cwv", [128, 24], F32))
    sb_alog = E(nc.sbuf_tensor("sb_alog", [8, 2], F32))
    sb_wscale = E(nc.sbuf_tensor("sb_wscale", [1, 1], F32))
    sb_epsb = E(nc.sbuf_tensor("sb_epsb", [128, 1], F32))
    sb_onesf = E(nc.sbuf_tensor("sb_onesf", [128, 512], F32))
    V_BIAS, V_CONVB, V_D, V_NORMW, V_DTB, V_GAMMA = 0, 8, 14, 18, 22, 24

    rows1 = E(nc.sbuf_tensor("rows1", [128, 2048], F32))
    rowbf = E(nc.sbuf_tensor("rowbf", [128, 2048], BF16))
    ROWP = (0, 32, 64, 96)

    def row(t, i, off, width):
        return t[ROWP[i]:ROWP[i] + 1, off:off + width]

    sb_csum = E(nc.sbuf_tensor("sb_csum", [128, 4424], BF16))
    CS_Q, CS_K, CS_V, CS_INP, CS_HP = 0, 1024, 2048, 3072, 4384

    sb_ffnw = E(nc.sbuf_tensor("sb_ffnw", [128, 8 * 1024], BF16))

    P0 = E(nc.psum_tensor("P0", [128, 1024], F32))
    P1 = E(nc.psum_tensor("P1", [128, 1024], F32))
    P2 = E(nc.psum_tensor("P2", [128, 1024], F32))
    P3 = E(nc.psum_tensor("P3", [128, 1024], F32))

    kb.dma("sync", sb_ident[:], c_ident, s_ld)
    kb.dma("sync", sb_ones[:], c_ones, s_ld)
    kb.dma("sync", sb_tri[:], c_tri, s_ld)
    kb.dma("sync", sb_dmask[:], c_diagmask, s_ld)
    kb.dma("sync", sb_id16[:], c_id16, s_ld)
    kb.dma("sync", sb_vecs[:, 0:8], ffn_bias, s_ld)
    kb.dma("sync", sb_vecs[:, 8:14], conv_bv, s_ld)
    kb.dma("sync", sb_vecs[:, 14:18], d_my, s_ld)
    kb.dma("sync", sb_vecs[:, 18:22], normw_my, s_ld)
    kb.dma("sync", sb_vecs[:, 22:23], dtbias_rep, s_ld)
    kb.dma("sync", sb_vecs[:, 24:36], gamma_rep, s_ld)
    kb.dma("sync", sb_cwv[:], conv_wv, s_ld)
    kb.dma("sync", sb_alog[:, 0:1], alog_my, s_ld)
    kb.dma("sync", sb_wscale[:], wscale, s_ld)
    LD_CONST = s_ld.v
    kb.v("memset", sb_epsb[:], EPS)
    kb.v("memset", sb_onesf[:], 1.0)

    kb.wait("scalar", s_ld, LD_CONST)
    kb.s("activation", out=sb_alog[0:8, 1:2], in_=sb_alog[0:8, 0:1], func=AF.Exp, sem=s_sc)
    A_NEG_AT = s_sc.v
    kb.wait("vector", s_sc, A_NEG_AT)
    kb.v("tensor_scalar_mul", sb_alog[0:8, 1:2], sb_alog[0:8, 1:2], -1.0, sem=s_cv)
    A_READY = s_cv.v

    es_attn = ExitStack()
    EA = es_attn.enter_context
    qres = EA(nc.sbuf_tensor("qres", [128, 8 * 2048], BF16))
    wo_bA = EA(nc.sbuf_tensor("wo_bA", [64, 8 * 1024], BF16))
    wo_bB = EA(nc.sbuf_tensor("wo_bB", [64, 8 * 1024], BF16))

    es_1a = ExitStack()
    E1 = es_1a.enter_context
    wcv = E1(nc.sbuf_tensor("wcv", [128, 2 * 1024], F32))
    wq_b = E1(nc.sbuf_tensor("wq_b", [128, 8 * 1024], BF16))
    wk_b = E1(nc.sbuf_tensor("wk_b", [128, 8 * 1024], BF16))
    wv_b = E1(nc.sbuf_tensor("wv_b", [128, 8 * 1024], BF16))
    xblk = E1(nc.sbuf_tensor("xblk", [128, 2 * 2048], F32))
    xb16 = E1(nc.sbuf_tensor("xb16", [128, 2048], BF16))
    xsq = E1(nc.sbuf_tensor("xsq", [128, 2048], BF16))
    xn1 = E1(nc.sbuf_tensor("xn1", [128, 2048], BF16))
    kst = E1(nc.sbuf_tensor("kst", [128, 2048], BF16))
    vst = E1(nc.sbuf_tensor("vst", [128, 2 * 1024], BF16))

    s_wld = [sem("s_wld0"), sem("s_wld1")]
    s_wcv_v = sem("s_wcv_v")
    s_wcv_g = sem("s_wcv_g")
    conv_hist = []

    def load_convert_w(wsrc, wdstb, ncols, eng="vector"):
        kchunks = wsrc.shape[0] // 128
        wv_dram = wsrc.rearrange("(c p) n -> p c n", p=128)
        for kc in range(kchunks):
            par = load_convert_w.par = getattr(load_convert_w, "par", 0)
            buf = wcv[:, (par % 2) * 1024:(par % 2) * 1024 + ncols]
            prev = [h for h in conv_hist if h[2] == par % 2]
            if prev:
                kb.wait("sync", prev[-1][0], prev[-1][1])
            kb.dma("sync", buf, wv_dram[:, kc, :], s_wld[par % 2])
            kb.wait(eng, s_wld[par % 2])
            cs = s_wcv_v if eng == "vector" else s_wcv_g
            kb._c(eng, "tensor_copy", (), dict(out=wdstb[:, kc * ncols:(kc + 1) * ncols], in_=buf), cs)
            conv_hist.append((cs, cs.v, par % 2))
            load_convert_w.par = par + 1

    load_convert_w(wq, wq_b, 1024, "vector")
    load_convert_w(wk, wk_b, 1024, "gpsimd")
    load_convert_w(wv, wv_b, 1024, "vector")
    wo_d = wo.rearrange("p (c n) -> p c n", c=16)
    for kc in range(16):
        par = load_convert_w.par = getattr(load_convert_w, "par", 0)
        buf = wcv[0:64, (par % 2) * 1024:(par % 2) * 1024 + 1024]
        prev = [h for h in conv_hist if h[2] == par % 2]
        if prev:
            kb.wait("sync", prev[-1][0], prev[-1][1])
        kb.dma("sync", buf, wo_d[0:64, kc, :], s_wld[par % 2])
        kb.wait("gpsimd", s_wld[par % 2])
        dstw = wo_bA if kc < 8 else wo_bB
        kb._c("gpsimd", "tensor_copy", (), dict(out=dstw[0:64, (kc % 8) * 1024:(kc % 8 + 1) * 1024], in_=buf), s_wcv_g)
        conv_hist.append((s_wcv_g, s_wcv_g.v, par % 2))
        load_convert_w.par = par + 1
    W_ATTN_READY_VV = s_wcv_v.v
    W_ATTN_READY_VG = s_wcv_g.v

    s_cs = sem("s_cs")
    kb.wait("tensor", s_wcv_v, W_ATTN_READY_VV)
    kb.wait("tensor", s_wcv_g, W_ATTN_READY_VG)
    kb.wait("tensor", s_ld, LD_CONST)
    for wi, (wb, coff) in enumerate(((wq_b, CS_Q), (wk_b, CS_K), (wv_b, CS_V))):
        for oc in range(2):
            for kc in range(8):
                kb.t("matmul", P3[0:1, 0:512], sb_ones[:, 0:1],
                     wb[:, kc * 1024 + oc * 512: kc * 1024 + (oc + 1) * 512],
                     start=(kc == 0), stop=(kc == 7),
                     sem=s_pe if kc == 7 else None)
            v_pe = s_pe.v
            kb.wait("vector", s_pe, v_pe)
            kb.v("tensor_scalar", sb_csum[0:1, coff + oc * 512: coff + (oc + 1) * 512],
                 P3[0:1, 0:512], -1.0, None, op0=ALU.mult, sem=s_cv)
            kb.wait("tensor", s_cv, s_cv.v)
    CS_READY = s_cv.v

    s_x = sem("s_x")
    s_ev = sem("s_ev")
    s_stg = sem("s_stg")

    for i in range(NBLK):
        bb = i % 2
        xcur = xblk[:, bb * 2048:(bb + 1) * 2048]
        kb.dma("sync", xcur.rearrange("p (c t) -> p c t", c=8),
               xt.rearrange("(c p) t -> p c t", p=128)[:, :, i * 256:(i + 1) * 256], s_x)
        XLD = s_x.v
        kb.wait("vector", s_x, XLD)
        kb.v("tensor_copy", out=xb16[:], in_=xcur, sem=s_cv)
        XB16 = s_cv.v
        kb.wait("scalar", s_x, XLD)
        kb.s("activation", out=xsq[:], in_=xcur, func=AF.Square, sem=s_sc)
        XSQ = s_sc.v
        kb.wait("tensor", s_cv, XB16)
        kb.wait("tensor", s_sc, XSQ)
        for kc in range(8):
            kb.t("matmul", P3[0:1, 0:256], sb_ones[:, 0:1], xb16[:, kc * 256:(kc + 1) * 256],
                 start=(kc == 0), stop=(kc == 7))
        for kc in range(8):
            kb.t("matmul", P3[0:1, 256:512], sb_ones[:, 0:1], xsq[:, kc * 256:(kc + 1) * 256],
                 start=(kc == 0), stop=(kc == 7), sem=(s_pe if kc == 7 else None))
        STATS = s_pe.v
        R_MU, R_M2, R_MU2, R_VAR, R_LNV, R_RHO, R_S = (k * 256 for k in range(7))
        kb.wait("vector", s_pe, STATS)
        kb.v("tensor_scalar_mul", rows1[0:1, R_MU:R_MU + 256], P3[0:1, 0:256], 1.0 / 1024.0)
        kb.v("tensor_scalar_mul", rows1[0:1, R_M2:R_M2 + 256], P3[0:1, 256:512], 1.0 / 1024.0)
        kb.v("tensor_mul", rows1[0:1, R_MU2:R_MU2 + 256], rows1[0:1, R_MU:R_MU + 256], rows1[0:1, R_MU:R_MU + 256])
        kb.v("tensor_sub", rows1[0:1, R_VAR:R_VAR + 256], rows1[0:1, R_M2:R_M2 + 256], rows1[0:1, R_MU2:R_MU2 + 256], sem=s_cv)
        VAR = s_cv.v
        kb.wait("tensor", s_cv, VAR)
        kb.wait("scalar", s_cv, VAR)
        kb.s("activation", out=rows1[0:1, R_LNV:R_LNV + 256], in_=rows1[0:1, R_VAR:R_VAR + 256], func=AF.Ln, bias=sb_epsb[0:1, 0:1])
        kb.s("activation", out=rows1[0:1, R_RHO:R_RHO + 256], in_=rows1[0:1, R_LNV:R_LNV + 256], func=AF.Exp, scale=-0.5, sem=s_sc)
        RHO_S = s_sc.v
        kb.wait("vector", s_sc, RHO_S)
        kb.v("tensor_mul", rows1[0:1, R_S:R_S + 256], rows1[0:1, R_MU:R_MU + 256], rows1[0:1, R_RHO:R_RHO + 256])
        kb.v("tensor_copy", out=rowbf[0:1, 0:256], in_=rows1[0:1, R_S:R_S + 256])
        kb.v("tensor_copy", out=rowbf[0:1, 256:512], in_=rows1[0:1, R_RHO:R_RHO + 256], sem=s_cv)
        ROWS_BF = s_cv.v
        kb.wait("tensor", s_cv, ROWS_BF)
        kb.t("matmul", P2[:, 0:256], sb_ones[0:1, :], rowbf[0:1, 256:512], start=True, stop=True, sem=s_pe)
        RHOB = s_pe.v
        kb.wait("vector", s_pe, RHOB)
        kb.v("tensor_tensor", out=xn1[:].rearrange("p (c t) -> p c t", c=8), in0=xcur.rearrange("p (c t) -> p c t", c=8),
             in1=P2[:, 0:256].rearrange("p (o t) -> p o t", o=1).broadcast_to([128, 8, 256]), op=ALU.mult, sem=s_cv)
        XN1 = s_cv.v
        kb.wait("tensor", s_cv, XN1)

        stg0 = s_stg.v
        kb.wait("scalar", s_stg, stg0)
        for proj, (wb, coff) in enumerate(((wq_b, CS_Q), (wk_b, CS_K))):
            for oc in range(8):
                pb = (P0 if oc < 4 else P1)
                pslice = pb[:, (oc % 4) * 256:(oc % 4 + 1) * 256]
                for kc in range(8):
                    kb.t("matmul", pslice, wb[:, kc * 1024 + oc * 128: kc * 1024 + oc * 128 + 128],
                         xn1[:, kc * 256:(kc + 1) * 256], start=(kc == 0), stop=False)
                kb.t("matmul", pslice, sb_csum[0:1, coff + oc * 128: coff + oc * 128 + 128],
                     rowbf[0:1, 0:256], start=False, stop=True,
                     sem=(s_pe if oc == 7 else None))
            PROJ = s_pe.v
            kb.wait("scalar", s_pe, PROJ)
            for oc in range(8):
                pb = (P0 if oc < 4 else P1)
                pslice = pb[:, (oc % 4) * 256:(oc % 4 + 1) * 256]
                dst = (qres[:, oc * 2048 + i * 256: oc * 2048 + (i + 1) * 256] if proj == 0
                       else kst[:, oc * 256:(oc + 1) * 256])
                kb.s("copy", out=dst, in_=pslice, sem=(s_ev if oc == 7 else None))
            EVAC = s_ev.v
            kb.wait("tensor", s_ev, EVAC)
        kb.wait("sync", s_ev, s_ev.v)
        kb.dma("sync",
               e0_in.ap()[0:2048, :].rearrange("(c p t2) n -> p c (t2 n)", p=128, t2=2)[:, :, i * 256:(i + 1) * 256],
               kst[:].rearrange("p (c t) -> p c t", c=8), s_stg)

        for tt in range(2):
            for half in range(2):
                pb = (P0 if half == 0 else P1)
                for kc in range(8):
                    kb.t("matmul", pb[:, 0:512],
                         xn1[:, kc * 256 + tt * 128: kc * 256 + tt * 128 + 128],
                         wv_b[:, kc * 1024 + half * 512: kc * 1024 + (half + 1) * 512],
                         start=(kc == 0), stop=False)
                kb.t("matmul", pb[:, 0:512], rowbf[0:1, tt * 128: tt * 128 + 128],
                     sb_csum[0:1, CS_V + half * 512: CS_V + (half + 1) * 512],
                     start=False, stop=True, sem=(s_pe if half == 1 else None))
            VPROJ = s_pe.v
            kb.wait("scalar", s_pe, VPROJ)
            kb.s("copy", out=vst[:, tt * 1024: tt * 1024 + 512], in_=P0[:, 0:512])
            kb.s("copy", out=vst[:, tt * 1024 + 512: (tt + 1) * 1024], in_=P1[:, 0:512], sem=s_ev)
            kb.wait("tensor", s_ev, s_ev.v)
        kb.wait("sync", s_ev, s_ev.v)
        kb.dma("sync", e0_in.ap()[2048 + i * 256: 2048 + (i + 1) * 256, :]
               .rearrange("(tt p) n -> p tt n", p=128),
               vst[:].rearrange("p (tt n) -> p tt n", tt=2), s_stg)
        kb.wait("sync", s_cv, XN1)

    d_wk = dbg("wk_b", (128, 8 * 1024), BF16)
    if d_wk is not None:
        kb.wait("sync", s_wcv_g, W_ATTN_READY_VG)
        kb.dma("sync", d_wk, wk_b[:], s_st)
    d_qf = dbg("qres_bf", (128, 8 * 2048), BF16)
    if d_qf is not None:
        kb.wait("sync", s_ev, s_ev.v)
        kb.dma("sync", d_qf, qres[:], s_st)
    d_e0 = dbg("e0_in", (4096, 1024), BF16)
    if d_e0 is not None:
        kb.wait("sync", s_stg, s_stg.v)
        kb.dma("sync", d_e0, e0_in.ap(), s_st)

    es_1a.close()

    s_x1 = sem("s_x1")
    kb.dma("gpsimd", x1d.ap(), xt, s_x1)
    X1_INIT = s_x1.v

    kb.wait("gpsimd", s_stg, s_stg.v)
    s_cc.bump(1)
    kb.emit("gpsimd", lambda e: e.collective_compute(
        "AllGather", ALU.bypass, replica_groups=GROUPS,
        ins=[e0_in.ap()], outs=[e0_out.ap()]).then_inc(s_cc.h))
    E0_DONE = s_cc.v

    fl = ExitStack()
    EF = fl.enter_context
    ktile = EF(nc.sbuf_tensor("ktile", [128, 2 * 1024], BF16))
    vtile = EF(nc.sbuf_tensor("vtile", [128, 2 * 1040], BF16))
    pbuf = EF(nc.sbuf_tensor("pbuf", [128, 2 * 1024], BF16))
    osb = EF(nc.sbuf_tensor("osb", [64, 2048], BF16))
    dbsb = EF(nc.sbuf_tensor("dbsb", [64, 512], F32))
    x1part = EF(nc.sbuf_tensor("x1part", [128, 2048], F32))
    pTt = EF(nc.sbuf_tensor("pTt", [128, 8 * 256], I32))
    pTs = EF(nc.sbuf_tensor("pTs", [128, 256], I32))
    pTf = EF(nc.sbuf_tensor("pTf", [128, 256], F32))

    OB = [P2.ap()[:, 0:512], P2.ap()[:, 512:1024], P3.ap()[:, 0:512], P3.ap()[:, 512:1024]]
    SB = [P0, P1]

    s_pv = sem("s_pv")
    s_fl = sem("s_fl")
    s_ex = sem("s_ex")
    s_kv = [sem("s_kv0"), sem("s_kv1")]

    kb.wait("vector", s_ev, s_ev.v)
    kb.v("memset", vtile.ap().rearrange("p (b kt h j) -> p b kt h j", b=2, kt=2, h=8)[:, :, :, :, 64:65], 1.0, sem=s_cv)
    VONES = s_cv.v
    kb.wait("sync", s_cv, VONES)

    s_up = sem("s_up")
    kb.dma("sync", pTt[:].rearrange("p (c n) -> p c n", c=8),
           ffn_pT.rearrange("(c p) n -> p c n", p=128), s_up)
    UPLD = s_up.v
    kb.wait("vector", s_up, UPLD)
    for ic in range(8):
        for sft in range(4):
            kb.v("tensor_scalar", pTs[:], pTt[:, ic * 256:(ic + 1) * 256], 2 * sft, 3,
                 op0=ALU.logical_shift_right, op1=ALU.bitwise_and)
            kb.v("tensor_copy", out=pTf[:], in_=pTs[:])
            kb.v("tensor_scalar", sb_ffnw[:, ic * 1024 + sft * 256: ic * 1024 + (sft + 1) * 256],
                 pTf[:], 1.0, None, op0=ALU.subtract,
                 sem=(s_up if (ic == 7 and sft == 3) else None))
    FFNW_RDY = s_up.v

    e0k = [e0_out.ap()[4096 * rk: 4096 * rk + 2048, :].rearrange(
        "(c p t2) n -> p c (t2 n)", p=128, t2=2) for rk in range(2)]

    kb.wait("sync", s_cc, E0_DONE)

    itc = 0
    step = 0
    bank_hist = {0: [], 1: []}
    phist = {0: [], 1: []}
    obank_free = [0, 0, 0, 0]

    QRW = [qres, qres]
    for w in range(2):
        for i in range(8):
            NKT = 2 * i + 2
            opv_done = [0] * 4
            for t in range(NKT):
                buf = itc % 2
                if itc >= 2:
                    kb.wait("sync", s_fl, itc - 1)
                rank, slot = t % 2, t // 2
                kb.dma("sync", ktile[:, buf * 1024:(buf + 1) * 1024].rearrange("p (c t) -> p c t", c=4),
                       e0k[rank][:, 4 * w: 4 * w + 4, slot * 256:(slot + 1) * 256], s_kv[buf])
                for kt_ in range(2):
                    kb.dma("sync",
                           vtile[:, buf * 1040 + kt_ * 520: buf * 1040 + (kt_ + 1) * 520]
                           .rearrange("p (h j) -> p h j", h=8)[:, :, 0:64],
                           e0_out.ap()[4096 * rank + 2048 + slot * 256 + kt_ * 128:
                                       4096 * rank + 2048 + slot * 256 + (kt_ + 1) * 128,
                                       512 * w: 512 * (w + 1)].rearrange("p (h j) -> p h j", h=8),
                           s_kv[buf])
                TILE_LD = s_kv[buf].v
                kb.wait("tensor", s_kv[buf], TILE_LD)
                for hp in range(4):
                    sbank = step % 2
                    Sp = SB[sbank]
                    if len(bank_hist[sbank]) >= 1:
                        kb.wait("tensor", s_ex, bank_hist[sbank][-1])
                    if t == 0:
                        kb.wait("tensor", s_sc, obank_free[hp])
                    for h01 in range(2):
                        h = 2 * hp + h01
                        c, b64 = h // 2, (h % 2) * 64
                        for kt in range(2):
                            kb.t("matmul", Sp[:, h01 * 512 + kt * 256: h01 * 512 + (kt + 1) * 256],
                                 ktile[b64:b64 + 64, buf * 1024 + c * 256 + kt * 128: buf * 1024 + c * 256 + (kt + 1) * 128],
                                 qres[b64:b64 + 64, (4 * w + c) * 2048 + i * 256: (4 * w + c) * 2048 + (i + 1) * 256],
                                 start=True, stop=True,
                                 sem=(s_pe if (h01 == 1 and kt == 1) else None))
                    QK_D = s_pe.v
                    phb = step % 2
                    phalf = pbuf[:, phb * 1024:(phb + 1) * 1024]
                    if len(phist[phb]) >= 1:
                        kb.wait("scalar", s_pv, phist[phb][-1])
                    kb.wait("scalar", s_pe, QK_D)
                    kb.s("activation", out=phalf, in_=Sp[:], func=AF.Exp, scale=0.125, sem=s_ex)
                    EXP_D = s_ex.v
                    bank_hist[sbank].append(EXP_D)
                    d_ph = dbg("ph_w%d_i%d_hp%d_t%d" % (w, i, hp, t), (128, 1024), BF16) if (w == 0 and i == 0) else None
                    d_s = dbg("s_w%d_i%d_hp%d_t%d" % (w, i, hp, t), (128, 1024), F32) if (w == 0 and i == 0) else None
                    if False and d_s is not None:
                        pass
                    masked = (t >= NKT - 2)
                    if masked:
                        win = t - (NKT - 2)
                        kb.wait("vector", s_ex, EXP_D)
                        for h01 in range(2):
                            kb.v("tensor_tensor", out=phalf[:, h01 * 512:(h01 + 1) * 512],
                                 in0=phalf[:, h01 * 512:(h01 + 1) * 512],
                                 in1=sb_dmask[:, win * 512:(win + 1) * 512],
                                 op=ALU.mult, sem=(s_cv if h01 == 1 else None))
                        MASK_D = s_cv.v
                        kb.wait("tensor", s_cv, MASK_D)
                    if d_ph is not None:
                        if masked:
                            kb.wait("sync", s_cv, MASK_D)
                        kb.dma("sync", d_ph, phalf, s_st)
                        kb.wait("tensor", s_st, s_st.v)
                    kb.wait("tensor", s_ex, EXP_D)
                    for kt in range(2):
                        for h01 in range(2):
                            h = 2 * hp + h01
                            kb.t("matmul", OB[hp][0:65, h01 * 256:(h01 + 1) * 256],
                                 vtile[:, buf * 1040 + kt * 520 + h * 65: buf * 1040 + kt * 520 + h * 65 + 65],
                                 phalf[:, h01 * 512 + kt * 256: h01 * 512 + (kt + 1) * 256],
                                 start=(t == 0 and kt == 0 and h01 == 0),
                                 stop=(t == NKT - 1 and kt == 1 and h01 == 1),
                                 skip_group_check=True,
                                 sem=(s_pv if (kt == 1 and h01 == 1) else None))
                    PV_D = s_pv.v
                    phist[phb].append(PV_D)
                    if t == NKT - 1:
                        opv_done[hp] = PV_D
                    if hp == 3:
                        s_fl.bump(1)
                        kb.emit("tensor", lambda e, h=s_fl.h, vv=None: e.sem_inc(h, 1))
                    step += 1
                itc += 1
            if bank_hist[0]:
                kb.wait("tensor", s_ex, bank_hist[0][-1])
            for hp in range(4):
                kb.wait("vector", s_pv, max(opv_done))
                kb.v("reciprocal", out=rows1[0:1, 0:512], in_=OB[hp][64:65, 0:512])
                kb.v("tensor_copy", out=rowbf[0:1, 0:512], in_=rows1[0:1, 0:512], sem=s_cv)
                RECIP_D = s_cv.v
                kb.wait("tensor", s_cv, RECIP_D)
                kb.t("matmul", P0[0:64, 0:512], sb_ones[0:1, 0:64], rowbf[0:1, 0:512],
                     start=True, stop=True, sem=s_pe)
                DB_D = s_pe.v
                kb.wait("scalar", s_pe, DB_D)
                kb.s("copy", out=dbsb[:], in_=P0[0:64, 0:512], sem=s_sc)
                DBS_D = s_sc.v
                kb.wait("vector", s_sc, DBS_D)
                kb.v("tensor_tensor", out=osb[0:64, hp * 512: (hp + 1) * 512],
                     in0=OB[hp][0:64, 0:512], in1=dbsb[:], op=ALU.mult, sem=s_cv)
                OSB_D = s_cv.v
                kb.wait("tensor", s_sc, DBS_D)
            OSB_ALL = s_cv.v
            kb.wait("tensor", s_cv, OSB_ALL)
            wot = wo_bA if w == 0 else wo_bB
            for oc in range(8):
                ob = OB[oc // 2]
                osl = ob[0:128, (oc % 2) * 256:(oc % 2 + 1) * 256]
                for h in range(8):
                    kb.t("matmul", osl, wot[0:64, h * 1024 + oc * 128: h * 1024 + oc * 128 + 128],
                         osb[0:64, h * 256:(h + 1) * 256],
                         start=(h == 0 and oc % 2 == 0), stop=(h == 7 and oc % 2 == 1),
                         skip_group_check=True,
                         sem=(s_pe if (h == 7 and oc == 7) else None))
            WO_D = s_pe.v
            kb.wait("scalar", s_pe, WO_D)
            for oc in range(8):
                ob = OB[oc // 2]
                osl = ob[0:128, (oc % 2) * 256:(oc % 2 + 1) * 256]
                kb.s("copy", out=x1part[:, oc * 256:(oc + 1) * 256], in_=osl,
                     sem=(s_sc if oc == 7 else None))
            WOE_D = s_sc.v
            for b_ in range(4):
                obank_free[b_] = WOE_D
            kb.wait("gpsimd", s_sc, WOE_D)
            kb.wait("gpsimd", s_x1, X1_INIT)
            s_x1.bump(16)
            kb.emit("gpsimd", lambda e, dst=x1d.ap().rearrange("(c p) t -> p c t", p=128)[:, :, i * 256:(i + 1) * 256],
                    srcp=x1part.ap().rearrange("p (c t) -> p c t", c=8), h=s_x1.h:
                    e.dma_start(out=dst, in_=srcp, accum_op=ALU.add).then_inc(h, 16))
            kb.wait("scalar", s_x1, s_x1.v)

    fl.close()
    es_attn.close()
    X1_ALL = s_x1.v

    d_x1 = dbg("x1d", (1024, 2048), F32)
    if d_x1 is not None:
        kb.wait("sync", s_x1, X1_ALL)
        kb.dma("sync", d_x1, x1d.ap(), s_st)

    ef2 = ExitStack()
    E2 = ef2.enter_context
    x1w = E2(nc.sbuf_tensor("x1w", [128, 4096], F32))
    xb2 = E2(nc.sbuf_tensor("xb2", [128, 4096], BF16))
    xq2 = E2(nc.sbuf_tensor("xq2", [128, 4096], BF16))
    xtmp = E2(nc.sbuf_tensor("xtmp", [128, 4096], F32))
    xint = E2(nc.sbuf_tensor("xint", [128, 4096], BF16))
    mmx = E2(nc.sbuf_tensor("mmx", [128, 1040], BF16))
    soutb = E2(nc.sbuf_tensor("soutb", [128, 512], F32))
    x2w = E2(nc.sbuf_tensor("x2w", [128, 4096], F32))
    rows3 = E2(nc.sbuf_tensor("rows3", [128, 2048], F32))

    RW = 512
    def emit_ffn(nwin, ld_x, store_out):
        emit_ffn.tag = getattr(emit_ffn, "tag", 0)
        emit_ffn.st_sem = getattr(emit_ffn, "st_sem", s_stg)
        """Quantized ternary FFN over nwin windows of 512 tokens.
        ld_x(w): emit loads of x-window into x1w [128, (8, 512)] f32, return s_x value.
        store_out(w): emit store of x2w; x2w = x1w + ffn(LN(x1w))."""
        for w_ in range(nwin):
            kb.wait("sync", s_cv, s_cv.v)
            kb.wait("sync", s_sc, s_sc.v)
            XW = ld_x(w_)
            kb.wait("vector", s_x, XW)
            kb.v("tensor_copy", out=xb2[:], in_=x1w[:], sem=s_cv)
            XB = s_cv.v
            kb.wait("scalar", s_x, XW)
            kb.s("activation", out=xq2[:], in_=x1w[:], func=AF.Square, sem=s_sc)
            XQ = s_sc.v
            kb.wait("tensor", s_cv, XB)
            kb.wait("tensor", s_sc, XQ)
            for kc in range(8):
                kb.t("matmul", P3[0:1, 0:512], sb_ones[:, 0:1], xb2[:, kc * 512:(kc + 1) * 512],
                     start=(kc == 0), stop=(kc == 7))
            for kc in range(8):
                kb.t("matmul", P3[0:1, 512:1024], sb_ones[:, 0:1], xq2[:, kc * 512:(kc + 1) * 512],
                     start=(kc == 0), stop=(kc == 7), sem=(s_pe if kc == 7 else None))
            STATS = s_pe.v
            kb.wait("vector", s_cv, XB)
            kb.v("tensor_tensor", out=mmx[:, 0:512], in0=xb2[:, 0:512], in1=xb2[:, 512:1024], op=ALU.max)
            for kc in range(2, 8):
                kb.v("tensor_tensor", out=mmx[:, 0:512], in0=mmx[:, 0:512],
                     in1=xb2[:, kc * 512:(kc + 1) * 512], op=ALU.max)
            kb.v("tensor_tensor", out=mmx[:, 512:1024], in0=xb2[:, 0:512], in1=xb2[:, 512:1024], op=ALU.min)
            for kc in range(2, 8):
                kb.v("tensor_tensor", out=mmx[:, 512:1024], in0=mmx[:, 512:1024],
                     in1=xb2[:, kc * 512:(kc + 1) * 512], op=ALU.min, sem=(s_cv if kc == 7 else None))
            FOLD = s_cv.v
            PTb = P2.ap().bitcast(BF16)
            kb.wait("tensor", s_cv, FOLD)
            for q_ in range(4):
                kb.t("transpose", PTb[:, q_ * 128:(q_ + 1) * 128], mmx[:, q_ * 128:(q_ + 1) * 128], sb_ident[:])
            for q_ in range(4):
                kb.t("transpose", PTb[:, 512 + q_ * 128: 512 + (q_ + 1) * 128],
                     mmx[:, 512 + q_ * 128: 512 + (q_ + 1) * 128], sb_ident[:],
                     sem=(s_pe if q_ == 3 else None))
            TRD = s_pe.v
            kb.wait("vector", s_pe, TRD)
            for q_ in range(4):
                kb.v("tensor_reduce", mmx[:, 1024 + q_: 1024 + q_ + 1], PTb[:, q_ * 128:(q_ + 1) * 128],
                     axis=AX, op=ALU.max)
            for q_ in range(4):
                kb.v("tensor_reduce", mmx[:, 1028 + q_: 1028 + q_ + 1], PTb[:, 512 + q_ * 128: 512 + (q_ + 1) * 128],
                     axis=AX, op=ALU.min, sem=(s_cv if q_ == 3 else None))
            RED = s_cv.v
            d_mm = dbg("ffn_mmx_%d_%d" % (emit_ffn.tag, w_), (128, 1032), BF16)
            if d_mm is not None:
                kb.wait("sync", s_cv, RED)
                kb.dma("sync", d_mm, mmx[:, 0:1032], s_st)
                kb.wait("tensor", s_st, s_st.v)
            kb.wait("tensor", s_cv, RED)
            for q_ in range(4):
                kb.t("matmul", P1[0:1, q_ * 128:(q_ + 1) * 128], mmx[:, 1024 + q_: 1024 + q_ + 1],
                     sb_ident[:], start=True, stop=True)
            for q_ in range(4):
                kb.t("matmul", P1[0:1, 512 + q_ * 128: 512 + (q_ + 1) * 128], mmx[:, 1028 + q_: 1028 + q_ + 1],
                     sb_ident[:], start=True, stop=True, sem=(s_pe if q_ == 3 else None))
            ROWS_PE = s_pe.v
            R_SCR, R_MU, R_RHO, R_MU2 = 0, 512, 1024, 1536
            R3_M, R3_C, R3_SOUT, R3_SCR = 0, 512, 1024, 1536
            kb.wait("vector", s_pe, ROWS_PE)
            kb.v("tensor_scalar_mul", rows1[0:1, R_MU:R_MU + 512], P3[0:1, 0:512], 1.0 / 1024.0)
            kb.v("tensor_scalar_mul", rows1[0:1, R_SCR:R_SCR + 512], P3[0:1, 512:1024], 1.0 / 1024.0, sem=s_cv)
            P3_FREE = s_cv.v
            kb.v("tensor_mul", rows1[0:1, R_MU2:R_MU2 + 512], rows1[0:1, R_MU:R_MU + 512], rows1[0:1, R_MU:R_MU + 512])
            kb.v("tensor_sub", rows1[0:1, R_SCR:R_SCR + 512], rows1[0:1, R_SCR:R_SCR + 512],
                 rows1[0:1, R_MU2:R_MU2 + 512], sem=s_cv)
            VARD = s_cv.v
            kb.wait("scalar", s_cv, VARD)
            kb.s("activation", out=rows1[0:1, R_MU2:R_MU2 + 512], in_=rows1[0:1, R_SCR:R_SCR + 512],
                 func=AF.Ln, bias=sb_epsb[0:1, 0:1])
            kb.s("activation", out=rows1[0:1, R_RHO:R_RHO + 512], in_=rows1[0:1, R_MU2:R_MU2 + 512],
                 func=AF.Exp, scale=-0.5, sem=s_sc)
            RHOD = s_sc.v
            kb.wait("vector", s_sc, RHOD)
            kb.v("tensor_sub", rows3[0:1, R3_M:R3_M + 512], P1[0:1, 0:512], rows1[0:1, R_MU:R_MU + 512])
            kb.v("tensor_sub", rows3[0:1, R3_SCR:R3_SCR + 512], rows1[0:1, R_MU:R_MU + 512], P1[0:1, 512:1024])
            kb.v("tensor_tensor", out=rows3[0:1, R3_M:R3_M + 512], in0=rows3[0:1, R3_M:R3_M + 512],
                 in1=rows3[0:1, R3_SCR:R3_SCR + 512], op=ALU.max, sem=s_cv)
            P1_FREE = s_cv.v
            kb.v("reciprocal", out=rows3[0:1, R3_C:R3_C + 512], in_=rows3[0:1, R3_M:R3_M + 512])
            kb.v("tensor_scalar_mul", rows3[0:1, R3_C:R3_C + 512], rows3[0:1, R3_C:R3_C + 512], 127.0)
            kb.v("reciprocal", out=rows3[0:1, R3_SCR:R3_SCR + 512], in_=rows3[0:1, R3_C:R3_C + 512])
            kb.v("tensor_mul", rows3[0:1, R3_SOUT:R3_SOUT + 512], rows3[0:1, R3_SCR:R3_SCR + 512],
                 rows1[0:1, R_RHO:R_RHO + 512])
            kb.v("tensor_scalar", rows3[0:1, R3_SOUT:R3_SOUT + 512], rows3[0:1, R3_SOUT:R3_SOUT + 512],
                 sb_wscale[0:1, 0:1], None, op0=ALU.mult)
            kb.v("tensor_mul", rows3[0:1, R3_SCR:R3_SCR + 512], rows1[0:1, R_MU:R_MU + 512],
                 rows3[0:1, R3_C:R3_C + 512], sem=s_cv)
            ROWS_D = s_cv.v
            d_r3 = dbg("ffn_rows3_%d_%d" % (emit_ffn.tag, w_), (1, 2048))
            if d_r3 is not None:
                kb.wait("sync", s_cv, ROWS_D)
                kb.dma("sync", d_r3, rows3[0:1, :], s_st)
            d_r1 = dbg("ffn_rows1_%d_%d" % (emit_ffn.tag, w_), (1, 2048))
            if d_r1 is not None:
                kb.wait("sync", s_cv, ROWS_D)
                kb.dma("sync", d_r1, rows1[0:1, :], s_st)
            kb.wait("tensor", s_cv, ROWS_D)
            kb.t("matmul", P0[:, 0:512], sb_onesf[0:1, 0:128],
                 rows3[0:1, R3_C:R3_C + 512], start=True, stop=True)
            kb.t("matmul", P0[:, 512:1024], sb_onesf[0:1, 0:128], rows3[0:1, R3_SCR:R3_SCR + 512],
                 start=True, stop=True)
            kb.t("matmul", P3[:, 0:512], sb_onesf[0:1, 0:128], rows3[0:1, R3_SOUT:R3_SOUT + 512],
                 start=True, stop=True, sem=s_pe)
            BCAST = s_pe.v
            kb.wait("scalar", s_pe, BCAST)
            kb.s("copy", out=soutb[:], in_=P3[:, 0:512], sem=s_sc)
            SOUTB = s_sc.v
            kb.wait("vector", s_pe, BCAST)
            kb.v("tensor_tensor", out=xtmp[:].rearrange("p (c t) -> p c t", c=8),
                 in0=x1w[:].rearrange("p (c t) -> p c t", c=8),
                 in1=P0[:, 0:512].rearrange("p (o t) -> p o t", o=1).broadcast_to([128, 8, 512]), op=ALU.mult)
            kb.v("tensor_tensor", out=xtmp[:].rearrange("p (c t) -> p c t", c=8),
                 in0=xtmp[:].rearrange("p (c t) -> p c t", c=8),
                 in1=P0[:, 512:1024].rearrange("p (o t) -> p o t", o=1).broadcast_to([128, 8, 512]), op=ALU.subtract)
            kb.v("tensor_scalar", xint[:], xtmp[:], MAGIC, MAGIC, op0=ALU.add, op1=ALU.subtract, sem=s_cv)
            XINT = s_cv.v
            d_xi = dbg("ffn_xint_%d_%d" % (emit_ffn.tag, w_), (128, 4096), BF16)
            if d_xi is not None:
                kb.wait("sync", s_cv, XINT)
                kb.dma("sync", d_xi, xint[:], s_st)
                kb.wait("tensor", s_st, s_st.v)
            kb.wait("tensor", s_cv, XINT)
            FB = [P1.ap()[:, 0:512], P1.ap()[:, 512:1024], P2.ap()[:, 0:512], P2.ap()[:, 512:1024]]
            for grp in range(2):
                if grp == 0:
                    kb.wait("tensor", s_cv, P1_FREE)
                    kb.wait("tensor", s_sc, SOUTB)
                for oci in range(4):
                    oc = grp * 4 + oci
                    for kc in range(8):
                        kb.t("matmul", FB[oci], sb_ffnw[:, kc * 1024 + oc * 128: kc * 1024 + oc * 128 + 128],
                             xint[:, kc * 512:(kc + 1) * 512],
                             start=(kc == 0), stop=(kc == 7),
                             sem=(s_pe if (kc == 7 and oci == 3) else None))
                FFND = s_pe.v
                kb.wait("vector", s_pe, FFND)
                for oci in range(4):
                    oc = grp * 4 + oci
                    kb.v("tensor_tensor", out=xtmp[:, oc * 512:(oc + 1) * 512], in0=FB[oci],
                         in1=soutb[:], op=ALU.mult)
                    kb.v("scalar_tensor_tensor", x2w[:, oc * 512:(oc + 1) * 512],
                         xtmp[:, oc * 512:(oc + 1) * 512], sb_vecs[:, V_BIAS + oc:V_BIAS + oc + 1],
                         x1w[:, oc * 512:(oc + 1) * 512],
                         op0=ALU.add, op1=ALU.add, sem=(s_cv if oci == 3 else None))
                EVC = s_cv.v
                kb.wait("tensor", s_cv, EVC)
            store_out(w_)
            kb.wait("vector", emit_ffn.st_sem, emit_ffn.st_sem.v)

    ph2_gate = dict(s_x1=s_x1.v, s_sc=s_sc.v, s_cv=s_cv.v, s_pe=s_pe.v, s_up=FFNW_RDY)
    for engx in ("tensor", "vector", "scalar", "sync", "gpsimd"):
        kb.wait(engx, s_x1, ph2_gate["s_x1"])
        kb.wait(engx, s_up, FFNW_RDY)
        kb.wait(engx, s_sc, ph2_gate["s_sc"])
        kb.wait(engx, s_cv, ph2_gate["s_cv"])
        kb.wait(engx, s_pe, ph2_gate["s_pe"])

    def ld_x1(w_):
        if s_stg.v:
            kb.wait("sync", s_stg, s_stg.v)
        kb.dma("sync", x1w[:].rearrange("p (c t) -> p c t", c=8),
               x1d.ap().rearrange("(c p) t -> p c t", p=128)[:, :, w_ * 512:(w_ + 1) * 512], s_x)
        return s_x.v

    def st_x2(w_):
        kb.wait("sync", s_cv, s_cv.v)
        kb.dma("sync", e1_in.ap().rearrange("(c p) t -> p c t", p=128)[:, :, w_ * 512:(w_ + 1) * 512],
               x2w[:].rearrange("p (c t) -> p c t", c=8), s_stg)

    emit_ffn.st_sem = s_stg
    emit_ffn(4, ld_x1, st_x2)
    emit_ffn.tag = 1
    X2_STAGED = s_stg.v

    d_x2 = dbg("e1_in", (1024, 2048), F32)
    if d_x2 is not None:
        kb.wait("sync", s_stg, X2_STAGED)
        kb.dma("sync", d_x2, e1_in.ap(), s_st)

    ef2.close()

    kb.wait("gpsimd", s_stg, X2_STAGED)
    s_cc1.bump(1)
    kb.emit("gpsimd", lambda e: e.collective_compute(
        "AllGather", ALU.bypass, replica_groups=GROUPS,
        ins=[e1_in.ap()], outs=[e1_out.ap()]).then_inc(s_cc1.h))
    E1_DONE = s_cc1.v

    if ph < 3:
        kb.wait("sync", s_stg, s_stg.v)
        kb.wait("sync", s_cc, s_cc.v)
        kb.wait("sync", s_cc1, s_cc1.v)
        kb.wait("sync", s_cc2, s_cc2.v)
        kb.wait("vector", s_stg, s_stg.v)
        kb.v("memset", x1w[:, 0:2048] if False else xb2.ap().bitcast(F32)[:, 0:1024], 0.0, sem=s_cv)
        kb.wait("sync", s_cv, s_cv.v)
        for wo_ in range(4):
            kb.dma("sync", out_mine.rearrange("(c p) t -> p c t", p=128)[:, :, wo_ * 512:(wo_ + 1) * 512],
                   x2w[:].rearrange("p (c t) -> p c t", c=8), s_out)
        kb.wait("sync", s_out, s_out.v)
        ef2_hold = True
    e3 = ExitStack()
    E3 = e3.enter_context
    w_inp_b = E3(nc.sbuf_tensor("w_inp_b", [128, 8 * 1312], BF16))
    w_hp_b = E3(nc.sbuf_tensor("w_hp_b", [128, 9 * 40], BF16))
    w_outp_b = E3(nc.sbuf_tensor("w_outp_b", [128, 4 * 1024], BF16))
    big1 = E3(nc.sbuf_tensor("big1", [128, 4096], F32))
    wcv3 = big1
    uw = E3(nc.sbuf_tensor("uw", [128, 4096], BF16))
    cumw = E3(nc.sbuf_tensor("cumw", [128, 4096], BF16))
    ucar = E3(nc.sbuf_tensor("ucar", [128, 8], F32))
    xb3 = E3(nc.sbuf_tensor("xb3", [128, 4096], BF16))
    xq3 = E3(nc.sbuf_tensor("xq3", [128, 4096], BF16))
    zw = E3(nc.sbuf_tensor("zw", [128, 4 * 512], BF16))
    xpre = E3(nc.sbuf_tensor("xpre", [128, 6 * 515], BF16))
    xpost = E3(nc.sbuf_tensor("xpost", [128, 6 * 512], BF16))
    dtTw = E3(nc.sbuf_tensor("dtTw", [32, 512], BF16))
    invw = E3(nc.sbuf_tensor("invw", [40, 512], F32))
    hbs = E3(nc.sbuf_tensor("hbs", [40, 512], F32))
    hbT = E3(nc.sbuf_tensor("hbT", [128, 4 * 44], F32))
    daT = E3(nc.sbuf_tensor("daT", [128, 4 * 32], F32))
    tk1 = E3(nc.sbuf_tensor("tk1", [128, 28], F32))
    tk8 = E3(nc.sbuf_tensor("tk8", [128, 24], F32))
    dt16 = E3(nc.sbuf_tensor("dt16", [128, 16], F32))
    dtsp = E3(nc.sbuf_tensor("dtsp", [128, 16], F32))
    dtspT = E3(nc.sbuf_tensor("dtspT", [16, 128], F32))
    mydtT = E3(nc.sbuf_tensor("mydtT", [8, 128], F32))
    mydt = E3(nc.sbuf_tensor("mydt", [128, 4 * 8], F32))
    acum = E3(nc.sbuf_tensor("acum", [8, 4096], F32))
    adt_rows = E3(nc.sbuf_tensor("adt_rows", [8, 4096], F32))
    dt_tok = E3(nc.sbuf_tensor("dt_tok", [128, 512], F32))
    acar = E3(nc.sbuf_tensor("acar", [8, 1], F32))
    aendr = E3(nc.sbuf_tensor("aendr", [1, 40], F32))

    e2t = E3(nc.sbuf_tensor("e2t", [128, 8], F32))
    e2i = E3(nc.sbuf_tensor("e2i", [128, 8], F32))
    lrow = E3(nc.sbuf_tensor("lrow", [1, 16], F32))
    xTt = E3(nc.sbuf_tensor("xTt", [128, 512], BF16))
    dtx = E3(nc.sbuf_tensor("dtx", [128, 512], BF16))
    wdtx = E3(nc.sbuf_tensor("wdtx", [128, 512], BF16))
    BTt = E3(nc.sbuf_tensor("BTt", [128, 128], BF16))
    PMt = E3(nc.sbuf_tensor("PMt", [128, 8 * 128], BF16))
    MWs = E3(nc.sbuf_tensor("MWs", [128, 512], F32))
    Gms = E3(nc.sbuf_tensor("Gms", [128, 128], BF16))
    state = E3(nc.sbuf_tensor("state", [128, 512], F32))
    state_bf = E3(nc.sbuf_tensor("state_bf", [128, 512], BF16))
    ytk = E3(nc.sbuf_tensor("ytk", [128, 512], BF16))
    ytmp = E3(nc.sbuf_tensor("ytmp", [128, 512], F32))
    yfm = E3(nc.sbuf_tensor("yfm", [128, 4 * 512], BF16))
    yzt = xb3.ap().bitcast(BF16)[:, 0:2048] if False else None

    selm = E3(nc.sbuf_tensor("selm", [16, 8], F32))
    sb_iota = E3(nc.sbuf_tensor("sb_iota", [128, 336], BF16))
    identf = E3(nc.sbuf_tensor("identf", [128, 128], F32))
    id8 = E3(nc.sbuf_tensor("id8", [8, 8], F32))
    rows4 = E3(nc.sbuf_tensor("rows4", [1, 2048], F32))
    yzt = xb3
    yzw = xq3
    cwf = cumw.ap().bitcast(F32)
    LAm = cwf[0:2, 0:1024]
    LBm = cwf[0:2, 1024:2048]

    g3 = dict(pe=s_pe.v, cv=s_cv.v, sc=s_sc.v, stg=s_stg.v, x=s_x.v)
    for engx in ("tensor", "vector", "scalar", "sync", "gpsimd"):
        kb.wait(engx, s_pe, g3["pe"])
        kb.wait(engx, s_cv, g3["cv"])
        kb.wait(engx, s_sc, g3["sc"])
        kb.wait(engx, s_stg, g3["stg"])

    kb.dma("sync", selm[:], c_sel, s_ld)
    kb.dma("sync", sb_iota[:], c_iota12, s_ld)
    kb.dma("sync", identf[:], c_identf, s_ld)
    kb.dma("sync", id8[:], c_id8, s_ld)
    LD3 = s_ld.v

    def load_convert3(wsrc, wdstb, ncols, kchunks, rows=128):
        for kc in range(kchunks):
            par = load_convert_w.par = getattr(load_convert_w, "par", 0)
            buf = wcv3[0:rows, (par % 2) * 1312:(par % 2) * 1312 + ncols]
            prev = [h for h in conv_hist if h[2] == par % 2]
            if prev:
                kb.wait("sync", prev[-1][0], prev[-1][1])
            kb.dma("sync", buf, wsrc.rearrange("(c p) n -> p c n", p=rows)[:, kc, :], s_wld[par % 2])
            kb.wait("vector", s_wld[par % 2])
            kb._c("vector", "tensor_copy", (), dict(out=wdstb[0:rows, kc * ncols:(kc + 1) * ncols], in_=buf), s_wcv_v)
            conv_hist.append((s_wcv_v, s_wcv_v.v, par % 2))
            load_convert_w.par = par + 1

    load_convert3(w_inp, w_inp_b, 1312, 8)
    load_convert3(w_outp, w_outp_b, 1024, 4)
    load_convert3(w_hp[0:1024, :], w_hp_b, 40, 8)
    par = load_convert_w.par
    bufh = wcv3[0:32, (par % 2) * 1312:(par % 2) * 1312 + 40]
    prev = [h for h in conv_hist if h[2] == par % 2]
    if prev:
        kb.wait("sync", prev[-1][0], prev[-1][1])
    kb.dma("sync", bufh, w_hp[1024:1056, :], s_wld[par % 2])
    kb.wait("vector", s_wld[par % 2])
    kb._c("vector", "tensor_copy", (), dict(out=w_hp_b[0:32, 8 * 40:9 * 40], in_=bufh), s_wcv_v)
    conv_hist.append((s_wcv_v, s_wcv_v.v, par % 2))
    load_convert_w.par = par + 1
    W3_CV = s_wcv_v.v

    kb.wait("tensor", s_wcv_v, W3_CV)
    for oc in range(3):
        wdt = min(512, 1312 - oc * 512)
        for kc in range(8):
            kb.t("matmul", P3[0:1, 0:wdt], sb_ones[:, 0:1],
                 w_inp_b[:, kc * 1312 + oc * 512: kc * 1312 + oc * 512 + wdt],
                 start=(kc == 0), stop=(kc == 7), sem=(s_pe if kc == 7 else None))
        CSD = s_pe.v
        kb.wait("vector", s_pe, CSD)
        kb.v("tensor_scalar", sb_csum[0:1, CS_INP + oc * 512: CS_INP + oc * 512 + wdt],
             P3[0:1, 0:wdt], -1.0, None, op0=ALU.mult, sem=s_cv)
        kb.wait("tensor", s_cv, s_cv.v)
    for kc in range(8):
        kb.t("matmul", P3[0:1, 0:40], sb_ones[:, 0:1], w_hp_b[:, kc * 40:(kc + 1) * 40],
             start=(kc == 0), stop=(kc == 7), sem=(s_pe if kc == 7 else None))
    CSD = s_pe.v
    kb.wait("vector", s_pe, CSD)
    kb.v("tensor_scalar", sb_csum[0:1, CS_HP:CS_HP + 40], P3[0:1, 0:40], -1.0, None, op0=ALU.mult, sem=s_cv)
    kb.wait("tensor", s_cv, s_cv.v)

    kb.v("memset", ucar[:], 0.0)
    kb.v("memset", acar[:], 0.0)
    kb.v("memset", aendr[:], 0.0)
    kb.v("memset", state[:], 0.0)
    kb.v("memset", state_bf[:], 0.0)
    kb.v("memset", xpre[:], 0.0)
    kb.v("memset", rows1[0:1, 0:2048], 0.0, sem=s_cv)
    INIT3 = s_cv.v
    R_CUMS_CAR = 1920

    kb.wait("sync", s_cc1, E1_DONE)

    e1v = e1_out.ap().rearrange("(k c p) t -> k c p t", k=2, p=128)

    s_hd = sem("s_hd")
    s_hp = sem("s_hp")
    s_hs = sem("s_hs")

    def hades_front(w_):
        kb.soft_barrier((s_pe, s_cv, s_sc, s_stg, s_x))
        kb.wait("vector", s_pe, prevg.get("ssd_pe", 0))
        kb.wait("scalar", s_pe, prevg.get("ssd_pe", 0))
        kb.wait("scalar", s_cv, prevg.get("yz", 0))
        kb.wait("sync", s_stg, prevg.get("e2st", 0))
        for k_ in range(2):
            kb.dma("sync", big1[:].rearrange("p (c r t) -> p c r t", c=8, r=2)[:, :, k_, :],
                   e1v[k_, :, :, w_ * 256:(w_ + 1) * 256].rearrange("c p t -> p c t"), s_x)
        XW = s_x.v
        kb.dma("sync", invw[:], c_invden[:, w_ * 512:(w_ + 1) * 512], s_x)
        INVW = s_x.v
        kb.wait("vector", s_x, XW)
        kb.v("tensor_copy", out=xb3[:], in_=big1[:], sem=s_cv)
        XB = s_cv.v
        kb.wait("scalar", s_x, XW)
        kb.s("activation", out=xq3[:], in_=big1[:], func=AF.Square, sem=s_sc)
        XQ = s_sc.v
        kb.wait("tensor", s_cv, XB)
        kb.wait("tensor", s_sc, XQ)
        for kc in range(8):
            kb.t("matmul", P3[0:1, 0:512], sb_ones[:, 0:1], xb3[:, kc * 512:(kc + 1) * 512],
                 start=(kc == 0), stop=(kc == 7))
        for kc in range(8):
            kb.t("matmul", P3[0:1, 512:1024], sb_ones[:, 0:1], xq3[:, kc * 512:(kc + 1) * 512],
                 start=(kc == 0), stop=(kc == 7), sem=(s_pe if kc == 7 else None))
        STATS = s_pe.v
        R_SCR, R_MU, R_RHO, R_MU2 = 0, 512, 1024, 1536
        kb.wait("vector", s_pe, STATS)
        kb.v("tensor_scalar_mul", rows1[0:1, R_MU:R_MU + 512], P3[0:1, 0:512], 1.0 / 1024.0)
        kb.v("tensor_scalar_mul", rows1[0:1, R_SCR:R_SCR + 512], P3[0:1, 512:1024], 1.0 / 1024.0, sem=s_cv)
        P3F = s_cv.v
        kb.v("tensor_mul", rows1[0:1, R_MU2:R_MU2 + 512], rows1[0:1, R_MU:R_MU + 512], rows1[0:1, R_MU:R_MU + 512])
        kb.v("tensor_sub", rows1[0:1, R_SCR:R_SCR + 512], rows1[0:1, R_SCR:R_SCR + 512],
             rows1[0:1, R_MU2:R_MU2 + 512], sem=s_cv)
        VARD = s_cv.v
        kb.wait("scalar", s_cv, VARD)
        kb.s("activation", out=rows1[0:1, R_MU2:R_MU2 + 512], in_=rows1[0:1, R_SCR:R_SCR + 512],
             func=AF.Ln, bias=sb_epsb[0:1, 0:1])
        kb.s("activation", out=rows1[0:1, R_RHO:R_RHO + 512], in_=rows1[0:1, R_MU2:R_MU2 + 512],
             func=AF.Exp, scale=-0.5, sem=s_sc)
        RHOD = s_sc.v
        kb.wait("vector", s_sc, RHOD)
        kb.v("tensor_mul", rows4[0:1, 0:512], rows1[0:1, R_MU:R_MU + 512], rows1[0:1, R_RHO:R_RHO + 512], sem=s_cv)
        kb.wait("vector", s_cv, s_cv.v)
        kb.v("tensor_copy", out=rowbf[0:1, 0:512], in_=rows4[0:1, 0:512])
        kb.v("tensor_copy", out=rowbf[0:1, 512:1024], in_=rows1[0:1, R_RHO:R_RHO + 512], sem=s_cv)
        kb.v("tensor_tensor_scan", out=rows4[0:1, 512:1024], data0=sb_onesf[0:1, 0:512],
             data1=rows4[0:1, 0:512], initial=rows1[0:1, R_CUMS_CAR:R_CUMS_CAR + 1],
             op0=ALU.mult, op1=ALU.add, sem=s_cv)
        kb.wait("vector", s_cv, s_cv.v)
        kb.v("tensor_copy", out=rows1[0:1, R_CUMS_CAR:R_CUMS_CAR + 1], in_=rows4[0:1, 1023:1024])
        kb.v("tensor_copy", out=rowbf[0:1, 1024:1536], in_=rows4[0:1, 512:1024], sem=s_cv)
        ROWSD = s_cv.v
        kb.wait("tensor", s_cv, ROWSD)
        kb.t("matmul", P2[:, 0:512], sb_ones[0:1, :], rowbf[0:1, 512:1024], start=True, stop=True, sem=s_pe)
        RHOB = s_pe.v
        kb.wait("vector", s_pe, RHOB)
        kb.v("tensor_tensor", out=uw[:].rearrange("p (c t) -> p c t", c=8),
             in0=big1[:].rearrange("p (c t) -> p c t", c=8),
             in1=P2[:, 0:512].rearrange("p (o t) -> p o t", o=1).broadcast_to([128, 8, 512]),
             op=ALU.mult, sem=s_cv)
        UD = s_cv.v
        for kc in range(8):
            kb.v("tensor_tensor_scan", out=cumw[:, kc * 512:(kc + 1) * 512],
                 data0=sb_onesf[:, 0:512], data1=uw[:, kc * 512:(kc + 1) * 512],
                 initial=ucar[:, kc:kc + 1], op0=ALU.mult, op1=ALU.add,
                 sem=(s_cv if kc == 7 else None))
        kb.wait("vector", s_cv, s_cv.v)
        kb.v("tensor_copy", out=ucar[:],
             in_=cumw[:].rearrange("p (c t) -> p c t", c=8)[:, :, 511:512].rearrange("p c o -> p (c o)"),
             sem=s_cv)
        CUMD = s_cv.v
        d_cumw = dbg("h_cumw_%d" % w_, (128, 4096), BF16)
        if d_cumw is not None:
            kb.wait("sync", s_cv, CUMD)
            kb.dma("sync", d_cumw, cumw[:], s_st)
            kb.wait("tensor", s_st, s_st.v)
        if FUP < 3:
            return dict(SILUD=s_sc.v, FRONT_SC=s_sc.v, ACUM=s_cv.v, AEND=s_x.v, HBC=s_pe.v)
        kb.wait("tensor", s_cv, UD)
        IB = [P0.ap()[:, 0:512], P0.ap()[:, 512:1024], P1.ap()[:, 0:512], P1.ap()[:, 512:1024]]
        kb.v("tensor_copy", out=xpre[:].rearrange("p (c t) -> p c t", c=6)[:, :, 0:3],
             in_=xpre[:].rearrange("p (c t) -> p c t", c=6)[:, :, 512:515], sem=s_cv)
        HALO = s_cv.v
        for grp in range(3):
            tiles = [grp * 4 + j for j in range(4) if grp * 4 + j < 11]
            for j, tile in enumerate(tiles):
                r0 = tile * 128
                wdt = 128 if tile < 10 else 32
                for kc in range(8):
                    kb.t("matmul", IB[j][0:128, 0:512] if wdt == 128 else IB[j][0:32, 0:512],
                         w_inp_b[:, kc * 1312 + r0: kc * 1312 + r0 + wdt],
                         uw[:, kc * 512:(kc + 1) * 512], start=(kc == 0), stop=False)
                kb.t("matmul", IB[j][0:wdt, 0:512], sb_csum[0:1, CS_INP + r0: CS_INP + r0 + wdt],
                     rowbf[0:1, 0:512], start=False, stop=True,
                     sem=(s_pe if j == len(tiles) - 1 else None))
            IPD = s_pe.v
            kb.wait("scalar", s_pe, IPD)
            if grp == 0:
                kb.wait("scalar", s_cv, HALO)
            for j, tile in enumerate(tiles):
                if tile < 4:
                    dst = zw[:, tile * 512:(tile + 1) * 512]
                elif tile < 10:
                    cch = tile - 4
                    dst = xpre[:, cch * 515 + 3: cch * 515 + 515]
                else:
                    dst = dtTw[0:32, 0:512]
                kb.s("copy", out=dst, in_=IB[j][0:(128 if tile < 10 else 32), 0:512],
                     sem=(s_sc if j == len(tiles) - 1 else None))
            IPE = s_sc.v
            kb.wait("tensor", s_sc, IPE)
        if FUP < 4:
            return dict(SILUD=s_sc.v, FRONT_SC=s_sc.v, ACUM=s_cv.v, AEND=s_x.v, HBC=s_pe.v)
        kb.soft_barrier((s_pe, s_cv, s_sc))
        kb.wait("vector", s_sc, s_sc.v)
        for cch in range(6):
            base = cch * 515
            kb.v("tensor_scalar", xpost[:, cch * 512:(cch + 1) * 512], xpre[:, base + 0: base + 512],
                 sb_cwv[:, cch * 4 + 0: cch * 4 + 1], sb_vecs[:, V_CONVB + cch:V_CONVB + cch + 1],
                 op0=ALU.mult, op1=ALU.add)
            for kk in range(1, 4):
                kb.v("scalar_tensor_tensor", xpost[:, cch * 512:(cch + 1) * 512],
                     xpre[:, base + kk: base + kk + 512],
                     sb_cwv[:, cch * 4 + kk: cch * 4 + kk + 1],
                     xpost[:, cch * 512:(cch + 1) * 512],
                     op0=ALU.mult, op1=ALU.add, sem=(s_cv if (cch == 5 and kk == 3) else None))
        CONVD = s_cv.v
        kb.wait("scalar", s_cv, CONVD)
        for cch in range(6):
            kb.s("activation", out=xpost[:, cch * 512:(cch + 1) * 512],
                 in_=xpost[:, cch * 512:(cch + 1) * 512], func=AF.Silu,
                 sem=(s_sc if cch == 5 else None))
        SILUD = s_sc.v
        if FUP < 5:
            return dict(SILUD=SILUD, FRONT_SC=s_sc.v, ACUM=s_cv.v, AEND=s_x.v, HBC=s_pe.v)
        kb.wait("tensor", s_cv, CUMD)
        for kc in range(8):
            kb.t("matmul", P2[0:40, 512:1024][0:40, 0:512] if False else P2[0:40, 512:1024],
                 w_hp_b[:, kc * 40:(kc + 1) * 40], uw[:, kc * 512:(kc + 1) * 512],
                 start=(kc == 0), stop=False)
        kb.t("matmul", P2[0:40, 512:1024], w_hp_b[0:32, 8 * 40:9 * 40], dtTw[0:32, 0:512],
             start=False, stop=False)
        kb.t("matmul", P2[0:40, 512:1024], sb_csum[0:1, CS_HP:CS_HP + 40], rowbf[0:1, 0:512],
             start=False, stop=True, sem=s_pe)
        HBU = s_pe.v
        for kc in range(8):
            kb.t("matmul", P3[0:40, 0:512], w_hp_b[:, kc * 40:(kc + 1) * 40],
                 cumw[:, kc * 512:(kc + 1) * 512], start=(kc == 0), stop=False)
        kb.t("matmul", P3[0:40, 0:512], sb_csum[0:1, CS_HP:CS_HP + 40], rowbf[0:1, 1024:1536],
             start=False, stop=True, sem=s_pe)
        HBC = s_pe.v
        kb.wait("vector", s_pe, HBC)
        kb.wait("vector", s_x, INVW)
        kb.v("tensor_tensor", out=hbs[:], in0=P3[0:40, 0:512], in1=invw[:], op=ALU.mult)
        kb.v("tensor_tensor", out=hbs[:], in0=P2[0:40, 512:1024], in1=hbs[:], op=ALU.subtract, sem=s_cv)
        HBSD = s_cv.v
        d_rb = dbg("h_rowbf_%d" % w_, (1, 2048), BF16)
        if d_rb is not None:
            kb.wait("sync", s_cv, ROWSD)
            kb.dma("sync", d_rb[0:1, 0:1536], rowbf[0:1, 0:1536], s_st)
        d_dtT = dbg("h_dtTw_%d" % w_, (32, 512), BF16)
        if d_dtT is not None:
            kb.wait("sync", s_sc, s_sc.v)
            kb.dma("sync", d_dtT, dtTw[:], s_st)
        d_hb = dbg("h_hbs_%d" % w_, (40, 512))
        if d_hb is not None:
            kb.wait("sync", s_cv, HBSD)
            kb.dma("sync", d_hb, hbs[:], s_st)
        d_xp = dbg("h_xpost_%d" % w_, (128, 6 * 512), BF16)
        if d_xp is not None:
            kb.wait("sync", s_sc, SILUD)
            kb.dma("sync", d_xp, xpost[:], s_st)
        d_zw = dbg("h_zw_%d" % w_, (128, 2048), BF16)
        if d_zw is not None:
            kb.wait("sync", s_sc, SILUD)
            kb.dma("sync", d_zw, zw[:], s_st)
        d_uw = dbg("h_uw_%d" % w_, (128, 4096), BF16)
        if d_uw is not None:
            kb.wait("sync", s_cv, UD)
            kb.dma("sync", d_uw, uw[:], s_st)
            kb.wait("gpsimd", s_st, s_st.v)
        if FUP < 6:
            return dict(SILUD=SILUD, FRONT_SC=s_sc.v, ACUM=s_cv.v, AEND=s_x.v, HBC=HBC)
        kb.soft_barrier((s_pe, s_cv, s_sc))
        kb.wait("tensor", s_cv, HBSD)
        for tt in range(4):
            kb.t("matmul", P3[:, 512 + tt * 44: 512 + tt * 44 + 40],
                 hbs[0:40, tt * 128:(tt + 1) * 128], identf[0:40, 0:40],
                 start=True, stop=True)
            kb.t("matmul", P2[:, 512 + tt * 32: 512 + tt * 32 + 32],
                 dtTw[0:32, tt * 128:(tt + 1) * 128], sb_ident[0:32, 0:32],
                 start=True, stop=True, sem=(s_pe if tt == 3 else None))
        TRD = s_pe.v
        kb.wait("scalar", s_pe, TRD)
        for tt in range(4):
            kb.s("copy", out=hbT[:, tt * 44: tt * 44 + 40], in_=P3[:, 512 + tt * 44: 512 + tt * 44 + 40])
            kb.s("copy", out=daT[:, tt * 32:(tt + 1) * 32], in_=P2[:, 512 + tt * 32: 512 + tt * 32 + 32],
                 sem=(s_sc if tt == 3 else None))
        TRE = s_sc.v
        kb.wait("vector", s_sc, TRE)
        TDTprev = 0
        for tt in range(4):
            cglob = 4 * w_ + tt
            h28 = hbT[:, tt * 44: tt * 44 + 28]
            kb.v("tensor_tensor", out=PMt[:, 0:784].rearrange("p (i j) -> p i j", i=28),
                 in0=h28.rearrange("p (o j) -> p o j", o=1).broadcast_to([128, 28, 28]),
                 in1=h28.rearrange("p (i o) -> p i o", o=1).broadcast_to([128, 28, 28]),
                 op=ALU.is_gt)
            kb.v("reduce_sum", out=tk1[:, 0:28], in_=PMt[:, 0:784].rearrange("p (i j) -> p i j", i=28),
                 axis=AX, sem=s_cv)
            kb.wait("vector", s_cv, s_cv.v)
            kb.v("tensor_copy", out=PMt[:, 784:812], in_=tk1[:, 0:28], sem=s_cv)
            kb.wait("vector", s_cv, s_cv.v)
            kb.v("tensor_tensor", out=PMt[:, 0:336].rearrange("p (j i) -> p j i", j=12),
                 in0=PMt[:, 784:812].rearrange("p (o i) -> p o i", o=1).broadcast_to([128, 12, 28]),
                 in1=sb_iota[:].rearrange("p (j i) -> p j i", j=12),
                 op=ALU.is_equal)
            kb.v("tensor_copy", out=PMt[:, 812:840], in_=daT[:, tt * 32: tt * 32 + 28], sem=s_cv)
            kb.wait("vector", s_cv, s_cv.v)
            kb.v("tensor_tensor", out=PMt[:, 0:336].rearrange("p (j i) -> p j i", j=12),
                 in0=PMt[:, 0:336].rearrange("p (j i) -> p j i", j=12),
                 in1=PMt[:, 812:840].rearrange("p (o i) -> p o i", o=1).broadcast_to([128, 12, 28]),
                 op=ALU.mult, sem=s_cv)
            kb.wait("vector", s_cv, s_cv.v)
            kb.v("reduce_sum", out=dt16[:, 0:12], in_=PMt[:, 0:336].rearrange("p (j i) -> p j i", j=12),
                 axis=AX)
            kb.v("tensor_mul", tk1[:, 0:12], hbT[:, tt * 44 + 28: tt * 44 + 40],
                 sb_vecs[:, V_GAMMA:V_GAMMA + 12], sem=s_cv)
            kb.wait("vector", s_cv, s_cv.v)
            kb.v("tensor_scalar", dt16[:, 0:12], dt16[:, 0:12], sb_vecs[:, V_DTB:V_DTB + 1], None, op0=ALU.add, sem=s_cv)
            kb.wait("vector", s_cv, s_cv.v)
            kb.v("tensor_add", dt16[:, 0:12], dt16[:, 0:12], tk1[:, 0:12])
            kb.v("tensor_scalar", dt16[:, 12:16], daT[:, tt * 32 + 28: tt * 32 + 32],
                 sb_vecs[:, V_DTB:V_DTB + 1], None, op0=ALU.add, sem=s_cv)
            DT16 = s_cv.v
            d_v12 = dbg("h_v12_%d" % w_, (128, 28)) if tt == 0 else None
            if d_v12 is not None:
                kb.wait("sync", s_cv, DT16)
                kb.dma("sync", d_v12, tk1[:, 0:28], s_st)
            d_d16 = dbg("h_dt16_%d" % w_, (128, 16)) if tt == 0 else None
            if d_d16 is not None:
                kb.wait("sync", s_cv, DT16)
                kb.dma("sync", d_d16, dt16[:], s_st)
            d_eqm = dbg("h_eqm_%d" % w_, (128, 336), BF16) if tt == 0 else None
            if d_eqm is not None:
                kb.wait("sync", s_cv, DT16)
                kb.dma("sync", d_eqm, PMt[:, 0:336], s_st)
            if d_v12 is not None or d_d16 is not None:
                kb.wait("vector", s_st, s_st.v)
                kb.wait("scalar", s_st, s_st.v)
            kb.wait("scalar", s_cv, DT16)
            kb.wait("scalar", s_pe, TDTprev)
            kb.s("activation", out=dtsp[:], in_=dt16[:], func=AF.Abs, sem=s_sc)
            kb.wait("scalar", s_sc, s_sc.v)
            kb.s("activation", out=dtsp[:], in_=dtsp[:], func=AF.Exp, scale=-1.0, sem=s_sc)
            kb.wait("scalar", s_sc, s_sc.v)
            kb.s("activation", out=dtsp[:], in_=dtsp[:], func=AF.Ln, bias=sb_onesf[:, 0:1], sem=s_sc)
            SPL = s_sc.v
            kb.wait("vector", s_sc, SPL)
            kb.wait("vector", s_cv, DT16)
            kb.v("tensor_scalar_max", tk1[:, 0:16], dt16[:], 0.0, sem=s_cv)
            kb.wait("vector", s_cv, s_cv.v)
            kb.v("tensor_add", dtsp[:], dtsp[:], tk1[:, 0:16], sem=s_cv)
            DTSP = s_cv.v
            kb.wait("vector", s_cv, DTSP)
            kb.v("tensor_copy", out=dt_tok[:, cglob * 16:(cglob + 1) * 16], in_=dtsp[:], sem=s_cv)
            kb.wait("tensor", s_cv, DTSP)
            kb.t("matmul", P3[0:16, 512 + 176: 512 + 176 + 128], dtsp[:], identf[:, 0:128],
                 start=True, stop=True, sem=s_pe)
            TDT = s_pe.v
            TDTprev = TDT
            kb.wait("scalar", s_pe, TDT)
            kb.s("copy", out=dtspT[:], in_=P3[0:16, 688:816], sem=s_sc)
            TDE = s_sc.v
            kb.wait("tensor", s_sc, TDE)
            kb.t("matmul", P3[0:8, 816:944], selm[:], dtspT[:], start=True, stop=True, sem=s_pe)
            SELD = s_pe.v
            kb.wait("scalar", s_pe, SELD)
            kb.s("copy", out=mydtT[:], in_=P3[0:8, 816:944], sem=s_sc)
            MDT = s_sc.v
            kb.wait("vector", s_sc, MDT)
            kb.v("tensor_scalar", adt_rows[0:8, cglob * 128:(cglob + 1) * 128], mydtT[:],
                 sb_alog[0:8, 1:2], None, op0=ALU.mult, sem=s_cv)
            ADT = s_cv.v
            kb.wait("tensor", s_sc, MDT)
            kb.t("matmul", P3[0:128, 944:952], mydtT[:], id8[:], start=True, stop=True, sem=s_pe)
            MDTT = s_pe.v
            kb.wait("scalar", s_pe, MDTT)
            kb.s("copy", out=mydt[:, tt * 8:(tt + 1) * 8], in_=P3[:, 944:952], sem=s_sc)
        FRONT_SC = s_sc.v
        kb.wait("vector", s_cv, s_cv.v)
        kb.v("tensor_tensor_scan", out=acum[0:8, w_ * 512:(w_ + 1) * 512],
             data0=sb_onesf[0:8, 0:512], data1=adt_rows[0:8, w_ * 512:(w_ + 1) * 512],
             initial=acar[:, 0:1], op0=ALU.mult, op1=ALU.add, sem=s_cv)
        kb.wait("vector", s_cv, s_cv.v)
        kb.v("tensor_copy", out=acar[:, 0:1], in_=acum[0:8, w_ * 512 + 511: w_ * 512 + 512], sem=s_cv)
        ACUM = s_cv.v
        d_ac = dbg("h_acum_%d" % w_, (8, 512))
        if d_ac is not None:
            kb.wait("sync", s_cv, ACUM)
            kb.dma("sync", d_ac, acum[0:8, w_ * 512:(w_ + 1) * 512], s_st)
        d_ad = dbg("h_adt_%d" % w_, (8, 512))
        if d_ad is not None:
            kb.wait("sync", s_cv, ACUM)
            kb.dma("sync", d_ad, adt_rows[0:8, w_ * 512:(w_ + 1) * 512], s_st)
        d_md = dbg("h_mydt_%d" % w_, (128, 32))
        if d_md is not None:
            kb.wait("sync", s_sc, FRONT_SC)
            kb.dma("sync", d_md, mydt[:, 0:32], s_st)
        return dict(SILUD=SILUD, FRONT_SC=FRONT_SC, ACUM=ACUM, AEND=s_x.v, HBC=HBC)
    prevg = dict(yz=0, op=0, e2st=0)

    def ssd_chunk(w_, tt, fr):
        cg = 4 * w_ + tt
        kb.soft_barrier((s_pe, s_cv, s_sc, s_x))
        kb.wait("tensor", s_cv, prevg.get("ssd_cv", 0))
        kb.wait("tensor", s_sc, prevg.get("ssd_sc", 0))
        if tt == 0:
            kb.wait("sync", s_pe, fr["HBC"])
            kb.dma("sync", LAm[0:1, :], c_onesrow[0:1, 0:1024], s_x)
            kb.dma("sync", LBm[1:2, :], c_negones[0:1, 0:1024], s_x)
            fr["LAB0"] = s_x.v
        kb.wait("tensor", s_sc, fr["SILUD"])
        kb.t("matmul", P1[:, 512:640], xpost[:, 4 * 512 + tt * 128: 4 * 512 + (tt + 1) * 128],
             sb_ident[:], start=True, stop=True, sem=s_pe)
        BTD = s_pe.v
        kb.wait("scalar", s_pe, BTD)
        kb.s("copy", out=BTt[:], in_=P1[:, 512:640], sem=s_sc)
        BTE = s_sc.v
        for cch in range(4):
            kb.t("matmul", P2[:, 512 + cch * 128: 512 + (cch + 1) * 128],
                 xpost[:, cch * 512 + tt * 128: cch * 512 + (tt + 1) * 128],
                 sb_ident[:], start=True, stop=True, sem=(s_pe if cch == 3 else None))
        XTD = s_pe.v
        kb.wait("tensor", s_cv, fr["ACUM"])
        kb.wait("tensor", s_sc, BTE)
        kb.t("matmul", P1[0:1, 656:664], acum[0:8, cg * 128 + 127: cg * 128 + 128], id8[:],
             start=True, stop=True, sem=s_pe)
        AEP = s_pe.v
        kb.wait("vector", s_pe, AEP)
        kb.v("tensor_scalar", rows4[0:1, 1024:1032], P1[0:1, 656:664], -1.0, None, op0=ALU.mult)
        kb.v("tensor_scalar", rows4[0:1, 1040:1048], aendr[0:1, 0:8], -1.0, None, op0=ALU.mult)
        kb.v("tensor_tensor", out=rows4[0:1, 1032:1040], in0=P1[0:1, 656:664],
             in1=aendr[0:1, 0:8], op=ALU.subtract)
        kb.v("tensor_copy", out=aendr[0:1, 0:8], in_=P1[0:1, 656:664], sem=s_cv)
        AER = s_cv.v
        kb.wait("scalar", s_cv, AER)
        kb.s("activation", out=rows4[0:1, 1032:1040], in_=rows4[0:1, 1032:1040], func=AF.Exp, sem=s_sc)
        LROW = s_sc.v
        kb.wait("tensor", s_cv, AER)
        kb.t("matmul", P1[:, 640:648], acum[0:8, cg * 128:(cg + 1) * 128], id8[:], start=True, stop=False)
        kb.t("matmul", P1[:, 640:648], sb_onesf[0:1, 0:128], rows4[0:1, 1024:1032],
             start=False, stop=True, sem=s_pe)
        kb.t("matmul", P1[:, 664:672], acum[0:8, cg * 128:(cg + 1) * 128], id8[:], start=True, stop=False)
        kb.t("matmul", P1[:, 664:672], sb_onesf[0:1, 0:128], rows4[0:1, 1040:1048],
             start=False, stop=True, sem=s_pe)
        E2P = s_pe.v
        kb.wait("scalar", s_pe, E2P)
        kb.s("activation", out=e2t[:], in_=P1[:, 640:648], func=AF.Exp, scale=-1.0)
        kb.s("activation", out=e2i[:], in_=P1[:, 664:672], func=AF.Exp, sem=s_sc)
        E2D = s_sc.v
        d_e2 = dbg("h_e2_%d_%d" % (w_, tt), (128, 16))
        if d_e2 is not None:
            kb.wait("sync", s_sc, E2D)
            kb.dma("sync", d_e2[:, 0:8], e2t[:], s_st)
            kb.dma("sync", d_e2[:, 8:16], e2i[:], s_st)
        kb.wait("tensor", s_sc, E2D)
        kb.t("matmul", P1[:, 648:656], sb_onesf[0:1, 0:128], rows4[0:1, 1032:1040],
             start=True, stop=True, sem=s_pe)
        LBD = s_pe.v
        kb.wait("vector", s_pe, XTD)
        kb.v("tensor_tensor", out=dtx[:].rearrange("p (h j) -> p h j", h=8),
             in0=P2[:, 512:1024].rearrange("p (h j) -> p h j", h=8),
             in1=mydt[:, tt * 8:(tt + 1) * 8].rearrange("p (h o) -> p h o", o=1).broadcast_to([128, 8, 64]),
             op=ALU.mult)
        kb.wait("vector", s_sc, E2D)
        kb.v("tensor_mul", tk8[:, 16:24], mydt[:, tt * 8:(tt + 1) * 8], e2t[:], sem=s_cv)
        kb.wait("vector", s_cv, s_cv.v)
        kb.v("tensor_tensor", out=wdtx[:].rearrange("p (h j) -> p h j", h=8),
             in0=P2[:, 512:1024].rearrange("p (h j) -> p h j", h=8),
             in1=tk8[:, 16:24].rearrange("p (h o) -> p h o", o=1).broadcast_to([128, 8, 64]),
             op=ALU.mult, sem=s_cv)
        DTXD = s_cv.v
        kb.t("matmul", P2[:, 0:128], xpost[:, 4 * 512 + tt * 128: 4 * 512 + (tt + 1) * 128],
             xpost[:, 5 * 512 + tt * 128: 5 * 512 + (tt + 1) * 128], start=True, stop=True, sem=s_pe)
        GD = s_pe.v
        kb.wait("vector", s_pe, GD)
        kb.v("tensor_tensor", out=Gms[:], in0=P2[:, 0:128], in1=sb_tri[:], op=ALU.mult, sem=s_cv)
        GMD = s_cv.v
        kb.wait("sync", s_cv, fr["ACUM"])
        kb.dma("sync", LAm[1:2, :], acum.ap()[0:8, cg * 128:(cg + 1) * 128], s_x)
        kb.dma("sync", LBm[0:1, :], acum.ap()[0:8, cg * 128:(cg + 1) * 128], s_x)
        LABD = s_x.v
        kb.wait("tensor", s_x, max(LABD, fr.get("LAB0", 0)))
        for grp in range(2):
            for hh in range(4):
                h = grp * 4 + hh
                kb.t("matmul", P3[:, hh * 128:(hh + 1) * 128], LAm[0:2, h * 128:(h + 1) * 128],
                     LBm[0:2, h * 128:(h + 1) * 128], start=True, stop=True,
                     sem=(s_pe if hh == 3 else None))
            MPD = s_pe.v
            kb.wait("vector", s_pe, MPD)
            kb.v("tensor_scalar", MWs[:], P3[:, 0:512], 0.0, None, op0=ALU.min, sem=s_cv)
            MWD = s_cv.v
            kb.wait("scalar", s_cv, MWD)
            kb.s("activation", out=PMt[:, grp * 512:(grp + 1) * 512], in_=MWs[:], func=AF.Exp, sem=s_sc)
            PME = s_sc.v
            kb.wait("tensor", s_sc, PME)
            kb.wait("vector", s_sc, PME)
            kb.v("tensor_tensor", out=PMt[:, grp * 512:(grp + 1) * 512].rearrange("p (h t) -> p h t", h=4),
                 in0=PMt[:, grp * 512:(grp + 1) * 512].rearrange("p (h t) -> p h t", h=4),
                 in1=Gms[:].rearrange("p (o t) -> p o t", o=1).broadcast_to([128, 4, 128]),
                 op=ALU.mult, sem=s_cv)
            kb.wait("vector", s_cv, GMD)
        PMD = s_cv.v
        d_mw = dbg("h_mw_%d_%d" % (w_, tt), (128, 512)) if tt == 0 else None
        if d_mw is not None:
            pass
        d_gm = dbg("h_gm_%d_%d" % (w_, tt), (128, 128), BF16) if tt == 0 else None
        if d_gm is not None:
            kb.wait("sync", s_cv, GMD)
            kb.dma("sync", d_gm, Gms[:], s_st)
        kb.wait("tensor", s_cv, PMD)
        kb.wait("tensor", s_cv, DTXD)
        for h in range(8):
            kb.t("matmul", P0[:, h * 64:(h + 1) * 64], PMt[:, h * 128:(h + 1) * 128],
                 dtx[:, h * 64:(h + 1) * 64], start=(h == 0), stop=(h == 7),
                 skip_group_check=True, sem=(s_pe if h == 7 else None))
        INTRA = s_pe.v
        kb.t("matmul", P0[:, 512:1024], xpost.ap().bitcast(BF16)[:, 5 * 512 + tt * 128: 5 * 512 + (tt + 1) * 128] if False else xpost[:, 5 * 512 + tt * 128: 5 * 512 + (tt + 1) * 128],
             state_bf[:], start=True, stop=True, sem=s_pe)
        INTER = s_pe.v
        kb.wait("tensor", s_sc, BTE)
        kb.t("matmul", P1[:, 0:512], BTt[:], wdtx[:], start=True, stop=True, sem=s_pe)
        TCD = s_pe.v
        kb.wait("vector", s_pe, INTER)
        kb.v("tensor_tensor", out=ytmp[:].rearrange("p (h j) -> p h j", h=8),
             in0=P0[:, 512:1024].rearrange("p (h j) -> p h j", h=8),
             in1=e2i[:].rearrange("p (h o) -> p h o", o=1).broadcast_to([128, 8, 64]), op=ALU.mult)
        kb.wait("vector", s_pe, INTRA)
        kb.v("tensor_tensor", out=ytk[:], in0=P0[:, 0:512], in1=ytmp[:], op=ALU.add, sem=s_cv)
        YTK = s_cv.v
        kb.wait("vector", s_pe, LBD)
        kb.v("tensor_tensor", out=state[:].rearrange("p (h j) -> p h j", h=8),
             in0=state[:].rearrange("p (h j) -> p h j", h=8),
             in1=P1[:, 648:656].rearrange("p (h o) -> p h o", o=1).broadcast_to([128, 8, 64]), op=ALU.mult)
        kb.wait("vector", s_pe, TCD)
        kb.v("tensor_add", state[:], state[:], P1[:, 0:512])
        kb.v("tensor_copy", out=state_bf[:], in_=state[:], sem=s_cv)
        STU = s_cv.v
        d_pm = dbg("h_pm_%d_%d" % (w_, tt), (128, 1024), BF16)
        if d_pm is not None:
            kb.wait("sync", s_cv, PMD)
            kb.dma("sync", d_pm, PMt[:], s_st)
        d_dtx = dbg("h_dtx_%d_%d" % (w_, tt), (128, 512), BF16)
        if d_dtx is not None:
            kb.wait("sync", s_cv, DTXD)
            kb.dma("sync", d_dtx, dtx[:], s_st)
        d_ytk = dbg("h_ytk_%d_%d" % (w_, tt), (128, 512), BF16)
        if d_ytk is not None:
            kb.wait("sync", s_cv, YTK)
            kb.dma("sync", d_ytk, ytk[:], s_st)

        kb.wait("tensor", s_cv, YTK)
        for cch in range(4):
            kb.t("matmul", P2[:, 512 + cch * 128: 512 + (cch + 1) * 128],
                 ytk[:, cch * 128:(cch + 1) * 128], sb_ident[:], start=True, stop=True,
                 sem=(s_pe if cch == 3 else None))
        YTR = s_pe.v
        kb.wait("vector", s_pe, YTR)
        for cch in range(4):
            kb.v("scalar_tensor_tensor", yfm[:, cch * 512 + tt * 128: cch * 512 + (tt + 1) * 128],
                 xpost[:, cch * 512 + tt * 128: cch * 512 + (tt + 1) * 128],
                 sb_vecs[:, V_D + cch:V_D + cch + 1],
                 P2[:, 512 + cch * 128: 512 + (cch + 1) * 128],
                 op0=ALU.mult, op1=ALU.add, sem=(s_cv if cch == 3 else None))
        YFMD = s_cv.v
        prevg["ssd_cv"] = s_cv.v
        prevg["ssd_sc"] = s_sc.v
        prevg["ssd_pe"] = s_pe.v
        return YFMD

    def hades_tail(w_, fr):
        kb.soft_barrier((s_pe, s_cv, s_sc, s_stg))
        d_yf = dbg("h_yfm_%d" % w_, (128, 2048), BF16)
        if d_yf is not None:
            kb.wait("sync", s_cv, s_cv.v)
            kb.dma("sync", d_yf, yfm[:], s_st)
        kb.wait("scalar", s_cv, s_cv.v)
        for cch in range(4):
            kb.s("activation", out=zw[:, cch * 512:(cch + 1) * 512], in_=zw[:, cch * 512:(cch + 1) * 512],
                 func=AF.Silu, sem=(s_sc if cch == 3 else None))
        SZ = s_sc.v
        kb.wait("vector", s_sc, SZ)
        for cch in range(4):
            kb.v("tensor_mul", yzt[:, cch * 512:(cch + 1) * 512], yfm[:, cch * 512:(cch + 1) * 512],
                 zw[:, cch * 512:(cch + 1) * 512])
            kb.v("tensor_scalar", yzw[:, cch * 512:(cch + 1) * 512], yzt[:, cch * 512:(cch + 1) * 512],
                 sb_vecs[:, V_NORMW + cch:V_NORMW + cch + 1], None, op0=ALU.mult,
                 sem=(s_cv if cch == 3 else None))
        YZD = s_cv.v
        prevg["yz"] = YZD
        kb.wait("scalar", s_cv, YZD)
        for cch in range(4):
            kb.s("activation", out=cumw[:, cch * 512:(cch + 1) * 512], in_=yzt[:, cch * 512:(cch + 1) * 512],
                 func=AF.Square, sem=(s_sc if cch == 3 else None))
        SQD = s_sc.v
        kb.wait("tensor", s_sc, SQD)
        for kc in range(4):
            kb.t("matmul", P3[0:1, 0:512], sb_ones[:, 0:1], cumw[:, kc * 512:(kc + 1) * 512],
                 start=(kc == 0), stop=(kc == 3), sem=(s_pe if kc == 3 else None))
        YQS = s_pe.v
        kb.wait("vector", s_pe, YQS)
        kb.v("tensor_copy", out=rows4[0:1, 0:512], in_=P3[0:1, 0:512], sem=s_cv)
        YQR = s_cv.v
        kb.soft_barrier((s_pe, s_cv, s_sc))
        OPB = [P0.ap()[:, 0:512], P0.ap()[:, 512:1024], P1.ap()[:, 0:512], P1.ap()[:, 512:1024]]
        kb.wait("tensor", s_cv, YZD)
        for grp in range(2):
            for oci in range(4):
                oc = grp * 4 + oci
                for kc in range(4):
                    kb.t("matmul", OPB[oci], w_outp_b[:, kc * 1024 + oc * 128: kc * 1024 + oc * 128 + 128],
                         yzw[:, kc * 512:(kc + 1) * 512], start=(kc == 0), stop=(kc == 3),
                         sem=(s_pe if (kc == 3 and oci == 3) else None))
            OPD = s_pe.v
            kb.wait("scalar", s_pe, OPD)
            if grp == 0:
                kb.wait("scalar", s_x, s_x.v)
            for oci in range(4):
                oc = grp * 4 + oci
                kb.s("copy", out=big1[:, oc * 512:(oc + 1) * 512], in_=OPB[oci],
                     sem=(s_sc if oci == 3 else None))
            OPE = s_sc.v
            kb.wait("tensor", s_sc, OPE)
        prevg["op"] = s_sc.v
        kb.wait("sync", s_sc, s_sc.v)
        kb.wait("sync", s_cv, YQR)
        for k_ in range(2):
            kb.dma("sync",
                   e2_in.ap()[k_ * 1025: k_ * 1025 + 1024, w_ * 256:(w_ + 1) * 256]
                   .rearrange("(c p) t -> p c t", p=128),
                   big1[:].rearrange("p (c t) -> p c t", c=8)[:, :, k_ * 256:(k_ + 1) * 256], s_stg)
            kb.dma("sync", e2_in.ap()[k_ * 1025 + 1024: k_ * 1025 + 1025, w_ * 256:(w_ + 1) * 256],
                   rows4[0:1, k_ * 256:(k_ + 1) * 256], s_stg)
        prevg["e2st"] = s_stg.v
        d_dt = dbg("h_dttok_%d" % w_, (128, 512))
        if d_dt is not None:
            kb.wait("sync", s_cv, s_cv.v)
            kb.dma("sync", d_dt, dt_tok[:], s_st)

    NWH = 8
    FUP = 9
    ssd_on = ph >= 31 or (3 < ph < 30)
    tail_on = ph >= 32 or (3 < ph < 30)
    for w_ in (range(NWH) if ph >= 3 else []):
        fr = hades_front(w_)
        if ssd_on:
            for tt in range(4):
                ssd_chunk(w_, tt, fr)
        if tail_on:
            hades_tail(w_, fr)
    E2_STAGED = s_stg.v

    if ph < 3 or (30 <= ph < 33):
        kb.wait("sync", s_stg, s_stg.v)
        d_e2t = dbg("e2in_t", (2050, 2048))
        if d_e2t is not None:
            kb.dma("sync", d_e2t, e2_in.ap(), s_st)
            kb.wait("sync", s_st, s_st.v)
        kb.wait("sync", s_cv, s_cv.v)
        kb.wait("sync", s_sc, s_sc.v)
        kb.wait("sync", s_pe, s_pe.v)
        kb.wait("sync", s_x, s_x.v)
        for wo_ in range(4):
            kb.dma("sync", out_mine.rearrange("(c p) t -> p c t", p=128)[:, :, wo_ * 512:(wo_ + 1) * 512],
                   big1[:].rearrange("p (c t) -> p c t", c=8), s_out)
        kb.wait("sync", s_out, s_out.v)
        e3.close()
        with nc.Block() as block:
            @block.tensor
            def _(e):
                for f in kb.ops["tensor"]:
                    f(e)
            @block.vector
            def _(e):
                for f in kb.ops["vector"]:
                    f(e)
            @block.scalar
            def _(e):
                for f in kb.ops["scalar"]:
                    f(e)
            @block.gpsimd
            def _(e):
                for f in kb.ops["gpsimd"]:
                    f(e)
            @block.sync
            def _(e):
                for f in kb.ops["sync"]:
                    f(e)
        es.close()
        return nc, dbg_aps
    kb.wait("gpsimd", s_stg, E2_STAGED)
    d_e2pre = dbg("e2in_pre", (2050, 2048))
    if d_e2pre is not None:
        kb.wait("sync", s_stg, E2_STAGED)
        kb.dma("sync", d_e2pre, e2_in.ap(), s_st)
        kb.wait("gpsimd", s_st, s_st.v)
    s_cc2.bump(1)
    kb.emit("gpsimd", lambda e: e.collective_compute(
        "AllGather", ALU.bypass, replica_groups=GROUPS,
        ins=[e2_in.ap()], outs=[e2_out.ap()]).then_inc(s_cc2.h))
    E2_DONE = s_cc2.v

    e3.close()
    ef4 = ExitStack()
    E4 = ef4.enter_context
    x1w = E4(nc.sbuf_tensor("x1w4", [128, 4096], F32))
    xb2 = E4(nc.sbuf_tensor("xb24", [128, 4096], BF16))
    xq2 = E4(nc.sbuf_tensor("xq24", [128, 4096], BF16))
    xtmp = E4(nc.sbuf_tensor("xtmp4", [128, 4096], F32))
    xint = E4(nc.sbuf_tensor("xint4", [128, 4096], BF16))
    mmx = E4(nc.sbuf_tensor("mmx4", [128, 1040], BF16))
    soutb = E4(nc.sbuf_tensor("soutb4", [128, 512], F32))
    x2w = E4(nc.sbuf_tensor("x2w4", [128, 4096], F32))
    rows3 = E4(nc.sbuf_tensor("rows34", [128, 2048], F32))
    rsw = E4(nc.sbuf_tensor("rsw", [128, 4096], F32))
    rstdr = E4(nc.sbuf_tensor("rstdr", [1, 2048], F32))
    sb_rmask = E4(nc.sbuf_tensor("sb_rmask", [128, 2], F32))

    g4 = dict(pe=s_pe.v, cv=s_cv.v, sc=s_sc.v)
    for engx in ("tensor", "vector", "scalar", "sync", "gpsimd"):
        kb.wait(engx, s_pe, g4["pe"])
        kb.wait(engx, s_cv, g4["cv"])
        kb.wait(engx, s_sc, g4["sc"])
        kb.wait(engx, s_cc2, E2_DONE)
    kb.dma("sync", sb_rmask[:], c_rmask, s_x)
    kb.dma("sync", rstdr[0:1, :], e2_out.ap()[1024:1025, :], s_x)
    kb.dma("sync", rsw[0:1, 0:2048], e2_out.ap()[3074:3075, :], s_x)
    kb.dma("sync", rsw[0:1, 2048:4096], e2_out.ap()[2049:2050, :], s_x)
    kb.dma("sync", rows3[0:1, 0:2048], e2_out.ap()[4099:4100, :], s_x)
    RSR = s_x.v
    kb.wait("vector", s_x, RSR)
    kb.v("tensor_add", rstdr[0:1, :], rstdr[0:1, :], rsw[0:1, 0:2048], sem=s_cv)
    kb.wait("vector", s_cv, s_cv.v)
    kb.v("tensor_add", rows3[0:1, 0:2048], rows3[0:1, 0:2048], rsw[0:1, 2048:4096], sem=s_cv)
    kb.wait("vector", s_cv, s_cv.v)
    kb.v("tensor_scalar", rstdr[0:1, :], rstdr[0:1, :], sb_rmask[0:1, 0:1], None, op0=ALU.mult, sem=s_cv)
    kb.wait("vector", s_cv, s_cv.v)
    kb.v("scalar_tensor_tensor", rstdr[0:1, :], rows3[0:1, 0:2048], sb_rmask[0:1, 1:2], rstdr[0:1, :],
         op0=ALU.mult, op1=ALU.add, sem=s_cv)
    kb.wait("vector", s_cv, s_cv.v)
    kb.v("tensor_scalar_mul", rstdr[0:1, :], rstdr[0:1, :], 1.0 / 1024.0, sem=s_cv)
    kb.wait("scalar", s_cv, s_cv.v)
    kb.s("activation", out=rstdr[0:1, :], in_=rstdr[0:1, :], func=AF.Ln, bias=sb_epsb[0:1, 0:1])
    kb.s("activation", out=rstdr[0:1, :], in_=rstdr[0:1, :], func=AF.Exp, scale=-0.5, sem=s_sc)
    RSTD = s_sc.v
    d_rst = dbg("p4_rstd", (1, 2048))
    if d_rst is not None:
        kb.wait("sync", s_sc, RSTD)
        kb.dma("sync", d_rst, rstdr[0:1, :], s_st)
    d_e2o = dbg("p4_e2out", (4100, 2048))
    if d_e2o is not None:
        kb.wait("sync", s_cc2, E2_DONE)
        kb.dma("sync", d_e2o, e2_out.ap(), s_st)
    d_e2i = dbg("p4_e2in", (2050, 2048))
    if d_e2i is not None:
        kb.wait("sync", s_cc2, E2_DONE)
        kb.dma("sync", d_e2i, e2_in.ap(), s_st)

    def ld_x3(w_):
        kb.soft_barrier((s_pe, s_cv, s_sc, s_out))
        kb.wait("sync", s_cv, s_cv.v)
        kb.wait("sync", s_out, s_out.v)
        kb.dma("sync", x2w[:].rearrange("p (c t) -> p c t", c=8),
               e1_in.ap().rearrange("(c p) t -> p c t", p=128)[:, :, w_ * 512:(w_ + 1) * 512], s_x)
        for cch in range(8):
            kb.dma("sync", rsw[:, cch * 512:(cch + 1) * 512],
                   e2_out.ap()[1025 + cch * 128: 1025 + (cch + 1) * 128, w_ * 512:(w_ + 1) * 512], s_x)
            kb.dma("sync", xtmp[:, cch * 512:(cch + 1) * 512],
                   e2_out.ap()[3075 + cch * 128: 3075 + (cch + 1) * 128, w_ * 512:(w_ + 1) * 512], s_x)
        kb.wait("vector", s_x, s_x.v)
        kb.v("tensor_add", rsw[:], rsw[:], xtmp[:])
        kb.v("tensor_scalar", rsw[:], rsw[:], sb_rmask[:, 1:2], None, op0=ALU.mult, sem=s_cv)
        RSB = s_cv.v
        kb.wait("sync", s_cv, RSB)
        for cch in range(8):
            kb.dma("sync", xtmp[:, cch * 512:(cch + 1) * 512],
                   e2_out.ap()[cch * 128:(cch + 1) * 128, w_ * 512:(w_ + 1) * 512], s_x)
        kb.wait("vector", s_x, s_x.v)
        kb.v("scalar_tensor_tensor", rsw[:], xtmp[:], sb_rmask[:, 0:1], rsw[:],
             op0=ALU.mult, op1=ALU.add, sem=s_cv)
        kb.wait("sync", s_cv, s_cv.v)
        for cch in range(8):
            kb.dma("sync", xtmp[:, cch * 512:(cch + 1) * 512],
                   e2_out.ap()[2050 + cch * 128: 2050 + (cch + 1) * 128, w_ * 512:(w_ + 1) * 512], s_x)
        kb.wait("vector", s_x, s_x.v)
        kb.v("scalar_tensor_tensor", rsw[:], xtmp[:], sb_rmask[:, 0:1], rsw[:],
             op0=ALU.mult, op1=ALU.add, sem=s_cv)
        XL = s_x.v
        kb.wait("tensor", s_sc, RSTD)
        kb.wait("tensor", s_cv, s_cv.v)
        kb.t("matmul", P2[:, 0:512], sb_onesf[0:1, 0:128], rstdr[0:1, w_ * 512:(w_ + 1) * 512],
             start=True, stop=True, sem=s_pe)
        RB = s_pe.v
        d_p2 = dbg("p4_p2_%d" % w_, (128, 512))
        if d_p2 is not None:
            kb.wait("scalar", s_pe, RB)
            kb.s("copy", out=soutb[:], in_=P2[:, 0:512], sem=s_sc)
            kb.wait("sync", s_sc, s_sc.v)
            kb.dma("sync", d_p2, soutb[:], s_st)
            kb.wait("scalar", s_st, s_st.v)
        d_rpre = dbg("p4_rswpre_%d" % w_, (128, 4096))
        if d_rpre is not None:
            kb.wait("sync", s_x, XL)
            kb.dma("sync", d_rpre, rsw[:], s_st)
        if d_p2 is not None or d_rpre is not None:
            kb.wait("vector", s_st, s_st.v)
        kb.wait("vector", s_pe, RB)
        kb.wait("vector", s_x, XL)
        kb.v("tensor_tensor", out=rsw[:].rearrange("p (c t) -> p c t", c=8),
             in0=rsw[:].rearrange("p (c t) -> p c t", c=8),
             in1=P2[:, 0:512].rearrange("p (o t) -> p o t", o=1).broadcast_to([128, 8, 512]), op=ALU.mult)
        kb.v("tensor_add", x1w[:], x2w[:], rsw[:], sem=s_cv)
        X3D = s_cv.v
        d_x3 = dbg("p4_x3_%d" % w_, (128, 4096))
        if d_x3 is not None:
            kb.wait("sync", s_cv, X3D)
            kb.dma("sync", d_x3, x1w[:], s_st)
        d_rs = dbg("p4_rs_%d" % w_, (128, 4096))
        if d_rs is not None:
            kb.wait("sync", s_cv, X3D)
            kb.dma("sync", d_rs, rsw[:], s_st)
        kb.wait("scalar", s_cv, X3D)
        kb.wait("vector", s_cv, X3D)
        return s_x.v

    def st_out(w_):
        kb.wait("sync", s_cv, s_cv.v)
        kb.dma("sync", out_mine.rearrange("(c p) t -> p c t", p=128)[:, :, w_ * 512:(w_ + 1) * 512],
               x2w[:].rearrange("p (c t) -> p c t", c=8), s_out)

    if ph >= 4 and ph != 33:
        emit_ffn.st_sem = s_out
        emit_ffn({35: 1, 36: 2, 37: 3}.get(ph, 4), ld_x3, st_out)
    else:
        kb.wait("sync", s_cc, s_cc.v)
        kb.wait("sync", s_cc1, s_cc1.v)
        kb.wait("sync", s_cc2, s_cc2.v)
        for wo_ in range(4):
            kb.dma("sync", out_mine.rearrange("(c p) t -> p c t", p=128)[:, :, wo_ * 512:(wo_ + 1) * 512],
                   x1w[:].rearrange("p (c t) -> p c t", c=8), s_out)
    OUT_ALL = s_out.v

    ef4.close()

    kb.wait("sync", s_st, s_st.v)
    kb.wait("sync", s_out, OUT_ALL)
    d_e2e = dbg("e2in_end", (2050, 2048))
    if d_e2e is not None:
        kb.dma("sync", d_e2e, e2_in.ap(), s_st)
        kb.wait("sync", s_st, s_st.v)
    d_e2oe = dbg("e2out_end", (4100, 2048))
    if d_e2oe is not None:
        kb.dma("sync", d_e2oe, e2_out.ap(), s_st)
        kb.wait("sync", s_st, s_st.v)

    with nc.Block() as block:
        @block.tensor
        def _(e):
            for f in kb.ops["tensor"]:
                f(e)

        @block.vector
        def _(e):
            for f in kb.ops["vector"]:
                f(e)

        @block.scalar
        def _(e):
            for f in kb.ops["scalar"]:
                f(e)

        @block.gpsimd
        def _(e):
            for f in kb.ops["gpsimd"]:
                f(e)

        @block.sync
        def _(e):
            for f in kb.ops["sync"]:
                f(e)

    es.close()
    return nc, dbg_aps


def prep_inputs(inputs):
    f32 = np.float32
    x = np.ascontiguousarray(inputs["L0_attn_a_x"], dtype=f32)
    wq = np.ascontiguousarray(inputs["L0_attn_a_Wq"], dtype=f32)
    wkv = inputs["L0_attn_a_Wkv"]
    wk = np.ascontiguousarray(wkv[:, :1024], dtype=f32)
    wv = np.ascontiguousarray(wkv[:, 1024:], dtype=f32)
    wo = np.ascontiguousarray(inputs["L0_attn_a_Wo"], dtype=f32)
    w_inp_T = np.ascontiguousarray(inputs["L0_attn_b_in_proj_w"].T, dtype=f32)
    w_hp_T = np.ascontiguousarray(inputs["L0_attn_b_h_proj_w"].T, dtype=f32)
    w_outp_T = np.ascontiguousarray(inputs["L0_attn_b_out_proj_w"].T, dtype=f32)
    pT = np.ascontiguousarray(inputs["L0_ffn_packed_weight"].reshape(256, 1024).T)
    bias = np.ascontiguousarray(inputs["L0_ffn_bias"].reshape(8, 128).T, dtype=f32)
    conv_w = inputs["L0_attn_b_conv_w"]
    conv_b = inputs["L0_attn_b_conv_b"]
    gamma = inputs["L0_attn_b_gamma"]
    alog = inputs["L0_attn_b_A_log"]
    dskip = inputs["L0_attn_b_D"]
    normw = inputs["L0_attn_b_norm_w"]
    dtb = inputs["L0_attn_b_dt_bias"]
    wsc = inputs["L0_ffn_weight_scale"]

    bf = ml_dtypes.bfloat16
    ident = np.eye(128, dtype=bf)
    onesm = np.ones((128, 128), dtype=bf)
    tri = np.tril(np.ones((128, 128), dtype=np.float32)).T.astype(bf)
    id16 = np.zeros((17, 16), np.float32)
    id16[:16, :16] = np.eye(16)
    invden = np.broadcast_to(1.0 / np.arange(1, S + 1, dtype=np.float64).astype(np.float32), (40, S)).copy()

    kl = np.arange(128)[:, None]
    ql = np.arange(256)[None, :]
    tri_rect = (kl <= ql).astype(np.float32)
    stri_rect = (kl + 128 <= ql).astype(np.float32)
    zeros = np.zeros((128, 256), np.float32)
    onesr = np.ones((128, 256), np.float32)
    dm_A = np.concatenate([tri_rect, stri_rect, zeros, zeros], axis=1).astype(bf)
    dm_B = np.concatenate([onesr, onesr, tri_rect, stri_rect], axis=1).astype(bf)

    in_maps = []
    for c in range(8):
        p, r = c // 2, c % 2
        blocks = [2 * i + r for i in range(NBLK)]
        xb = x[p]
        xt_mine = np.ascontiguousarray(
            np.concatenate([xb[j * 256:(j + 1) * 256, :] for j in blocks], axis=0).T)
        heads = list(range(8 * r, 8 * r + 8))
        zrows = np.concatenate([np.arange(64 * h, 64 * h + 64) for h in heads])
        xrows = 1024 + zrows
        bc_rows = np.arange(2048, 2304)
        dt_rows = np.arange(2304, 2336)
        sel = np.concatenate([zrows, xrows, bc_rows, dt_rows])
        w_inp_mine = np.ascontiguousarray(w_inp_T[:, sel])
        w_outp_mine = np.ascontiguousarray(w_outp_T[zrows, :])
        convsel = np.concatenate([xrows - 1024, np.arange(1024, 1280)])
        cwm = conv_w[convsel]
        cbm = conv_b[convsel]
        conv_wv = np.ascontiguousarray(cwm.reshape(6, 128, 4).transpose(1, 0, 2).reshape(128, 24), dtype=f32)
        conv_bv = np.ascontiguousarray(cbm.reshape(6, 128).T, dtype=f32)
        d_mine = np.repeat(dskip[heads], 64).reshape(4, 128).T.astype(f32)
        normw_mine = normw[zrows].reshape(4, 128).T.astype(f32)
        wo_hm = np.ascontiguousarray(wo.reshape(16, 64, 1024).transpose(1, 0, 2).reshape(64, 16 * 1024))
        selm = np.zeros((16, 8), np.float32)
        for jj, hh in enumerate(heads):
            selm[hh, jj] = 1.0
        m = dict(
            c_sel=selm, c_id8=np.eye(8, dtype=np.float32),
            c_identf=np.eye(128, dtype=np.float32),
            c_onesrow=np.ones((1, 4096), np.float32),
            c_negones=np.full((1, 4096), -1.0, np.float32),
            xt=xt_mine, wq=wq, wk=wk, wv=wv, wo=wo_hm,
            w_inp=w_inp_mine, w_hp=w_hp_T, w_outp=w_outp_mine,
            ffn_pT=pT, ffn_bias=bias,
            conv_wv=conv_wv, conv_bv=np.ascontiguousarray(conv_bv),
            dtbias_rep=np.full((128, 1), dtb[0], f32),
            gamma_rep=np.broadcast_to(gamma[:12].astype(f32), (128, 12)).copy(),
            alog_my=np.ascontiguousarray(alog[heads].reshape(8, 1), dtype=f32),
            d_my=np.ascontiguousarray(d_mine), normw_my=np.ascontiguousarray(normw_mine),
            wscale=wsc.reshape(1, 1).astype(f32),
            c_ident=ident, c_ones=onesm, c_tri=tri,
            c_diagmask=(dm_A if r == 0 else dm_B),
            c_id16=id16, c_invden=invden,
            c_rmask=np.broadcast_to(
                np.array([1.0, 0.0] if r == 0 else [0.0, 1.0], np.float32)[None, :], (128, 2)).copy(),
            c_iota12=np.broadcast_to(
                np.broadcast_to(np.arange(12, dtype=bf)[:, None], (12, 28)).reshape(1, 336),
                (128, 336)).copy(),
        )
        in_maps.append(m)
    return in_maps


def kernel(**inputs):
    in_maps = prep_inputs(inputs)
    nc, _ = build(None, ph=4)
    res = run_bass_kernel_spmd(nc, in_maps, core_ids=list(range(8)))
    out = np.zeros((4, 4096, 1024), np.float32)
    for c in range(8):
        p, r = c // 2, c % 2
        om = res.results[c]["out_mine"]
        for i in range(NBLK):
            j = 2 * i + r
            out[p, j * 256:(j + 1) * 256, :] = om[:, i * 256:(i + 1) * 256].T
    return out


# version 02
# speedup vs baseline: 1.0012x; 1.0012x over previous
import numpy as np
import ml_dtypes
from contextlib import ExitStack
import concourse.bass as bass
import concourse.mybir as mybir
from concourse.bass_utils import run_bass_kernel_spmd

F32 = mybir.dt.float32
BF16 = mybir.dt.bfloat16
I32 = mybir.dt.int32
AF = mybir.ActivationFunctionType
ALU = mybir.AluOpType
AX = mybir.AxisListType.X
MAGIC = 12582912.0

S, Dm = 4096, 1024
NBLK, BLK = 8, 256
NW, WIN = 8, 512
EPS = 1e-5
GROUPS = [[0, 1], [2, 3], [4, 5], [6, 7]]

PHASE_END = 99


class Sem:
    def __init__(self, h):
        self.h = h
        self.v = 0

    def bump(self, k=1):
        self.v += k
        return self.v


class KB:
    def __init__(self, nc):
        self.nc = nc
        self.ops = {e: [] for e in ("tensor", "vector", "scalar", "gpsimd", "sync")}

    def emit(self, eng, fn):
        self.ops[eng].append(fn)

    def soft_barrier(self, sems):
        for eng in ("tensor", "vector", "scalar", "sync", "gpsimd"):
            for sm in sems:
                self.wait(eng, sm)

    def wait(self, eng, sem, v=None):
        vv = sem.v if v is None else v
        if vv <= 0:
            return
        self.emit(eng, lambda e, h=sem.h, w=vv: e.wait_ge(h, w))

    def dma(self, q, out, in_, sem):
        sem.bump(16)
        self.emit(q, lambda e, o=out, i=in_, h=sem.h: e.dma_start(out=o, in_=i).then_inc(h, 16))

    def dma_slow(self, q, out, in_, sem):
        sem.bump(16)
        self.emit(q, lambda e, o=out, i=in_, h=sem.h: e.dma_start(out=o, in_=i, allow_slow_non_contiguous=True).then_inc(h, 16))

    def v(self, name, *a, sem=None, **kw):
        self._c("vector", name, a, kw, sem)

    def s(self, name, *a, sem=None, **kw):
        self._c("scalar", name, a, kw, sem)

    def t(self, name, *a, sem=None, **kw):
        self._c("tensor", name, a, kw, sem)

    def g(self, name, *a, sem=None, **kw):
        self._c("gpsimd", name, a, kw, sem)

    def _c(self, eng, name, a, kw, sem):
        if sem is not None:
            sem.bump(1)
            self.emit(eng, lambda e, n=name, aa=a, kk=kw, h=sem.h: getattr(e, n)(*aa, **kk).then_inc(h, 1))
        else:
            self.emit(eng, lambda e, n=name, aa=a, kk=kw: getattr(e, n)(*aa, **kk))


def build(role_any, debug_keys=(), ph=4):
    nc = bass.Bass("TRN2", target_bir_lowering=False, debug=False)
    kb = KB(nc)
    es = ExitStack()
    E = es.enter_context

    def sem(name):
        return Sem(E(nc.semaphore(name)))

    def inp(name, shape, dt):
        return nc.dram_tensor(name, list(shape), dt, kind="ExternalInput").ap()

    xt = inp("xt", (Dm, 2048), F32)
    wq = inp("wq", (Dm, Dm), F32)
    wk = inp("wk", (Dm, Dm), F32)
    wv = inp("wv", (Dm, Dm), F32)
    wo = inp("wo", (64, 16 * Dm), F32)
    w_inp = inp("w_inp", (Dm, 1312), F32)
    w_hp = inp("w_hp", (1056, 40), F32)
    w_outp = inp("w_outp", (512, Dm), F32)
    ffn_pT = inp("ffn_pT", (Dm, 256), I32)
    ffn_bias = inp("ffn_bias", (128, 8), F32)
    conv_wv = inp("conv_wv", (128, 24), F32)
    conv_bv = inp("conv_bv", (128, 6), F32)
    dtbias_rep = inp("dtbias_rep", (128, 1), F32)
    gamma_rep = inp("gamma_rep", (128, 12), F32)
    alog_my = inp("alog_my", (8, 1), F32)
    d_my = inp("d_my", (128, 4), F32)
    normw_my = inp("normw_my", (128, 4), F32)
    wscale = inp("wscale", (1, 1), F32)
    c_ident = inp("c_ident", (128, 128), BF16)
    c_ones = inp("c_ones", (128, 128), BF16)
    c_tri = inp("c_tri", (128, 128), BF16)
    c_diagmask = inp("c_diagmask", (128, 1024), BF16)
    c_id16 = inp("c_id16", (17, 16), F32)
    c_sel = inp("c_sel", (16, 8), F32)
    c_id8 = inp("c_id8", (8, 8), F32)
    c_identf = inp("c_identf", (128, 128), F32)
    c_onesrow = inp("c_onesrow", (1, 4096), F32)
    c_negones = inp("c_negones", (1, 4096), F32)
    c_invden = inp("c_invden", (40, S), F32)
    c_iota12 = inp("c_iota12", (128, 336), BF16)
    c_rmask = inp("c_rmask", (128, 2), F32)

    out_mine = nc.dram_tensor("out_mine", [Dm, 2048], F32, kind="ExternalOutput").ap()

    dbg_aps = {}

    def dbg(name, shape, dt=F32):
        if name in debug_keys:
            dbg_aps[name] = nc.dram_tensor("dbg_" + name, list(shape), dt, kind="ExternalOutput").ap()
            return dbg_aps[name]
        return None

    e0_in = nc.dram_tensor("e0_in", [4096, 1024], BF16, kind="Internal")
    e0_out = nc.dram_tensor("e0_out", [8192, 1024], BF16, kind="Internal")
    e1_in = nc.dram_tensor("e1_in", [1024, 2048], F32, kind="Internal")
    e1_out = nc.dram_tensor("e1_out", [2048, 2048], F32, kind="Internal")
    e2_in = nc.dram_tensor("e2_in", [2050, 2048], F32, kind="Internal")
    e2_out = nc.dram_tensor("e2_out", [4100, 2048], F32, kind="Internal")
    x1d = nc.dram_tensor("x1d", [1024, 2048], F32, kind="Internal")

    s_ld = sem("s_ld")
    s_cv = sem("s_cv")
    s_sc = sem("s_sc")
    s_pe = sem("s_pe")
    s_st = sem("s_st")
    s_cc = sem("s_cc")
    s_cc1 = sem("s_cc1")
    s_cc2 = sem("s_cc2")
    s_po = sem("s_po")
    s_out = sem("s_out")

    sb_ident = E(nc.sbuf_tensor("sb_ident", [128, 128], BF16))
    sb_ones = E(nc.sbuf_tensor("sb_ones", [128, 128], BF16))
    sb_tri = E(nc.sbuf_tensor("sb_tri", [128, 128], BF16))
    sb_dmask = E(nc.sbuf_tensor("sb_dmask", [128, 1024], BF16))
    sb_id16 = E(nc.sbuf_tensor("sb_id16", [17, 16], F32))
    sb_vecs = E(nc.sbuf_tensor("sb_vecs", [128, 64], F32))
    sb_cwv = E(nc.sbuf_tensor("sb_cwv", [128, 24], F32))
    sb_alog = E(nc.sbuf_tensor("sb_alog", [8, 2], F32))
    sb_wscale = E(nc.sbuf_tensor("sb_wscale", [1, 1], F32))
    sb_epsb = E(nc.sbuf_tensor("sb_epsb", [128, 1], F32))
    sb_onesf = E(nc.sbuf_tensor("sb_onesf", [128, 512], F32))
    V_BIAS, V_CONVB, V_D, V_NORMW, V_DTB, V_GAMMA = 0, 8, 14, 18, 22, 24

    rows1 = E(nc.sbuf_tensor("rows1", [128, 2048], F32))
    rowbf = E(nc.sbuf_tensor("rowbf", [128, 2048], BF16))
    ROWP = (0, 32, 64, 96)

    def row(t, i, off, width):
        return t[ROWP[i]:ROWP[i] + 1, off:off + width]

    sb_csum = E(nc.sbuf_tensor("sb_csum", [128, 4424], BF16))
    CS_Q, CS_K, CS_V, CS_INP, CS_HP = 0, 1024, 2048, 3072, 4384

    sb_ffnw = E(nc.sbuf_tensor("sb_ffnw", [128, 8 * 1024], BF16))

    P0 = E(nc.psum_tensor("P0", [128, 1024], F32))
    P1 = E(nc.psum_tensor("P1", [128, 1024], F32))
    P2 = E(nc.psum_tensor("P2", [128, 1024], F32))
    P3 = E(nc.psum_tensor("P3", [128, 1024], F32))

    kb.dma("sync", sb_ident[:], c_ident, s_ld)
    kb.dma("sync", sb_ones[:], c_ones, s_ld)
    kb.dma("sync", sb_tri[:], c_tri, s_ld)
    kb.dma("sync", sb_dmask[:], c_diagmask, s_ld)
    kb.dma("sync", sb_id16[:], c_id16, s_ld)
    kb.dma("sync", sb_vecs[:, 0:8], ffn_bias, s_ld)
    kb.dma("sync", sb_vecs[:, 8:14], conv_bv, s_ld)
    kb.dma("sync", sb_vecs[:, 14:18], d_my, s_ld)
    kb.dma("sync", sb_vecs[:, 18:22], normw_my, s_ld)
    kb.dma("sync", sb_vecs[:, 22:23], dtbias_rep, s_ld)
    kb.dma("sync", sb_vecs[:, 24:36], gamma_rep, s_ld)
    kb.dma("sync", sb_cwv[:], conv_wv, s_ld)
    kb.dma("sync", sb_alog[:, 0:1], alog_my, s_ld)
    kb.dma("sync", sb_wscale[:], wscale, s_ld)
    LD_CONST = s_ld.v
    kb.v("memset", sb_epsb[:], EPS)
    kb.v("memset", sb_onesf[:], 1.0)

    kb.wait("scalar", s_ld, LD_CONST)
    kb.s("activation", out=sb_alog[0:8, 1:2], in_=sb_alog[0:8, 0:1], func=AF.Exp, sem=s_sc)
    A_NEG_AT = s_sc.v
    kb.wait("vector", s_sc, A_NEG_AT)
    kb.v("tensor_scalar_mul", sb_alog[0:8, 1:2], sb_alog[0:8, 1:2], -1.0, sem=s_cv)
    A_READY = s_cv.v

    es_attn = ExitStack()
    EA = es_attn.enter_context
    qres = EA(nc.sbuf_tensor("qres", [128, 8 * 2048], BF16))
    wo_bA = EA(nc.sbuf_tensor("wo_bA", [64, 8 * 1024], BF16))
    wo_bB = EA(nc.sbuf_tensor("wo_bB", [64, 8 * 1024], BF16))

    es_1a = ExitStack()
    E1 = es_1a.enter_context
    wcv = E1(nc.sbuf_tensor("wcv", [128, 2 * 1024], F32))
    wq_b = E1(nc.sbuf_tensor("wq_b", [128, 8 * 1024], BF16))
    wk_b = E1(nc.sbuf_tensor("wk_b", [128, 8 * 1024], BF16))
    wv_b = E1(nc.sbuf_tensor("wv_b", [128, 8 * 1024], BF16))
    xblk = E1(nc.sbuf_tensor("xblk", [128, 2 * 2048], F32))
    xb16 = E1(nc.sbuf_tensor("xb16", [128, 2048], BF16))
    xsq = E1(nc.sbuf_tensor("xsq", [128, 2048], BF16))
    xn1 = E1(nc.sbuf_tensor("xn1", [128, 2048], BF16))
    kst = E1(nc.sbuf_tensor("kst", [128, 2048], BF16))
    vst = E1(nc.sbuf_tensor("vst", [128, 2 * 1024], BF16))

    s_wld = [sem("s_wld0"), sem("s_wld1")]
    s_wcv_v = sem("s_wcv_v")
    s_wcv_g = sem("s_wcv_g")
    conv_hist = []

    def load_convert_w(wsrc, wdstb, ncols, eng="vector"):
        kchunks = wsrc.shape[0] // 128
        wv_dram = wsrc.rearrange("(c p) n -> p c n", p=128)
        for kc in range(kchunks):
            par = load_convert_w.par = getattr(load_convert_w, "par", 0)
            buf = wcv[:, (par % 2) * 1024:(par % 2) * 1024 + ncols]
            prev = [h for h in conv_hist if h[2] == par % 2]
            if prev:
                kb.wait("sync", prev[-1][0], prev[-1][1])
            kb.dma("sync", buf, wv_dram[:, kc, :], s_wld[par % 2])
            kb.wait(eng, s_wld[par % 2])
            cs = s_wcv_v if eng == "vector" else s_wcv_g
            kb._c(eng, "tensor_copy", (), dict(out=wdstb[:, kc * ncols:(kc + 1) * ncols], in_=buf), cs)
            conv_hist.append((cs, cs.v, par % 2))
            load_convert_w.par = par + 1

    load_convert_w(wq, wq_b, 1024, "vector")
    load_convert_w(wk, wk_b, 1024, "gpsimd")
    load_convert_w(wv, wv_b, 1024, "vector")
    wo_d = wo.rearrange("p (c n) -> p c n", c=16)
    for kc in range(16):
        par = load_convert_w.par = getattr(load_convert_w, "par", 0)
        buf = wcv[0:64, (par % 2) * 1024:(par % 2) * 1024 + 1024]
        prev = [h for h in conv_hist if h[2] == par % 2]
        if prev:
            kb.wait("sync", prev[-1][0], prev[-1][1])
        kb.dma("sync", buf, wo_d[0:64, kc, :], s_wld[par % 2])
        kb.wait("gpsimd", s_wld[par % 2])
        dstw = wo_bA if kc < 8 else wo_bB
        kb._c("gpsimd", "tensor_copy", (), dict(out=dstw[0:64, (kc % 8) * 1024:(kc % 8 + 1) * 1024], in_=buf), s_wcv_g)
        conv_hist.append((s_wcv_g, s_wcv_g.v, par % 2))
        load_convert_w.par = par + 1
    W_ATTN_READY_VV = s_wcv_v.v
    W_ATTN_READY_VG = s_wcv_g.v

    s_cs = sem("s_cs")
    kb.wait("tensor", s_wcv_v, W_ATTN_READY_VV)
    kb.wait("tensor", s_wcv_g, W_ATTN_READY_VG)
    kb.wait("tensor", s_ld, LD_CONST)
    for wi, (wb, coff) in enumerate(((wq_b, CS_Q), (wk_b, CS_K), (wv_b, CS_V))):
        for oc in range(2):
            for kc in range(8):
                kb.t("matmul", P3[0:1, 0:512], sb_ones[:, 0:1],
                     wb[:, kc * 1024 + oc * 512: kc * 1024 + (oc + 1) * 512],
                     start=(kc == 0), stop=(kc == 7),
                     sem=s_pe if kc == 7 else None)
            v_pe = s_pe.v
            kb.wait("vector", s_pe, v_pe)
            kb.v("tensor_scalar", sb_csum[0:1, coff + oc * 512: coff + (oc + 1) * 512],
                 P3[0:1, 0:512], -1.0, None, op0=ALU.mult, sem=s_cv)
            kb.wait("tensor", s_cv, s_cv.v)
    CS_READY = s_cv.v

    s_x = sem("s_x")
    s_ev = sem("s_ev")
    s_stg = sem("s_stg")

    for i in range(NBLK):
        bb = i % 2
        xcur = xblk[:, bb * 2048:(bb + 1) * 2048]
        kb.dma("sync", xcur.rearrange("p (c t) -> p c t", c=8),
               xt.rearrange("(c p) t -> p c t", p=128)[:, :, i * 256:(i + 1) * 256], s_x)
        XLD = s_x.v
        kb.wait("vector", s_x, XLD)
        kb.v("tensor_copy", out=xb16[:], in_=xcur, sem=s_cv)
        XB16 = s_cv.v
        kb.wait("scalar", s_x, XLD)
        kb.s("activation", out=xsq[:], in_=xcur, func=AF.Square, sem=s_sc)
        XSQ = s_sc.v
        kb.wait("tensor", s_cv, XB16)
        kb.wait("tensor", s_sc, XSQ)
        for kc in range(8):
            kb.t("matmul", P3[0:1, 0:256], sb_ones[:, 0:1], xb16[:, kc * 256:(kc + 1) * 256],
                 start=(kc == 0), stop=(kc == 7))
        for kc in range(8):
            kb.t("matmul", P3[0:1, 256:512], sb_ones[:, 0:1], xsq[:, kc * 256:(kc + 1) * 256],
                 start=(kc == 0), stop=(kc == 7), sem=(s_pe if kc == 7 else None))
        STATS = s_pe.v
        R_MU, R_M2, R_MU2, R_VAR, R_LNV, R_RHO, R_S = (k * 256 for k in range(7))
        kb.wait("vector", s_pe, STATS)
        kb.v("tensor_scalar_mul", rows1[0:1, R_MU:R_MU + 256], P3[0:1, 0:256], 1.0 / 1024.0)
        kb.v("tensor_scalar_mul", rows1[0:1, R_M2:R_M2 + 256], P3[0:1, 256:512], 1.0 / 1024.0)
        kb.v("tensor_mul", rows1[0:1, R_MU2:R_MU2 + 256], rows1[0:1, R_MU:R_MU + 256], rows1[0:1, R_MU:R_MU + 256])
        kb.v("tensor_sub", rows1[0:1, R_VAR:R_VAR + 256], rows1[0:1, R_M2:R_M2 + 256], rows1[0:1, R_MU2:R_MU2 + 256], sem=s_cv)
        VAR = s_cv.v
        kb.wait("tensor", s_cv, VAR)
        kb.wait("scalar", s_cv, VAR)
        kb.s("activation", out=rows1[0:1, R_LNV:R_LNV + 256], in_=rows1[0:1, R_VAR:R_VAR + 256], func=AF.Ln, bias=sb_epsb[0:1, 0:1])
        kb.s("activation", out=rows1[0:1, R_RHO:R_RHO + 256], in_=rows1[0:1, R_LNV:R_LNV + 256], func=AF.Exp, scale=-0.5, sem=s_sc)
        RHO_S = s_sc.v
        kb.wait("vector", s_sc, RHO_S)
        kb.v("tensor_mul", rows1[0:1, R_S:R_S + 256], rows1[0:1, R_MU:R_MU + 256], rows1[0:1, R_RHO:R_RHO + 256])
        kb.v("tensor_copy", out=rowbf[0:1, 0:256], in_=rows1[0:1, R_S:R_S + 256])
        kb.v("tensor_copy", out=rowbf[0:1, 256:512], in_=rows1[0:1, R_RHO:R_RHO + 256], sem=s_cv)
        ROWS_BF = s_cv.v
        kb.wait("tensor", s_cv, ROWS_BF)
        kb.t("matmul", P2[:, 0:256], sb_ones[0:1, :], rowbf[0:1, 256:512], start=True, stop=True, sem=s_pe)
        RHOB = s_pe.v
        kb.wait("vector", s_pe, RHOB)
        kb.v("tensor_tensor", out=xn1[:].rearrange("p (c t) -> p c t", c=8), in0=xcur.rearrange("p (c t) -> p c t", c=8),
             in1=P2[:, 0:256].rearrange("p (o t) -> p o t", o=1).broadcast_to([128, 8, 256]), op=ALU.mult, sem=s_cv)
        XN1 = s_cv.v
        kb.wait("tensor", s_cv, XN1)

        stg0 = s_stg.v
        kb.wait("scalar", s_stg, stg0)
        for proj, (wb, coff) in enumerate(((wq_b, CS_Q), (wk_b, CS_K))):
            for oc in range(8):
                pb = (P0 if oc < 4 else P1)
                pslice = pb[:, (oc % 4) * 256:(oc % 4 + 1) * 256]
                for kc in range(8):
                    kb.t("matmul", pslice, wb[:, kc * 1024 + oc * 128: kc * 1024 + oc * 128 + 128],
                         xn1[:, kc * 256:(kc + 1) * 256], start=(kc == 0), stop=False)
                kb.t("matmul", pslice, sb_csum[0:1, coff + oc * 128: coff + oc * 128 + 128],
                     rowbf[0:1, 0:256], start=False, stop=True,
                     sem=(s_pe if oc == 7 else None))
            PROJ = s_pe.v
            kb.wait("scalar", s_pe, PROJ)
            for oc in range(8):
                pb = (P0 if oc < 4 else P1)
                pslice = pb[:, (oc % 4) * 256:(oc % 4 + 1) * 256]
                dst = (qres[:, oc * 2048 + i * 256: oc * 2048 + (i + 1) * 256] if proj == 0
                       else kst[:, oc * 256:(oc + 1) * 256])
                kb.s("copy", out=dst, in_=pslice, sem=(s_ev if oc == 7 else None))
            EVAC = s_ev.v
            kb.wait("tensor", s_ev, EVAC)
        kb.wait("sync", s_ev, s_ev.v)
        kb.dma("sync",
               e0_in.ap()[0:2048, :].rearrange("(c p t2) n -> p c (t2 n)", p=128, t2=2)[:, :, i * 256:(i + 1) * 256],
               kst[:].rearrange("p (c t) -> p c t", c=8), s_stg)

        for tt in range(2):
            for half in range(2):
                pb = (P0 if half == 0 else P1)
                for kc in range(8):
                    kb.t("matmul", pb[:, 0:512],
                         xn1[:, kc * 256 + tt * 128: kc * 256 + tt * 128 + 128],
                         wv_b[:, kc * 1024 + half * 512: kc * 1024 + (half + 1) * 512],
                         start=(kc == 0), stop=False)
                kb.t("matmul", pb[:, 0:512], rowbf[0:1, tt * 128: tt * 128 + 128],
                     sb_csum[0:1, CS_V + half * 512: CS_V + (half + 1) * 512],
                     start=False, stop=True, sem=(s_pe if half == 1 else None))
            VPROJ = s_pe.v
            kb.wait("scalar", s_pe, VPROJ)
            kb.s("copy", out=vst[:, tt * 1024: tt * 1024 + 512], in_=P0[:, 0:512])
            kb.s("copy", out=vst[:, tt * 1024 + 512: (tt + 1) * 1024], in_=P1[:, 0:512], sem=s_ev)
            kb.wait("tensor", s_ev, s_ev.v)
        kb.wait("sync", s_ev, s_ev.v)
        kb.dma("sync", e0_in.ap()[2048 + i * 256: 2048 + (i + 1) * 256, :]
               .rearrange("(tt p) n -> p tt n", p=128),
               vst[:].rearrange("p (tt n) -> p tt n", tt=2), s_stg)
        kb.wait("sync", s_cv, XN1)

    d_wk = dbg("wk_b", (128, 8 * 1024), BF16)
    if d_wk is not None:
        kb.wait("sync", s_wcv_g, W_ATTN_READY_VG)
        kb.dma("sync", d_wk, wk_b[:], s_st)
    d_qf = dbg("qres_bf", (128, 8 * 2048), BF16)
    if d_qf is not None:
        kb.wait("sync", s_ev, s_ev.v)
        kb.dma("sync", d_qf, qres[:], s_st)
    d_e0 = dbg("e0_in", (4096, 1024), BF16)
    if d_e0 is not None:
        kb.wait("sync", s_stg, s_stg.v)
        kb.dma("sync", d_e0, e0_in.ap(), s_st)

    es_1a.close()

    s_x1 = sem("s_x1")
    kb.dma("gpsimd", x1d.ap(), xt, s_x1)
    X1_INIT = s_x1.v

    kb.wait("gpsimd", s_stg, s_stg.v)
    s_cc.bump(1)
    kb.emit("gpsimd", lambda e: e.collective_compute(
        "AllGather", ALU.bypass, replica_groups=GROUPS,
        ins=[e0_in.ap()], outs=[e0_out.ap()]).then_inc(s_cc.h))
    E0_DONE = s_cc.v

    fl = ExitStack()
    EF = fl.enter_context
    ktile = EF(nc.sbuf_tensor("ktile", [128, 2 * 1024], BF16))
    vtile = EF(nc.sbuf_tensor("vtile", [128, 2 * 1040], BF16))
    pbuf = EF(nc.sbuf_tensor("pbuf", [128, 2 * 1024], BF16))
    osb = EF(nc.sbuf_tensor("osb", [64, 2048], BF16))
    dbsb = EF(nc.sbuf_tensor("dbsb", [64, 512], F32))
    x1part = EF(nc.sbuf_tensor("x1part", [128, 2048], F32))
    pTt = EF(nc.sbuf_tensor("pTt", [128, 8 * 256], I32))
    pTs = EF(nc.sbuf_tensor("pTs", [128, 256], I32))
    pTf = EF(nc.sbuf_tensor("pTf", [128, 256], F32))

    OB = [P2.ap()[:, 0:512], P2.ap()[:, 512:1024], P3.ap()[:, 0:512], P3.ap()[:, 512:1024]]
    SB = [P0, P1]

    s_pv = sem("s_pv")
    s_fl = sem("s_fl")
    s_ex = sem("s_ex")
    s_kv = [sem("s_kv0"), sem("s_kv1")]

    kb.wait("vector", s_ev, s_ev.v)
    kb.v("memset", vtile.ap().rearrange("p (b kt h j) -> p b kt h j", b=2, kt=2, h=8)[:, :, :, :, 64:65], 1.0, sem=s_cv)
    VONES = s_cv.v
    kb.wait("sync", s_cv, VONES)

    s_up = sem("s_up")
    kb.dma("sync", pTt[:].rearrange("p (c n) -> p c n", c=8),
           ffn_pT.rearrange("(c p) n -> p c n", p=128), s_up)
    UPLD = s_up.v
    kb.wait("vector", s_up, UPLD)
    for ic in range(8):
        for sft in range(4):
            kb.v("tensor_scalar", pTs[:], pTt[:, ic * 256:(ic + 1) * 256], 2 * sft, 3,
                 op0=ALU.logical_shift_right, op1=ALU.bitwise_and)
            kb.v("tensor_copy", out=pTf[:], in_=pTs[:])
            kb.v("tensor_scalar", sb_ffnw[:, ic * 1024 + sft * 256: ic * 1024 + (sft + 1) * 256],
                 pTf[:], 1.0, None, op0=ALU.subtract,
                 sem=(s_up if (ic == 7 and sft == 3) else None))
    FFNW_RDY = s_up.v

    e0k = [e0_out.ap()[4096 * rk: 4096 * rk + 2048, :].rearrange(
        "(c p t2) n -> p c (t2 n)", p=128, t2=2) for rk in range(2)]

    kb.wait("sync", s_cc, E0_DONE)

    itc = 0
    step = 0
    bank_hist = {0: [], 1: []}
    phist = {0: [], 1: []}
    obank_free = [0, 0, 0, 0]

    QRW = [qres, qres]
    for w in range(2):
        for i in range(8):
            NKT = 2 * i + 2
            opv_done = [0] * 4
            for t in range(NKT):
                buf = itc % 2
                if itc >= 2:
                    kb.wait("sync", s_fl, itc - 1)
                rank, slot = t % 2, t // 2
                kb.dma("sync", ktile[:, buf * 1024:(buf + 1) * 1024].rearrange("p (c t) -> p c t", c=4),
                       e0k[rank][:, 4 * w: 4 * w + 4, slot * 256:(slot + 1) * 256], s_kv[buf])
                for kt_ in range(2):
                    kb.dma("sync",
                           vtile[:, buf * 1040 + kt_ * 520: buf * 1040 + (kt_ + 1) * 520]
                           .rearrange("p (h j) -> p h j", h=8)[:, :, 0:64],
                           e0_out.ap()[4096 * rank + 2048 + slot * 256 + kt_ * 128:
                                       4096 * rank + 2048 + slot * 256 + (kt_ + 1) * 128,
                                       512 * w: 512 * (w + 1)].rearrange("p (h j) -> p h j", h=8),
                           s_kv[buf])
                TILE_LD = s_kv[buf].v
                kb.wait("tensor", s_kv[buf], TILE_LD)
                for hp in range(4):
                    sbank = step % 2
                    Sp = SB[sbank]
                    if len(bank_hist[sbank]) >= 1:
                        kb.wait("tensor", s_ex, bank_hist[sbank][-1])
                    if t == 0:
                        kb.wait("tensor", s_sc, obank_free[hp])
                    for h01 in range(2):
                        h = 2 * hp + h01
                        c, b64 = h // 2, (h % 2) * 64
                        for kt in range(2):
                            kb.t("matmul", Sp[:, h01 * 512 + kt * 256: h01 * 512 + (kt + 1) * 256],
                                 ktile[b64:b64 + 64, buf * 1024 + c * 256 + kt * 128: buf * 1024 + c * 256 + (kt + 1) * 128],
                                 qres[b64:b64 + 64, (4 * w + c) * 2048 + i * 256: (4 * w + c) * 2048 + (i + 1) * 256],
                                 start=True, stop=True,
                                 sem=(s_pe if (h01 == 1 and kt == 1) else None))
                    QK_D = s_pe.v
                    phb = step % 2
                    phalf = pbuf[:, phb * 1024:(phb + 1) * 1024]
                    if len(phist[phb]) >= 1:
                        kb.wait("scalar", s_pv, phist[phb][-1])
                    kb.wait("scalar", s_pe, QK_D)
                    kb.s("activation", out=phalf, in_=Sp[:], func=AF.Exp, scale=0.125, sem=s_ex)
                    EXP_D = s_ex.v
                    bank_hist[sbank].append(EXP_D)
                    d_ph = dbg("ph_w%d_i%d_hp%d_t%d" % (w, i, hp, t), (128, 1024), BF16) if (w == 0 and i == 0) else None
                    d_s = dbg("s_w%d_i%d_hp%d_t%d" % (w, i, hp, t), (128, 1024), F32) if (w == 0 and i == 0) else None
                    if False and d_s is not None:
                        pass
                    masked = (t >= NKT - 2)
                    if masked:
                        win = t - (NKT - 2)
                        kb.wait("vector", s_ex, EXP_D)
                        for h01 in range(2):
                            kb.v("tensor_tensor", out=phalf[:, h01 * 512:(h01 + 1) * 512],
                                 in0=phalf[:, h01 * 512:(h01 + 1) * 512],
                                 in1=sb_dmask[:, win * 512:(win + 1) * 512],
                                 op=ALU.mult, sem=(s_cv if h01 == 1 else None))
                        MASK_D = s_cv.v
                        kb.wait("tensor", s_cv, MASK_D)
                    if d_ph is not None:
                        if masked:
                            kb.wait("sync", s_cv, MASK_D)
                        kb.dma("sync", d_ph, phalf, s_st)
                        kb.wait("tensor", s_st, s_st.v)
                    kb.wait("tensor", s_ex, EXP_D)
                    for kt in range(2):
                        for h01 in range(2):
                            h = 2 * hp + h01
                            kb.t("matmul", OB[hp][0:65, h01 * 256:(h01 + 1) * 256],
                                 vtile[:, buf * 1040 + kt * 520 + h * 65: buf * 1040 + kt * 520 + h * 65 + 65],
                                 phalf[:, h01 * 512 + kt * 256: h01 * 512 + (kt + 1) * 256],
                                 start=(t == 0 and kt == 0 and h01 == 0),
                                 stop=(t == NKT - 1 and kt == 1 and h01 == 1),
                                 skip_group_check=True,
                                 sem=(s_pv if (kt == 1 and h01 == 1) else None))
                    PV_D = s_pv.v
                    phist[phb].append(PV_D)
                    if t == NKT - 1:
                        opv_done[hp] = PV_D
                    if hp == 3:
                        s_fl.bump(1)
                        kb.emit("tensor", lambda e, h=s_fl.h, vv=None: e.sem_inc(h, 1))
                    step += 1
                itc += 1
            if bank_hist[0]:
                kb.wait("tensor", s_ex, bank_hist[0][-1])
            for hp in range(4):
                kb.wait("vector", s_pv, max(opv_done))
                kb.v("reciprocal", out=rows1[0:1, 0:512], in_=OB[hp][64:65, 0:512])
                kb.v("tensor_copy", out=rowbf[0:1, 0:512], in_=rows1[0:1, 0:512], sem=s_cv)
                RECIP_D = s_cv.v
                kb.wait("tensor", s_cv, RECIP_D)
                kb.t("matmul", P0[0:64, 0:512], sb_ones[0:1, 0:64], rowbf[0:1, 0:512],
                     start=True, stop=True, sem=s_pe)
                DB_D = s_pe.v
                kb.wait("scalar", s_pe, DB_D)
                kb.s("copy", out=dbsb[:], in_=P0[0:64, 0:512], sem=s_sc)
                DBS_D = s_sc.v
                kb.wait("vector", s_sc, DBS_D)
                kb.v("tensor_tensor", out=osb[0:64, hp * 512: (hp + 1) * 512],
                     in0=OB[hp][0:64, 0:512], in1=dbsb[:], op=ALU.mult, sem=s_cv)
                OSB_D = s_cv.v
                kb.wait("tensor", s_sc, DBS_D)
            OSB_ALL = s_cv.v
            kb.wait("tensor", s_cv, OSB_ALL)
            wot = wo_bA if w == 0 else wo_bB
            for oc in range(8):
                ob = OB[oc // 2]
                osl = ob[0:128, (oc % 2) * 256:(oc % 2 + 1) * 256]
                for h in range(8):
                    kb.t("matmul", osl, wot[0:64, h * 1024 + oc * 128: h * 1024 + oc * 128 + 128],
                         osb[0:64, h * 256:(h + 1) * 256],
                         start=(h == 0 and oc % 2 == 0), stop=(h == 7 and oc % 2 == 1),
                         skip_group_check=True,
                         sem=(s_pe if (h == 7 and oc == 7) else None))
            WO_D = s_pe.v
            kb.wait("scalar", s_pe, WO_D)
            for oc in range(8):
                ob = OB[oc // 2]
                osl = ob[0:128, (oc % 2) * 256:(oc % 2 + 1) * 256]
                kb.s("copy", out=x1part[:, oc * 256:(oc + 1) * 256], in_=osl,
                     sem=(s_sc if oc == 7 else None))
            WOE_D = s_sc.v
            for b_ in range(4):
                obank_free[b_] = WOE_D
            kb.wait("gpsimd", s_sc, WOE_D)
            kb.wait("gpsimd", s_x1, X1_INIT)
            s_x1.bump(16)
            kb.emit("gpsimd", lambda e, dst=x1d.ap().rearrange("(c p) t -> p c t", p=128)[:, :, i * 256:(i + 1) * 256],
                    srcp=x1part.ap().rearrange("p (c t) -> p c t", c=8), h=s_x1.h:
                    e.dma_start(out=dst, in_=srcp, accum_op=ALU.add).then_inc(h, 16))
            kb.wait("scalar", s_x1, s_x1.v)

    fl.close()
    es_attn.close()
    X1_ALL = s_x1.v

    d_x1 = dbg("x1d", (1024, 2048), F32)
    if d_x1 is not None:
        kb.wait("sync", s_x1, X1_ALL)
        kb.dma("sync", d_x1, x1d.ap(), s_st)

    ef2 = ExitStack()
    E2 = ef2.enter_context
    x1w = E2(nc.sbuf_tensor("x1w", [128, 4096], F32))
    xb2 = E2(nc.sbuf_tensor("xb2", [128, 4096], BF16))
    xq2 = E2(nc.sbuf_tensor("xq2", [128, 4096], BF16))
    xtmp = E2(nc.sbuf_tensor("xtmp", [128, 4096], F32))
    xint = E2(nc.sbuf_tensor("xint", [128, 4096], BF16))
    mmx = E2(nc.sbuf_tensor("mmx", [128, 1040], BF16))
    soutb = E2(nc.sbuf_tensor("soutb", [128, 512], F32))
    x2w = E2(nc.sbuf_tensor("x2w", [128, 4096], F32))
    rows3 = E2(nc.sbuf_tensor("rows3", [128, 2048], F32))

    RW = 512
    def emit_ffn(nwin, ld_x, store_out):
        emit_ffn.tag = getattr(emit_ffn, "tag", 0)
        emit_ffn.st_sem = getattr(emit_ffn, "st_sem", s_stg)
        """Quantized ternary FFN over nwin windows of 512 tokens.
        ld_x(w): emit loads of x-window into x1w [128, (8, 512)] f32, return s_x value.
        store_out(w): emit store of x2w; x2w = x1w + ffn(LN(x1w))."""
        for w_ in range(nwin):
            kb.wait("sync", s_cv, s_cv.v)
            kb.wait("sync", s_sc, s_sc.v)
            XW = ld_x(w_)
            kb.wait("vector", s_x, XW)
            kb.v("tensor_copy", out=xb2[:], in_=x1w[:], sem=s_cv)
            XB = s_cv.v
            kb.wait("scalar", s_x, XW)
            kb.s("activation", out=xq2[:], in_=x1w[:], func=AF.Square, sem=s_sc)
            XQ = s_sc.v
            kb.wait("tensor", s_cv, XB)
            kb.wait("tensor", s_sc, XQ)
            for kc in range(8):
                kb.t("matmul", P3[0:1, 0:512], sb_ones[:, 0:1], xb2[:, kc * 512:(kc + 1) * 512],
                     start=(kc == 0), stop=(kc == 7))
            for kc in range(8):
                kb.t("matmul", P3[0:1, 512:1024], sb_ones[:, 0:1], xq2[:, kc * 512:(kc + 1) * 512],
                     start=(kc == 0), stop=(kc == 7), sem=(s_pe if kc == 7 else None))
            STATS = s_pe.v
            kb.wait("vector", s_cv, XB)
            kb.v("tensor_tensor", out=mmx[:, 0:512], in0=xb2[:, 0:512], in1=xb2[:, 512:1024], op=ALU.max)
            for kc in range(2, 8):
                kb.v("tensor_tensor", out=mmx[:, 0:512], in0=mmx[:, 0:512],
                     in1=xb2[:, kc * 512:(kc + 1) * 512], op=ALU.max)
            kb.v("tensor_tensor", out=mmx[:, 512:1024], in0=xb2[:, 0:512], in1=xb2[:, 512:1024], op=ALU.min)
            for kc in range(2, 8):
                kb.v("tensor_tensor", out=mmx[:, 512:1024], in0=mmx[:, 512:1024],
                     in1=xb2[:, kc * 512:(kc + 1) * 512], op=ALU.min, sem=(s_cv if kc == 7 else None))
            FOLD = s_cv.v
            PTb = P2.ap().bitcast(BF16)
            kb.wait("tensor", s_cv, FOLD)
            for q_ in range(4):
                kb.t("transpose", PTb[:, q_ * 128:(q_ + 1) * 128], mmx[:, q_ * 128:(q_ + 1) * 128], sb_ident[:])
            for q_ in range(4):
                kb.t("transpose", PTb[:, 512 + q_ * 128: 512 + (q_ + 1) * 128],
                     mmx[:, 512 + q_ * 128: 512 + (q_ + 1) * 128], sb_ident[:],
                     sem=(s_pe if q_ == 3 else None))
            TRD = s_pe.v
            kb.wait("vector", s_pe, TRD)
            for q_ in range(4):
                kb.v("tensor_reduce", mmx[:, 1024 + q_: 1024 + q_ + 1], PTb[:, q_ * 128:(q_ + 1) * 128],
                     axis=AX, op=ALU.max)
            for q_ in range(4):
                kb.v("tensor_reduce", mmx[:, 1028 + q_: 1028 + q_ + 1], PTb[:, 512 + q_ * 128: 512 + (q_ + 1) * 128],
                     axis=AX, op=ALU.min, sem=(s_cv if q_ == 3 else None))
            RED = s_cv.v
            d_mm = dbg("ffn_mmx_%d_%d" % (emit_ffn.tag, w_), (128, 1032), BF16)
            if d_mm is not None:
                kb.wait("sync", s_cv, RED)
                kb.dma("sync", d_mm, mmx[:, 0:1032], s_st)
                kb.wait("tensor", s_st, s_st.v)
            kb.wait("tensor", s_cv, RED)
            for q_ in range(4):
                kb.t("matmul", P1[0:1, q_ * 128:(q_ + 1) * 128], mmx[:, 1024 + q_: 1024 + q_ + 1],
                     sb_ident[:], start=True, stop=True)
            for q_ in range(4):
                kb.t("matmul", P1[0:1, 512 + q_ * 128: 512 + (q_ + 1) * 128], mmx[:, 1028 + q_: 1028 + q_ + 1],
                     sb_ident[:], start=True, stop=True, sem=(s_pe if q_ == 3 else None))
            ROWS_PE = s_pe.v
            R_SCR, R_MU, R_RHO, R_MU2 = 0, 512, 1024, 1536
            R3_M, R3_C, R3_SOUT, R3_SCR = 0, 512, 1024, 1536
            kb.wait("vector", s_pe, ROWS_PE)
            kb.v("tensor_scalar_mul", rows1[0:1, R_MU:R_MU + 512], P3[0:1, 0:512], 1.0 / 1024.0)
            kb.v("tensor_scalar_mul", rows1[0:1, R_SCR:R_SCR + 512], P3[0:1, 512:1024], 1.0 / 1024.0, sem=s_cv)
            P3_FREE = s_cv.v
            kb.v("tensor_mul", rows1[0:1, R_MU2:R_MU2 + 512], rows1[0:1, R_MU:R_MU + 512], rows1[0:1, R_MU:R_MU + 512])
            kb.v("tensor_sub", rows1[0:1, R_SCR:R_SCR + 512], rows1[0:1, R_SCR:R_SCR + 512],
                 rows1[0:1, R_MU2:R_MU2 + 512], sem=s_cv)
            VARD = s_cv.v
            kb.wait("scalar", s_cv, VARD)
            kb.s("activation", out=rows1[0:1, R_MU2:R_MU2 + 512], in_=rows1[0:1, R_SCR:R_SCR + 512],
                 func=AF.Ln, bias=sb_epsb[0:1, 0:1])
            kb.s("activation", out=rows1[0:1, R_RHO:R_RHO + 512], in_=rows1[0:1, R_MU2:R_MU2 + 512],
                 func=AF.Exp, scale=-0.5, sem=s_sc)
            RHOD = s_sc.v
            kb.wait("vector", s_sc, RHOD)
            kb.v("tensor_sub", rows3[0:1, R3_M:R3_M + 512], P1[0:1, 0:512], rows1[0:1, R_MU:R_MU + 512])
            kb.v("tensor_sub", rows3[0:1, R3_SCR:R3_SCR + 512], rows1[0:1, R_MU:R_MU + 512], P1[0:1, 512:1024])
            kb.v("tensor_tensor", out=rows3[0:1, R3_M:R3_M + 512], in0=rows3[0:1, R3_M:R3_M + 512],
                 in1=rows3[0:1, R3_SCR:R3_SCR + 512], op=ALU.max, sem=s_cv)
            P1_FREE = s_cv.v
            kb.v("reciprocal", out=rows3[0:1, R3_C:R3_C + 512], in_=rows3[0:1, R3_M:R3_M + 512])
            kb.v("tensor_scalar_mul", rows3[0:1, R3_C:R3_C + 512], rows3[0:1, R3_C:R3_C + 512], 127.0)
            kb.v("reciprocal", out=rows3[0:1, R3_SCR:R3_SCR + 512], in_=rows3[0:1, R3_C:R3_C + 512])
            kb.v("tensor_mul", rows3[0:1, R3_SOUT:R3_SOUT + 512], rows3[0:1, R3_SCR:R3_SCR + 512],
                 rows1[0:1, R_RHO:R_RHO + 512])
            kb.v("tensor_scalar", rows3[0:1, R3_SOUT:R3_SOUT + 512], rows3[0:1, R3_SOUT:R3_SOUT + 512],
                 sb_wscale[0:1, 0:1], None, op0=ALU.mult)
            kb.v("tensor_mul", rows3[0:1, R3_SCR:R3_SCR + 512], rows1[0:1, R_MU:R_MU + 512],
                 rows3[0:1, R3_C:R3_C + 512], sem=s_cv)
            ROWS_D = s_cv.v
            d_r3 = dbg("ffn_rows3_%d_%d" % (emit_ffn.tag, w_), (1, 2048))
            if d_r3 is not None:
                kb.wait("sync", s_cv, ROWS_D)
                kb.dma("sync", d_r3, rows3[0:1, :], s_st)
            d_r1 = dbg("ffn_rows1_%d_%d" % (emit_ffn.tag, w_), (1, 2048))
            if d_r1 is not None:
                kb.wait("sync", s_cv, ROWS_D)
                kb.dma("sync", d_r1, rows1[0:1, :], s_st)
            kb.wait("tensor", s_cv, ROWS_D)
            kb.t("matmul", P0[:, 0:512], sb_onesf[0:1, 0:128],
                 rows3[0:1, R3_C:R3_C + 512], start=True, stop=True)
            kb.t("matmul", P0[:, 512:1024], sb_onesf[0:1, 0:128], rows3[0:1, R3_SCR:R3_SCR + 512],
                 start=True, stop=True)
            kb.t("matmul", P3[:, 0:512], sb_onesf[0:1, 0:128], rows3[0:1, R3_SOUT:R3_SOUT + 512],
                 start=True, stop=True, sem=s_pe)
            BCAST = s_pe.v
            kb.wait("scalar", s_pe, BCAST)
            kb.s("copy", out=soutb[:], in_=P3[:, 0:512], sem=s_sc)
            SOUTB = s_sc.v
            kb.wait("vector", s_pe, BCAST)
            kb.v("tensor_tensor", out=xtmp[:].rearrange("p (c t) -> p c t", c=8),
                 in0=x1w[:].rearrange("p (c t) -> p c t", c=8),
                 in1=P0[:, 0:512].rearrange("p (o t) -> p o t", o=1).broadcast_to([128, 8, 512]), op=ALU.mult)
            kb.v("tensor_tensor", out=xtmp[:].rearrange("p (c t) -> p c t", c=8),
                 in0=xtmp[:].rearrange("p (c t) -> p c t", c=8),
                 in1=P0[:, 512:1024].rearrange("p (o t) -> p o t", o=1).broadcast_to([128, 8, 512]), op=ALU.subtract)
            kb.v("tensor_scalar", xint[:], xtmp[:], MAGIC, MAGIC, op0=ALU.add, op1=ALU.subtract, sem=s_cv)
            XINT = s_cv.v
            d_xi = dbg("ffn_xint_%d_%d" % (emit_ffn.tag, w_), (128, 4096), BF16)
            if d_xi is not None:
                kb.wait("sync", s_cv, XINT)
                kb.dma("sync", d_xi, xint[:], s_st)
                kb.wait("tensor", s_st, s_st.v)
            kb.wait("tensor", s_cv, XINT)
            FB = [P1.ap()[:, 0:512], P1.ap()[:, 512:1024], P2.ap()[:, 0:512], P2.ap()[:, 512:1024]]
            for grp in range(2):
                if grp == 0:
                    kb.wait("tensor", s_cv, P1_FREE)
                    kb.wait("tensor", s_sc, SOUTB)
                for oci in range(4):
                    oc = grp * 4 + oci
                    for kc in range(8):
                        kb.t("matmul", FB[oci], sb_ffnw[:, kc * 1024 + oc * 128: kc * 1024 + oc * 128 + 128],
                             xint[:, kc * 512:(kc + 1) * 512],
                             start=(kc == 0), stop=(kc == 7),
                             sem=(s_pe if (kc == 7 and oci == 3) else None))
                FFND = s_pe.v
                kb.wait("vector", s_pe, FFND)
                for oci in range(4):
                    oc = grp * 4 + oci
                    kb.v("tensor_tensor", out=xtmp[:, oc * 512:(oc + 1) * 512], in0=FB[oci],
                         in1=soutb[:], op=ALU.mult)
                    kb.v("scalar_tensor_tensor", x2w[:, oc * 512:(oc + 1) * 512],
                         xtmp[:, oc * 512:(oc + 1) * 512], sb_vecs[:, V_BIAS + oc:V_BIAS + oc + 1],
                         x1w[:, oc * 512:(oc + 1) * 512],
                         op0=ALU.add, op1=ALU.add, sem=(s_cv if oci == 3 else None))
                EVC = s_cv.v
                kb.wait("tensor", s_cv, EVC)
            store_out(w_)
            kb.wait("vector", emit_ffn.st_sem, emit_ffn.st_sem.v)

    ph2_gate = dict(s_x1=s_x1.v, s_sc=s_sc.v, s_cv=s_cv.v, s_pe=s_pe.v, s_up=FFNW_RDY)
    for engx in ("tensor", "vector", "scalar", "sync", "gpsimd"):
        kb.wait(engx, s_x1, ph2_gate["s_x1"])
        kb.wait(engx, s_up, FFNW_RDY)
        kb.wait(engx, s_sc, ph2_gate["s_sc"])
        kb.wait(engx, s_cv, ph2_gate["s_cv"])
        kb.wait(engx, s_pe, ph2_gate["s_pe"])

    def ld_x1(w_):
        if s_stg.v:
            kb.wait("sync", s_stg, s_stg.v)
        kb.dma("sync", x1w[:].rearrange("p (c t) -> p c t", c=8),
               x1d.ap().rearrange("(c p) t -> p c t", p=128)[:, :, w_ * 512:(w_ + 1) * 512], s_x)
        return s_x.v

    def st_x2(w_):
        kb.wait("sync", s_cv, s_cv.v)
        kb.dma("sync", e1_in.ap().rearrange("(c p) t -> p c t", p=128)[:, :, w_ * 512:(w_ + 1) * 512],
               x2w[:].rearrange("p (c t) -> p c t", c=8), s_stg)

    emit_ffn.st_sem = s_stg
    emit_ffn(4, ld_x1, st_x2)
    emit_ffn.tag = 1
    X2_STAGED = s_stg.v

    d_x2 = dbg("e1_in", (1024, 2048), F32)
    if d_x2 is not None:
        kb.wait("sync", s_stg, X2_STAGED)
        kb.dma("sync", d_x2, e1_in.ap(), s_st)

    ef2.close()

    kb.wait("gpsimd", s_stg, X2_STAGED)
    s_cc1.bump(1)
    kb.emit("gpsimd", lambda e: e.collective_compute(
        "AllGather", ALU.bypass, replica_groups=GROUPS,
        ins=[e1_in.ap()], outs=[e1_out.ap()]).then_inc(s_cc1.h))
    E1_DONE = s_cc1.v

    if ph < 3:
        kb.wait("sync", s_stg, s_stg.v)
        kb.wait("sync", s_cc, s_cc.v)
        kb.wait("sync", s_cc1, s_cc1.v)
        kb.wait("sync", s_cc2, s_cc2.v)
        kb.wait("vector", s_stg, s_stg.v)
        kb.v("memset", x1w[:, 0:2048] if False else xb2.ap().bitcast(F32)[:, 0:1024], 0.0, sem=s_cv)
        kb.wait("sync", s_cv, s_cv.v)
        for wo_ in range(4):
            kb.dma("sync", out_mine.rearrange("(c p) t -> p c t", p=128)[:, :, wo_ * 512:(wo_ + 1) * 512],
                   x2w[:].rearrange("p (c t) -> p c t", c=8), s_out)
        kb.wait("sync", s_out, s_out.v)
        ef2_hold = True
    e3 = ExitStack()
    E3 = e3.enter_context
    w_inp_b = E3(nc.sbuf_tensor("w_inp_b", [128, 8 * 1312], BF16))
    w_hp_b = E3(nc.sbuf_tensor("w_hp_b", [128, 9 * 40], BF16))
    w_outp_b = E3(nc.sbuf_tensor("w_outp_b", [128, 4 * 1024], BF16))
    big1 = E3(nc.sbuf_tensor("big1", [128, 4096], F32))
    wcv3 = big1
    uw = E3(nc.sbuf_tensor("uw", [128, 4096], BF16))
    cumw = E3(nc.sbuf_tensor("cumw", [128, 4096], BF16))
    ucar = E3(nc.sbuf_tensor("ucar", [128, 8], F32))
    xb3 = E3(nc.sbuf_tensor("xb3", [128, 4096], BF16))
    xq3 = E3(nc.sbuf_tensor("xq3", [128, 4096], BF16))
    zw = E3(nc.sbuf_tensor("zw", [128, 4 * 512], BF16))
    xpre = E3(nc.sbuf_tensor("xpre", [128, 6 * 515], BF16))
    xpost = E3(nc.sbuf_tensor("xpost", [128, 6 * 512], BF16))
    dtTw = E3(nc.sbuf_tensor("dtTw", [32, 512], BF16))
    invw = E3(nc.sbuf_tensor("invw", [40, 512], F32))
    hbs = E3(nc.sbuf_tensor("hbs", [40, 512], F32))
    hbT = E3(nc.sbuf_tensor("hbT", [128, 4 * 44], F32))
    daT = E3(nc.sbuf_tensor("daT", [128, 4 * 32], F32))
    tk1 = E3(nc.sbuf_tensor("tk1", [128, 28], F32))
    tk8 = E3(nc.sbuf_tensor("tk8", [128, 24], F32))
    dt16 = E3(nc.sbuf_tensor("dt16", [128, 16], F32))
    dtsp = E3(nc.sbuf_tensor("dtsp", [128, 16], F32))
    dtspT = E3(nc.sbuf_tensor("dtspT", [16, 128], F32))
    mydtT = E3(nc.sbuf_tensor("mydtT", [8, 128], F32))
    mydt = E3(nc.sbuf_tensor("mydt", [128, 4 * 8], F32))
    acum = E3(nc.sbuf_tensor("acum", [8, 4096], F32))
    adt_rows = E3(nc.sbuf_tensor("adt_rows", [8, 4096], F32))
    dt_tok = E3(nc.sbuf_tensor("dt_tok", [128, 512], F32))
    acar = E3(nc.sbuf_tensor("acar", [8, 1], F32))
    aendr = E3(nc.sbuf_tensor("aendr", [1, 40], F32))

    e2t = E3(nc.sbuf_tensor("e2t", [128, 8], F32))
    e2i = E3(nc.sbuf_tensor("e2i", [128, 8], F32))
    lrow = E3(nc.sbuf_tensor("lrow", [1, 16], F32))
    xTt = E3(nc.sbuf_tensor("xTt", [128, 512], BF16))
    dtx = E3(nc.sbuf_tensor("dtx", [128, 512], BF16))
    wdtx = E3(nc.sbuf_tensor("wdtx", [128, 512], BF16))
    BTt = E3(nc.sbuf_tensor("BTt", [128, 128], BF16))
    PMt = E3(nc.sbuf_tensor("PMt", [128, 8 * 128], BF16))
    MWs = E3(nc.sbuf_tensor("MWs", [128, 512], F32))
    Gms = E3(nc.sbuf_tensor("Gms", [128, 128], BF16))
    state = E3(nc.sbuf_tensor("state", [128, 512], F32))
    state_bf = E3(nc.sbuf_tensor("state_bf", [128, 512], BF16))
    ytk = E3(nc.sbuf_tensor("ytk", [128, 512], BF16))
    ytmp = E3(nc.sbuf_tensor("ytmp", [128, 512], F32))
    yfm = E3(nc.sbuf_tensor("yfm", [128, 4 * 512], BF16))
    yzt = xb3.ap().bitcast(BF16)[:, 0:2048] if False else None

    selm = E3(nc.sbuf_tensor("selm", [16, 8], F32))
    sb_iota = E3(nc.sbuf_tensor("sb_iota", [128, 336], BF16))
    identf = E3(nc.sbuf_tensor("identf", [128, 128], F32))
    id8 = E3(nc.sbuf_tensor("id8", [8, 8], F32))
    rows4 = E3(nc.sbuf_tensor("rows4", [1, 2048], F32))
    yzt = xb3
    yzw = xq3
    cwf = cumw.ap().bitcast(F32)
    LAm = cwf[0:2, 0:1024]
    LBm = cwf[0:2, 1024:2048]

    g3 = dict(pe=s_pe.v, cv=s_cv.v, sc=s_sc.v, stg=s_stg.v, x=s_x.v)
    for engx in ("tensor", "vector", "scalar", "sync", "gpsimd"):
        kb.wait(engx, s_pe, g3["pe"])
        kb.wait(engx, s_cv, g3["cv"])
        kb.wait(engx, s_sc, g3["sc"])
        kb.wait(engx, s_stg, g3["stg"])

    kb.dma("sync", selm[:], c_sel, s_ld)
    kb.dma("sync", sb_iota[:], c_iota12, s_ld)
    kb.dma("sync", identf[:], c_identf, s_ld)
    kb.dma("sync", id8[:], c_id8, s_ld)
    LD3 = s_ld.v

    def load_convert3(wsrc, wdstb, ncols, kchunks, rows=128):
        for kc in range(kchunks):
            par = load_convert_w.par = getattr(load_convert_w, "par", 0)
            buf = wcv3[0:rows, (par % 2) * 1312:(par % 2) * 1312 + ncols]
            prev = [h for h in conv_hist if h[2] == par % 2]
            if prev:
                kb.wait("sync", prev[-1][0], prev[-1][1])
            kb.dma("sync", buf, wsrc.rearrange("(c p) n -> p c n", p=rows)[:, kc, :], s_wld[par % 2])
            kb.wait("vector", s_wld[par % 2])
            kb._c("vector", "tensor_copy", (), dict(out=wdstb[0:rows, kc * ncols:(kc + 1) * ncols], in_=buf), s_wcv_v)
            conv_hist.append((s_wcv_v, s_wcv_v.v, par % 2))
            load_convert_w.par = par + 1

    load_convert3(w_inp, w_inp_b, 1312, 8)
    load_convert3(w_outp, w_outp_b, 1024, 4)
    load_convert3(w_hp[0:1024, :], w_hp_b, 40, 8)
    par = load_convert_w.par
    bufh = wcv3[0:32, (par % 2) * 1312:(par % 2) * 1312 + 40]
    prev = [h for h in conv_hist if h[2] == par % 2]
    if prev:
        kb.wait("sync", prev[-1][0], prev[-1][1])
    kb.dma("sync", bufh, w_hp[1024:1056, :], s_wld[par % 2])
    kb.wait("vector", s_wld[par % 2])
    kb._c("vector", "tensor_copy", (), dict(out=w_hp_b[0:32, 8 * 40:9 * 40], in_=bufh), s_wcv_v)
    conv_hist.append((s_wcv_v, s_wcv_v.v, par % 2))
    load_convert_w.par = par + 1
    W3_CV = s_wcv_v.v

    kb.wait("tensor", s_wcv_v, W3_CV)
    for oc in range(3):
        wdt = min(512, 1312 - oc * 512)
        for kc in range(8):
            kb.t("matmul", P3[0:1, 0:wdt], sb_ones[:, 0:1],
                 w_inp_b[:, kc * 1312 + oc * 512: kc * 1312 + oc * 512 + wdt],
                 start=(kc == 0), stop=(kc == 7), sem=(s_pe if kc == 7 else None))
        CSD = s_pe.v
        kb.wait("vector", s_pe, CSD)
        kb.v("tensor_scalar", sb_csum[0:1, CS_INP + oc * 512: CS_INP + oc * 512 + wdt],
             P3[0:1, 0:wdt], -1.0, None, op0=ALU.mult, sem=s_cv)
        kb.wait("tensor", s_cv, s_cv.v)
    for kc in range(8):
        kb.t("matmul", P3[0:1, 0:40], sb_ones[:, 0:1], w_hp_b[:, kc * 40:(kc + 1) * 40],
             start=(kc == 0), stop=(kc == 7), sem=(s_pe if kc == 7 else None))
    CSD = s_pe.v
    kb.wait("vector", s_pe, CSD)
    kb.v("tensor_scalar", sb_csum[0:1, CS_HP:CS_HP + 40], P3[0:1, 0:40], -1.0, None, op0=ALU.mult, sem=s_cv)
    kb.wait("tensor", s_cv, s_cv.v)

    kb.v("memset", ucar[:], 0.0)
    kb.v("memset", acar[:], 0.0)
    kb.v("memset", aendr[:], 0.0)
    kb.v("memset", state[:], 0.0)
    kb.v("memset", state_bf[:], 0.0)
    kb.v("memset", xpre[:], 0.0)
    kb.v("memset", rows1[0:1, 0:2048], 0.0, sem=s_cv)
    INIT3 = s_cv.v
    R_CUMS_CAR = 1920

    kb.wait("sync", s_cc1, E1_DONE)

    e1v = e1_out.ap().rearrange("(k c p) t -> k c p t", k=2, p=128)

    s_hd = sem("s_hd")
    s_hp = sem("s_hp")
    s_hs = sem("s_hs")

    def hades_front(w_):
        kb.soft_barrier((s_pe, s_cv, s_sc, s_stg, s_x))
        kb.wait("vector", s_pe, prevg.get("ssd_pe", 0))
        kb.wait("scalar", s_pe, prevg.get("ssd_pe", 0))
        kb.wait("scalar", s_cv, prevg.get("yz", 0))
        kb.wait("sync", s_stg, prevg.get("e2st", 0))
        for k_ in range(2):
            kb.dma("sync", big1[:].rearrange("p (c r t) -> p c r t", c=8, r=2)[:, :, k_, :],
                   e1v[k_, :, :, w_ * 256:(w_ + 1) * 256].rearrange("c p t -> p c t"), s_x)
        XW = s_x.v
        kb.dma("sync", invw[:], c_invden[:, w_ * 512:(w_ + 1) * 512], s_x)
        INVW = s_x.v
        kb.wait("vector", s_x, XW)
        kb.v("tensor_copy", out=xb3[:], in_=big1[:], sem=s_cv)
        XB = s_cv.v
        kb.wait("scalar", s_x, XW)
        kb.s("activation", out=xq3[:], in_=big1[:], func=AF.Square, sem=s_sc)
        XQ = s_sc.v
        kb.wait("tensor", s_cv, XB)
        kb.wait("tensor", s_sc, XQ)
        for kc in range(8):
            kb.t("matmul", P3[0:1, 0:512], sb_ones[:, 0:1], xb3[:, kc * 512:(kc + 1) * 512],
                 start=(kc == 0), stop=(kc == 7))
        for kc in range(8):
            kb.t("matmul", P3[0:1, 512:1024], sb_ones[:, 0:1], xq3[:, kc * 512:(kc + 1) * 512],
                 start=(kc == 0), stop=(kc == 7), sem=(s_pe if kc == 7 else None))
        STATS = s_pe.v
        R_SCR, R_MU, R_RHO, R_MU2 = 0, 512, 1024, 1536
        kb.wait("vector", s_pe, STATS)
        kb.v("tensor_scalar_mul", rows1[0:1, R_MU:R_MU + 512], P3[0:1, 0:512], 1.0 / 1024.0)
        kb.v("tensor_scalar_mul", rows1[0:1, R_SCR:R_SCR + 512], P3[0:1, 512:1024], 1.0 / 1024.0, sem=s_cv)
        P3F = s_cv.v
        kb.v("tensor_mul", rows1[0:1, R_MU2:R_MU2 + 512], rows1[0:1, R_MU:R_MU + 512], rows1[0:1, R_MU:R_MU + 512])
        kb.v("tensor_sub", rows1[0:1, R_SCR:R_SCR + 512], rows1[0:1, R_SCR:R_SCR + 512],
             rows1[0:1, R_MU2:R_MU2 + 512], sem=s_cv)
        VARD = s_cv.v
        kb.wait("scalar", s_cv, VARD)
        kb.s("activation", out=rows1[0:1, R_MU2:R_MU2 + 512], in_=rows1[0:1, R_SCR:R_SCR + 512],
             func=AF.Ln, bias=sb_epsb[0:1, 0:1])
        kb.s("activation", out=rows1[0:1, R_RHO:R_RHO + 512], in_=rows1[0:1, R_MU2:R_MU2 + 512],
             func=AF.Exp, scale=-0.5, sem=s_sc)
        RHOD = s_sc.v
        kb.wait("vector", s_sc, RHOD)
        kb.v("tensor_mul", rows4[0:1, 0:512], rows1[0:1, R_MU:R_MU + 512], rows1[0:1, R_RHO:R_RHO + 512], sem=s_cv)
        kb.wait("vector", s_cv, s_cv.v)
        kb.v("tensor_copy", out=rowbf[0:1, 0:512], in_=rows4[0:1, 0:512])
        kb.v("tensor_copy", out=rowbf[0:1, 512:1024], in_=rows1[0:1, R_RHO:R_RHO + 512], sem=s_cv)
        kb.v("tensor_tensor_scan", out=rows4[0:1, 512:1024], data0=sb_onesf[0:1, 0:512],
             data1=rows4[0:1, 0:512], initial=rows1[0:1, R_CUMS_CAR:R_CUMS_CAR + 1],
             op0=ALU.mult, op1=ALU.add, sem=s_cv)
        kb.wait("vector", s_cv, s_cv.v)
        kb.v("tensor_copy", out=rows1[0:1, R_CUMS_CAR:R_CUMS_CAR + 1], in_=rows4[0:1, 1023:1024])
        kb.v("tensor_copy", out=rowbf[0:1, 1024:1536], in_=rows4[0:1, 512:1024], sem=s_cv)
        ROWSD = s_cv.v
        kb.wait("tensor", s_cv, ROWSD)
        kb.t("matmul", P2[:, 0:512], sb_ones[0:1, :], rowbf[0:1, 512:1024], start=True, stop=True, sem=s_pe)
        RHOB = s_pe.v
        kb.wait("vector", s_pe, RHOB)
        kb.v("tensor_tensor", out=uw[:].rearrange("p (c t) -> p c t", c=8),
             in0=big1[:].rearrange("p (c t) -> p c t", c=8),
             in1=P2[:, 0:512].rearrange("p (o t) -> p o t", o=1).broadcast_to([128, 8, 512]),
             op=ALU.mult, sem=s_cv)
        UD = s_cv.v
        for kc in range(8):
            kb.v("tensor_tensor_scan", out=cumw[:, kc * 512:(kc + 1) * 512],
                 data0=sb_onesf[:, 0:512], data1=uw[:, kc * 512:(kc + 1) * 512],
                 initial=ucar[:, kc:kc + 1], op0=ALU.mult, op1=ALU.add,
                 sem=(s_cv if kc == 7 else None))
        kb.wait("vector", s_cv, s_cv.v)
        kb.v("tensor_copy", out=ucar[:],
             in_=cumw[:].rearrange("p (c t) -> p c t", c=8)[:, :, 511:512].rearrange("p c o -> p (c o)"),
             sem=s_cv)
        CUMD = s_cv.v
        d_cumw = dbg("h_cumw_%d" % w_, (128, 4096), BF16)
        if d_cumw is not None:
            kb.wait("sync", s_cv, CUMD)
            kb.dma("sync", d_cumw, cumw[:], s_st)
            kb.wait("tensor", s_st, s_st.v)
        if FUP < 3:
            return dict(SILUD=s_sc.v, FRONT_SC=s_sc.v, ACUM=s_cv.v, AEND=s_x.v, HBC=s_pe.v)
        kb.wait("tensor", s_cv, UD)
        IB = [P0.ap()[:, 0:512], P0.ap()[:, 512:1024], P1.ap()[:, 0:512], P1.ap()[:, 512:1024]]
        kb.v("tensor_copy", out=xpre[:].rearrange("p (c t) -> p c t", c=6)[:, :, 0:3],
             in_=xpre[:].rearrange("p (c t) -> p c t", c=6)[:, :, 512:515], sem=s_cv)
        HALO = s_cv.v
        for grp in range(3):
            tiles = [grp * 4 + j for j in range(4) if grp * 4 + j < 11]
            for j, tile in enumerate(tiles):
                r0 = tile * 128
                wdt = 128 if tile < 10 else 32
                for kc in range(8):
                    kb.t("matmul", IB[j][0:128, 0:512] if wdt == 128 else IB[j][0:32, 0:512],
                         w_inp_b[:, kc * 1312 + r0: kc * 1312 + r0 + wdt],
                         uw[:, kc * 512:(kc + 1) * 512], start=(kc == 0), stop=False)
                kb.t("matmul", IB[j][0:wdt, 0:512], sb_csum[0:1, CS_INP + r0: CS_INP + r0 + wdt],
                     rowbf[0:1, 0:512], start=False, stop=True,
                     sem=(s_pe if j == len(tiles) - 1 else None))
            IPD = s_pe.v
            kb.wait("scalar", s_pe, IPD)
            if grp == 0:
                kb.wait("scalar", s_cv, HALO)
            for j, tile in enumerate(tiles):
                if tile < 4:
                    dst = zw[:, tile * 512:(tile + 1) * 512]
                elif tile < 10:
                    cch = tile - 4
                    dst = xpre[:, cch * 515 + 3: cch * 515 + 515]
                else:
                    dst = dtTw[0:32, 0:512]
                kb.s("copy", out=dst, in_=IB[j][0:(128 if tile < 10 else 32), 0:512],
                     sem=(s_sc if j == len(tiles) - 1 else None))
            IPE = s_sc.v
            kb.wait("tensor", s_sc, IPE)
        if FUP < 4:
            return dict(SILUD=s_sc.v, FRONT_SC=s_sc.v, ACUM=s_cv.v, AEND=s_x.v, HBC=s_pe.v)
        kb.soft_barrier((s_pe, s_cv, s_sc))
        kb.wait("vector", s_sc, s_sc.v)
        for cch in range(6):
            base = cch * 515
            kb.v("tensor_scalar", xpost[:, cch * 512:(cch + 1) * 512], xpre[:, base + 0: base + 512],
                 sb_cwv[:, cch * 4 + 0: cch * 4 + 1], sb_vecs[:, V_CONVB + cch:V_CONVB + cch + 1],
                 op0=ALU.mult, op1=ALU.add)
            for kk in range(1, 4):
                kb.v("scalar_tensor_tensor", xpost[:, cch * 512:(cch + 1) * 512],
                     xpre[:, base + kk: base + kk + 512],
                     sb_cwv[:, cch * 4 + kk: cch * 4 + kk + 1],
                     xpost[:, cch * 512:(cch + 1) * 512],
                     op0=ALU.mult, op1=ALU.add, sem=(s_cv if (cch == 5 and kk == 3) else None))
        CONVD = s_cv.v
        kb.wait("scalar", s_cv, CONVD)
        for cch in range(6):
            kb.s("activation", out=xpost[:, cch * 512:(cch + 1) * 512],
                 in_=xpost[:, cch * 512:(cch + 1) * 512], func=AF.Silu,
                 sem=(s_sc if cch == 5 else None))
        SILUD = s_sc.v
        if FUP < 5:
            return dict(SILUD=SILUD, FRONT_SC=s_sc.v, ACUM=s_cv.v, AEND=s_x.v, HBC=s_pe.v)
        kb.wait("tensor", s_cv, CUMD)
        for kc in range(8):
            kb.t("matmul", P2[0:40, 512:1024][0:40, 0:512] if False else P2[0:40, 512:1024],
                 w_hp_b[:, kc * 40:(kc + 1) * 40], uw[:, kc * 512:(kc + 1) * 512],
                 start=(kc == 0), stop=False)
        kb.t("matmul", P2[0:40, 512:1024], w_hp_b[0:32, 8 * 40:9 * 40], dtTw[0:32, 0:512],
             start=False, stop=False)
        kb.t("matmul", P2[0:40, 512:1024], sb_csum[0:1, CS_HP:CS_HP + 40], rowbf[0:1, 0:512],
             start=False, stop=True, sem=s_pe)
        HBU = s_pe.v
        for kc in range(8):
            kb.t("matmul", P3[0:40, 0:512], w_hp_b[:, kc * 40:(kc + 1) * 40],
                 cumw[:, kc * 512:(kc + 1) * 512], start=(kc == 0), stop=False)
        kb.t("matmul", P3[0:40, 0:512], sb_csum[0:1, CS_HP:CS_HP + 40], rowbf[0:1, 1024:1536],
             start=False, stop=True, sem=s_pe)
        HBC = s_pe.v
        kb.wait("vector", s_pe, HBC)
        kb.wait("vector", s_x, INVW)
        kb.v("tensor_tensor", out=hbs[:], in0=P3[0:40, 0:512], in1=invw[:], op=ALU.mult)
        kb.v("tensor_tensor", out=hbs[:], in0=P2[0:40, 512:1024], in1=hbs[:], op=ALU.subtract, sem=s_cv)
        HBSD = s_cv.v
        d_rb = dbg("h_rowbf_%d" % w_, (1, 2048), BF16)
        if d_rb is not None:
            kb.wait("sync", s_cv, ROWSD)
            kb.dma("sync", d_rb[0:1, 0:1536], rowbf[0:1, 0:1536], s_st)
        d_dtT = dbg("h_dtTw_%d" % w_, (32, 512), BF16)
        if d_dtT is not None:
            kb.wait("sync", s_sc, s_sc.v)
            kb.dma("sync", d_dtT, dtTw[:], s_st)
        d_hb = dbg("h_hbs_%d" % w_, (40, 512))
        if d_hb is not None:
            kb.wait("sync", s_cv, HBSD)
            kb.dma("sync", d_hb, hbs[:], s_st)
        d_xp = dbg("h_xpost_%d" % w_, (128, 6 * 512), BF16)
        if d_xp is not None:
            kb.wait("sync", s_sc, SILUD)
            kb.dma("sync", d_xp, xpost[:], s_st)
        d_zw = dbg("h_zw_%d" % w_, (128, 2048), BF16)
        if d_zw is not None:
            kb.wait("sync", s_sc, SILUD)
            kb.dma("sync", d_zw, zw[:], s_st)
        d_uw = dbg("h_uw_%d" % w_, (128, 4096), BF16)
        if d_uw is not None:
            kb.wait("sync", s_cv, UD)
            kb.dma("sync", d_uw, uw[:], s_st)
            kb.wait("gpsimd", s_st, s_st.v)
        if FUP < 6:
            return dict(SILUD=SILUD, FRONT_SC=s_sc.v, ACUM=s_cv.v, AEND=s_x.v, HBC=HBC)
        kb.soft_barrier((s_pe, s_cv, s_sc))
        kb.wait("tensor", s_cv, HBSD)
        for tt in range(4):
            kb.t("matmul", P3[:, 512 + tt * 44: 512 + tt * 44 + 40],
                 hbs[0:40, tt * 128:(tt + 1) * 128], identf[0:40, 0:40],
                 start=True, stop=True)
            kb.t("matmul", P2[:, 512 + tt * 32: 512 + tt * 32 + 32],
                 dtTw[0:32, tt * 128:(tt + 1) * 128], sb_ident[0:32, 0:32],
                 start=True, stop=True, sem=(s_pe if tt == 3 else None))
        TRD = s_pe.v
        kb.wait("scalar", s_pe, TRD)
        for tt in range(4):
            kb.s("copy", out=hbT[:, tt * 44: tt * 44 + 40], in_=P3[:, 512 + tt * 44: 512 + tt * 44 + 40])
            kb.s("copy", out=daT[:, tt * 32:(tt + 1) * 32], in_=P2[:, 512 + tt * 32: 512 + tt * 32 + 32],
                 sem=(s_sc if tt == 3 else None))
        TRE = s_sc.v
        kb.wait("vector", s_sc, TRE)
        TDTprev = 0
        for tt in range(4):
            cglob = 4 * w_ + tt
            h28 = hbT[:, tt * 44: tt * 44 + 28]
            kb.v("tensor_tensor", out=PMt[:, 0:784].rearrange("p (i j) -> p i j", i=28),
                 in0=h28.rearrange("p (o j) -> p o j", o=1).broadcast_to([128, 28, 28]),
                 in1=h28.rearrange("p (i o) -> p i o", o=1).broadcast_to([128, 28, 28]),
                 op=ALU.is_gt)
            kb.v("reduce_sum", out=tk1[:, 0:28], in_=PMt[:, 0:784].rearrange("p (i j) -> p i j", i=28),
                 axis=AX, sem=s_cv)
            kb.wait("vector", s_cv, s_cv.v)
            kb.v("tensor_copy", out=PMt[:, 784:812], in_=tk1[:, 0:28], sem=s_cv)
            kb.wait("vector", s_cv, s_cv.v)
            kb.v("tensor_tensor", out=PMt[:, 0:336].rearrange("p (j i) -> p j i", j=12),
                 in0=PMt[:, 784:812].rearrange("p (o i) -> p o i", o=1).broadcast_to([128, 12, 28]),
                 in1=sb_iota[:].rearrange("p (j i) -> p j i", j=12),
                 op=ALU.is_equal)
            kb.v("tensor_copy", out=PMt[:, 812:840], in_=daT[:, tt * 32: tt * 32 + 28], sem=s_cv)
            kb.wait("vector", s_cv, s_cv.v)
            kb.v("tensor_tensor", out=PMt[:, 0:336].rearrange("p (j i) -> p j i", j=12),
                 in0=PMt[:, 0:336].rearrange("p (j i) -> p j i", j=12),
                 in1=PMt[:, 812:840].rearrange("p (o i) -> p o i", o=1).broadcast_to([128, 12, 28]),
                 op=ALU.mult, sem=s_cv)
            kb.wait("vector", s_cv, s_cv.v)
            kb.v("reduce_sum", out=dt16[:, 0:12], in_=PMt[:, 0:336].rearrange("p (j i) -> p j i", j=12),
                 axis=AX)
            kb.v("tensor_mul", tk1[:, 0:12], hbT[:, tt * 44 + 28: tt * 44 + 40],
                 sb_vecs[:, V_GAMMA:V_GAMMA + 12], sem=s_cv)
            kb.wait("vector", s_cv, s_cv.v)
            kb.v("tensor_scalar", dt16[:, 0:12], dt16[:, 0:12], sb_vecs[:, V_DTB:V_DTB + 1], None, op0=ALU.add, sem=s_cv)
            kb.wait("vector", s_cv, s_cv.v)
            kb.v("tensor_add", dt16[:, 0:12], dt16[:, 0:12], tk1[:, 0:12])
            kb.v("tensor_scalar", dt16[:, 12:16], daT[:, tt * 32 + 28: tt * 32 + 32],
                 sb_vecs[:, V_DTB:V_DTB + 1], None, op0=ALU.add, sem=s_cv)
            DT16 = s_cv.v
            d_v12 = dbg("h_v12_%d" % w_, (128, 28)) if tt == 0 else None
            if d_v12 is not None:
                kb.wait("sync", s_cv, DT16)
                kb.dma("sync", d_v12, tk1[:, 0:28], s_st)
            d_d16 = dbg("h_dt16_%d" % w_, (128, 16)) if tt == 0 else None
            if d_d16 is not None:
                kb.wait("sync", s_cv, DT16)
                kb.dma("sync", d_d16, dt16[:], s_st)
            d_eqm = dbg("h_eqm_%d" % w_, (128, 336), BF16) if tt == 0 else None
            if d_eqm is not None:
                kb.wait("sync", s_cv, DT16)
                kb.dma("sync", d_eqm, PMt[:, 0:336], s_st)
            if d_v12 is not None or d_d16 is not None:
                kb.wait("vector", s_st, s_st.v)
                kb.wait("scalar", s_st, s_st.v)
            kb.wait("scalar", s_cv, DT16)
            kb.wait("scalar", s_pe, TDTprev)
            kb.s("activation", out=dtsp[:], in_=dt16[:], func=AF.Abs, sem=s_sc)
            kb.wait("scalar", s_sc, s_sc.v)
            kb.s("activation", out=dtsp[:], in_=dtsp[:], func=AF.Exp, scale=-1.0, sem=s_sc)
            kb.wait("scalar", s_sc, s_sc.v)
            kb.s("activation", out=dtsp[:], in_=dtsp[:], func=AF.Ln, bias=sb_onesf[:, 0:1], sem=s_sc)
            SPL = s_sc.v
            kb.wait("vector", s_sc, SPL)
            kb.wait("vector", s_cv, DT16)
            kb.v("tensor_scalar_max", tk1[:, 0:16], dt16[:], 0.0, sem=s_cv)
            kb.wait("vector", s_cv, s_cv.v)
            kb.v("tensor_add", dtsp[:], dtsp[:], tk1[:, 0:16], sem=s_cv)
            DTSP = s_cv.v
            kb.wait("vector", s_cv, DTSP)
            kb.v("tensor_copy", out=dt_tok[:, cglob * 16:(cglob + 1) * 16], in_=dtsp[:], sem=s_cv)
            kb.wait("tensor", s_cv, DTSP)
            kb.t("matmul", P3[0:16, 512 + 176: 512 + 176 + 128], dtsp[:], identf[:, 0:128],
                 start=True, stop=True, sem=s_pe)
            TDT = s_pe.v
            TDTprev = TDT
            kb.wait("scalar", s_pe, TDT)
            kb.s("copy", out=dtspT[:], in_=P3[0:16, 688:816], sem=s_sc)
            TDE = s_sc.v
            kb.wait("tensor", s_sc, TDE)
            kb.t("matmul", P3[0:8, 816:944], selm[:], dtspT[:], start=True, stop=True, sem=s_pe)
            SELD = s_pe.v
            kb.wait("scalar", s_pe, SELD)
            kb.s("copy", out=mydtT[:], in_=P3[0:8, 816:944], sem=s_sc)
            MDT = s_sc.v
            kb.wait("vector", s_sc, MDT)
            kb.v("tensor_scalar", adt_rows[0:8, cglob * 128:(cglob + 1) * 128], mydtT[:],
                 sb_alog[0:8, 1:2], None, op0=ALU.mult, sem=s_cv)
            ADT = s_cv.v
            kb.wait("tensor", s_sc, MDT)
            kb.t("matmul", P3[0:128, 944:952], mydtT[:], id8[:], start=True, stop=True, sem=s_pe)
            MDTT = s_pe.v
            kb.wait("scalar", s_pe, MDTT)
            kb.s("copy", out=mydt[:, tt * 8:(tt + 1) * 8], in_=P3[:, 944:952], sem=s_sc)
        FRONT_SC = s_sc.v
        kb.wait("vector", s_cv, s_cv.v)
        kb.v("tensor_tensor_scan", out=acum[0:8, w_ * 512:(w_ + 1) * 512],
             data0=sb_onesf[0:8, 0:512], data1=adt_rows[0:8, w_ * 512:(w_ + 1) * 512],
             initial=acar[:, 0:1], op0=ALU.mult, op1=ALU.add, sem=s_cv)
        kb.wait("vector", s_cv, s_cv.v)
        kb.v("tensor_copy", out=acar[:, 0:1], in_=acum[0:8, w_ * 512 + 511: w_ * 512 + 512], sem=s_cv)
        ACUM = s_cv.v
        d_ac = dbg("h_acum_%d" % w_, (8, 512))
        if d_ac is not None:
            kb.wait("sync", s_cv, ACUM)
            kb.dma("sync", d_ac, acum[0:8, w_ * 512:(w_ + 1) * 512], s_st)
        d_ad = dbg("h_adt_%d" % w_, (8, 512))
        if d_ad is not None:
            kb.wait("sync", s_cv, ACUM)
            kb.dma("sync", d_ad, adt_rows[0:8, w_ * 512:(w_ + 1) * 512], s_st)
        d_md = dbg("h_mydt_%d" % w_, (128, 32))
        if d_md is not None:
            kb.wait("sync", s_sc, FRONT_SC)
            kb.dma("sync", d_md, mydt[:, 0:32], s_st)
        return dict(SILUD=SILUD, FRONT_SC=FRONT_SC, ACUM=ACUM, AEND=s_x.v, HBC=HBC)
    prevg = dict(yz=0, op=0, e2st=0)

    def ssd_chunk(w_, tt, fr):
        cg = 4 * w_ + tt
        kb.wait("tensor", s_cv, prevg.get("ssd_cv", 0))
        kb.wait("tensor", s_sc, prevg.get("ssd_sc", 0))
        if tt == 0:
            kb.wait("sync", s_pe, fr["HBC"])
            kb.dma("sync", LAm[0:1, :], c_onesrow[0:1, 0:1024], s_x)
            kb.dma("sync", LBm[1:2, :], c_negones[0:1, 0:1024], s_x)
            fr["LAB0"] = s_x.v
        kb.wait("tensor", s_sc, fr["SILUD"])
        kb.t("matmul", P1[:, 512:640], xpost[:, 4 * 512 + tt * 128: 4 * 512 + (tt + 1) * 128],
             sb_ident[:], start=True, stop=True, sem=s_pe)
        BTD = s_pe.v
        kb.wait("scalar", s_pe, BTD)
        kb.s("copy", out=BTt[:], in_=P1[:, 512:640], sem=s_sc)
        BTE = s_sc.v
        for cch in range(4):
            kb.t("matmul", P2[:, 512 + cch * 128: 512 + (cch + 1) * 128],
                 xpost[:, cch * 512 + tt * 128: cch * 512 + (tt + 1) * 128],
                 sb_ident[:], start=True, stop=True, sem=(s_pe if cch == 3 else None))
        XTD = s_pe.v
        kb.wait("tensor", s_cv, fr["ACUM"])
        kb.wait("tensor", s_sc, BTE)
        kb.t("matmul", P1[0:1, 656:664], acum[0:8, cg * 128 + 127: cg * 128 + 128], id8[:],
             start=True, stop=True, sem=s_pe)
        AEP = s_pe.v
        kb.wait("vector", s_pe, AEP)
        kb.v("tensor_scalar", rows4[0:1, 1024:1032], P1[0:1, 656:664], -1.0, None, op0=ALU.mult)
        kb.v("tensor_scalar", rows4[0:1, 1040:1048], aendr[0:1, 0:8], -1.0, None, op0=ALU.mult)
        kb.v("tensor_tensor", out=rows4[0:1, 1032:1040], in0=P1[0:1, 656:664],
             in1=aendr[0:1, 0:8], op=ALU.subtract)
        kb.v("tensor_copy", out=aendr[0:1, 0:8], in_=P1[0:1, 656:664], sem=s_cv)
        AER = s_cv.v
        kb.wait("scalar", s_cv, AER)
        kb.s("activation", out=rows4[0:1, 1032:1040], in_=rows4[0:1, 1032:1040], func=AF.Exp, sem=s_sc)
        LROW = s_sc.v
        kb.wait("tensor", s_cv, AER)
        kb.t("matmul", P1[:, 640:648], acum[0:8, cg * 128:(cg + 1) * 128], id8[:], start=True, stop=False)
        kb.t("matmul", P1[:, 640:648], sb_onesf[0:1, 0:128], rows4[0:1, 1024:1032],
             start=False, stop=True, sem=s_pe)
        kb.t("matmul", P1[:, 664:672], acum[0:8, cg * 128:(cg + 1) * 128], id8[:], start=True, stop=False)
        kb.t("matmul", P1[:, 664:672], sb_onesf[0:1, 0:128], rows4[0:1, 1040:1048],
             start=False, stop=True, sem=s_pe)
        E2P = s_pe.v
        kb.wait("scalar", s_pe, E2P)
        kb.s("activation", out=e2t[:], in_=P1[:, 640:648], func=AF.Exp, scale=-1.0)
        kb.s("activation", out=e2i[:], in_=P1[:, 664:672], func=AF.Exp, sem=s_sc)
        E2D = s_sc.v
        d_e2 = dbg("h_e2_%d_%d" % (w_, tt), (128, 16))
        if d_e2 is not None:
            kb.wait("sync", s_sc, E2D)
            kb.dma("sync", d_e2[:, 0:8], e2t[:], s_st)
            kb.dma("sync", d_e2[:, 8:16], e2i[:], s_st)
        kb.wait("tensor", s_sc, E2D)
        kb.t("matmul", P1[:, 648:656], sb_onesf[0:1, 0:128], rows4[0:1, 1032:1040],
             start=True, stop=True, sem=s_pe)
        LBD = s_pe.v
        kb.wait("vector", s_pe, XTD)
        kb.v("tensor_tensor", out=dtx[:].rearrange("p (h j) -> p h j", h=8),
             in0=P2[:, 512:1024].rearrange("p (h j) -> p h j", h=8),
             in1=mydt[:, tt * 8:(tt + 1) * 8].rearrange("p (h o) -> p h o", o=1).broadcast_to([128, 8, 64]),
             op=ALU.mult)
        kb.wait("vector", s_sc, E2D)
        kb.v("tensor_mul", tk8[:, 16:24], mydt[:, tt * 8:(tt + 1) * 8], e2t[:], sem=s_cv)
        kb.wait("vector", s_cv, s_cv.v)
        kb.v("tensor_tensor", out=wdtx[:].rearrange("p (h j) -> p h j", h=8),
             in0=P2[:, 512:1024].rearrange("p (h j) -> p h j", h=8),
             in1=tk8[:, 16:24].rearrange("p (h o) -> p h o", o=1).broadcast_to([128, 8, 64]),
             op=ALU.mult, sem=s_cv)
        DTXD = s_cv.v
        kb.t("matmul", P2[:, 0:128], xpost[:, 4 * 512 + tt * 128: 4 * 512 + (tt + 1) * 128],
             xpost[:, 5 * 512 + tt * 128: 5 * 512 + (tt + 1) * 128], start=True, stop=True, sem=s_pe)
        GD = s_pe.v
        kb.wait("vector", s_pe, GD)
        kb.v("tensor_tensor", out=Gms[:], in0=P2[:, 0:128], in1=sb_tri[:], op=ALU.mult, sem=s_cv)
        GMD = s_cv.v
        kb.wait("sync", s_cv, fr["ACUM"])
        kb.dma("sync", LAm[1:2, :], acum.ap()[0:8, cg * 128:(cg + 1) * 128], s_x)
        kb.dma("sync", LBm[0:1, :], acum.ap()[0:8, cg * 128:(cg + 1) * 128], s_x)
        LABD = s_x.v
        kb.wait("tensor", s_x, max(LABD, fr.get("LAB0", 0)))
        for grp in range(2):
            for hh in range(4):
                h = grp * 4 + hh
                kb.t("matmul", P3[:, hh * 128:(hh + 1) * 128], LAm[0:2, h * 128:(h + 1) * 128],
                     LBm[0:2, h * 128:(h + 1) * 128], start=True, stop=True,
                     sem=(s_pe if hh == 3 else None))
            MPD = s_pe.v
            kb.wait("vector", s_pe, MPD)
            kb.v("tensor_scalar", MWs[:], P3[:, 0:512], 0.0, None, op0=ALU.min, sem=s_cv)
            MWD = s_cv.v
            kb.wait("scalar", s_cv, MWD)
            kb.s("activation", out=PMt[:, grp * 512:(grp + 1) * 512], in_=MWs[:], func=AF.Exp, sem=s_sc)
            PME = s_sc.v
            kb.wait("tensor", s_sc, PME)
            kb.wait("vector", s_sc, PME)
            kb.v("tensor_tensor", out=PMt[:, grp * 512:(grp + 1) * 512].rearrange("p (h t) -> p h t", h=4),
                 in0=PMt[:, grp * 512:(grp + 1) * 512].rearrange("p (h t) -> p h t", h=4),
                 in1=Gms[:].rearrange("p (o t) -> p o t", o=1).broadcast_to([128, 4, 128]),
                 op=ALU.mult, sem=s_cv)
            kb.wait("vector", s_cv, GMD)
        PMD = s_cv.v
        d_mw = dbg("h_mw_%d_%d" % (w_, tt), (128, 512)) if tt == 0 else None
        if d_mw is not None:
            pass
        d_gm = dbg("h_gm_%d_%d" % (w_, tt), (128, 128), BF16) if tt == 0 else None
        if d_gm is not None:
            kb.wait("sync", s_cv, GMD)
            kb.dma("sync", d_gm, Gms[:], s_st)
        kb.wait("tensor", s_cv, PMD)
        kb.wait("tensor", s_cv, DTXD)
        for h in range(8):
            kb.t("matmul", P0[:, h * 64:(h + 1) * 64], PMt[:, h * 128:(h + 1) * 128],
                 dtx[:, h * 64:(h + 1) * 64], start=(h == 0), stop=(h == 7),
                 skip_group_check=True, sem=(s_pe if h == 7 else None))
        INTRA = s_pe.v
        kb.t("matmul", P0[:, 512:1024], xpost.ap().bitcast(BF16)[:, 5 * 512 + tt * 128: 5 * 512 + (tt + 1) * 128] if False else xpost[:, 5 * 512 + tt * 128: 5 * 512 + (tt + 1) * 128],
             state_bf[:], start=True, stop=True, sem=s_pe)
        INTER = s_pe.v
        kb.wait("tensor", s_sc, BTE)
        kb.t("matmul", P1[:, 0:512], BTt[:], wdtx[:], start=True, stop=True, sem=s_pe)
        TCD = s_pe.v
        kb.wait("vector", s_pe, INTER)
        kb.v("tensor_tensor", out=ytmp[:].rearrange("p (h j) -> p h j", h=8),
             in0=P0[:, 512:1024].rearrange("p (h j) -> p h j", h=8),
             in1=e2i[:].rearrange("p (h o) -> p h o", o=1).broadcast_to([128, 8, 64]), op=ALU.mult)
        kb.wait("vector", s_pe, INTRA)
        kb.v("tensor_tensor", out=ytk[:], in0=P0[:, 0:512], in1=ytmp[:], op=ALU.add, sem=s_cv)
        YTK = s_cv.v
        kb.wait("vector", s_pe, LBD)
        kb.v("tensor_tensor", out=state[:].rearrange("p (h j) -> p h j", h=8),
             in0=state[:].rearrange("p (h j) -> p h j", h=8),
             in1=P1[:, 648:656].rearrange("p (h o) -> p h o", o=1).broadcast_to([128, 8, 64]), op=ALU.mult)
        kb.wait("vector", s_pe, TCD)
        kb.v("tensor_add", state[:], state[:], P1[:, 0:512])
        kb.v("tensor_copy", out=state_bf[:], in_=state[:], sem=s_cv)
        STU = s_cv.v
        d_pm = dbg("h_pm_%d_%d" % (w_, tt), (128, 1024), BF16)
        if d_pm is not None:
            kb.wait("sync", s_cv, PMD)
            kb.dma("sync", d_pm, PMt[:], s_st)
        d_dtx = dbg("h_dtx_%d_%d" % (w_, tt), (128, 512), BF16)
        if d_dtx is not None:
            kb.wait("sync", s_cv, DTXD)
            kb.dma("sync", d_dtx, dtx[:], s_st)
        d_ytk = dbg("h_ytk_%d_%d" % (w_, tt), (128, 512), BF16)
        if d_ytk is not None:
            kb.wait("sync", s_cv, YTK)
            kb.dma("sync", d_ytk, ytk[:], s_st)

        kb.wait("tensor", s_cv, YTK)
        for cch in range(4):
            kb.t("matmul", P2[:, 512 + cch * 128: 512 + (cch + 1) * 128],
                 ytk[:, cch * 128:(cch + 1) * 128], sb_ident[:], start=True, stop=True,
                 sem=(s_pe if cch == 3 else None))
        YTR = s_pe.v
        kb.wait("vector", s_pe, YTR)
        for cch in range(4):
            kb.v("scalar_tensor_tensor", yfm[:, cch * 512 + tt * 128: cch * 512 + (tt + 1) * 128],
                 xpost[:, cch * 512 + tt * 128: cch * 512 + (tt + 1) * 128],
                 sb_vecs[:, V_D + cch:V_D + cch + 1],
                 P2[:, 512 + cch * 128: 512 + (cch + 1) * 128],
                 op0=ALU.mult, op1=ALU.add, sem=(s_cv if cch == 3 else None))
        YFMD = s_cv.v
        prevg["ssd_cv"] = s_cv.v
        prevg["ssd_sc"] = s_sc.v
        prevg["ssd_pe"] = s_pe.v
        return YFMD

    def hades_tail(w_, fr):
        kb.soft_barrier((s_pe, s_cv, s_sc, s_stg))
        d_yf = dbg("h_yfm_%d" % w_, (128, 2048), BF16)
        if d_yf is not None:
            kb.wait("sync", s_cv, s_cv.v)
            kb.dma("sync", d_yf, yfm[:], s_st)
        kb.wait("scalar", s_cv, s_cv.v)
        for cch in range(4):
            kb.s("activation", out=zw[:, cch * 512:(cch + 1) * 512], in_=zw[:, cch * 512:(cch + 1) * 512],
                 func=AF.Silu, sem=(s_sc if cch == 3 else None))
        SZ = s_sc.v
        kb.wait("vector", s_sc, SZ)
        for cch in range(4):
            kb.v("tensor_mul", yzt[:, cch * 512:(cch + 1) * 512], yfm[:, cch * 512:(cch + 1) * 512],
                 zw[:, cch * 512:(cch + 1) * 512])
            kb.v("tensor_scalar", yzw[:, cch * 512:(cch + 1) * 512], yzt[:, cch * 512:(cch + 1) * 512],
                 sb_vecs[:, V_NORMW + cch:V_NORMW + cch + 1], None, op0=ALU.mult,
                 sem=(s_cv if cch == 3 else None))
        YZD = s_cv.v
        prevg["yz"] = YZD
        kb.wait("scalar", s_cv, YZD)
        for cch in range(4):
            kb.s("activation", out=cumw[:, cch * 512:(cch + 1) * 512], in_=yzt[:, cch * 512:(cch + 1) * 512],
                 func=AF.Square, sem=(s_sc if cch == 3 else None))
        SQD = s_sc.v
        kb.wait("tensor", s_sc, SQD)
        for kc in range(4):
            kb.t("matmul", P3[0:1, 0:512], sb_ones[:, 0:1], cumw[:, kc * 512:(kc + 1) * 512],
                 start=(kc == 0), stop=(kc == 3), sem=(s_pe if kc == 3 else None))
        YQS = s_pe.v
        kb.wait("vector", s_pe, YQS)
        kb.v("tensor_copy", out=rows4[0:1, 0:512], in_=P3[0:1, 0:512], sem=s_cv)
        YQR = s_cv.v
        kb.soft_barrier((s_pe, s_cv, s_sc))
        OPB = [P0.ap()[:, 0:512], P0.ap()[:, 512:1024], P1.ap()[:, 0:512], P1.ap()[:, 512:1024]]
        kb.wait("tensor", s_cv, YZD)
        for grp in range(2):
            for oci in range(4):
                oc = grp * 4 + oci
                for kc in range(4):
                    kb.t("matmul", OPB[oci], w_outp_b[:, kc * 1024 + oc * 128: kc * 1024 + oc * 128 + 128],
                         yzw[:, kc * 512:(kc + 1) * 512], start=(kc == 0), stop=(kc == 3),
                         sem=(s_pe if (kc == 3 and oci == 3) else None))
            OPD = s_pe.v
            kb.wait("scalar", s_pe, OPD)
            if grp == 0:
                kb.wait("scalar", s_x, s_x.v)
            for oci in range(4):
                oc = grp * 4 + oci
                kb.s("copy", out=big1[:, oc * 512:(oc + 1) * 512], in_=OPB[oci],
                     sem=(s_sc if oci == 3 else None))
            OPE = s_sc.v
            kb.wait("tensor", s_sc, OPE)
        prevg["op"] = s_sc.v
        kb.wait("sync", s_sc, s_sc.v)
        kb.wait("sync", s_cv, YQR)
        for k_ in range(2):
            kb.dma("sync",
                   e2_in.ap()[k_ * 1025: k_ * 1025 + 1024, w_ * 256:(w_ + 1) * 256]
                   .rearrange("(c p) t -> p c t", p=128),
                   big1[:].rearrange("p (c t) -> p c t", c=8)[:, :, k_ * 256:(k_ + 1) * 256], s_stg)
            kb.dma("sync", e2_in.ap()[k_ * 1025 + 1024: k_ * 1025 + 1025, w_ * 256:(w_ + 1) * 256],
                   rows4[0:1, k_ * 256:(k_ + 1) * 256], s_stg)
        prevg["e2st"] = s_stg.v
        d_dt = dbg("h_dttok_%d" % w_, (128, 512))
        if d_dt is not None:
            kb.wait("sync", s_cv, s_cv.v)
            kb.dma("sync", d_dt, dt_tok[:], s_st)

    NWH = 8
    FUP = 9
    ssd_on = ph >= 31 or (3 < ph < 30)
    tail_on = ph >= 32 or (3 < ph < 30)
    for w_ in (range(NWH) if ph >= 3 else []):
        fr = hades_front(w_)
        if ssd_on:
            for tt in range(4):
                ssd_chunk(w_, tt, fr)
        if tail_on:
            hades_tail(w_, fr)
    E2_STAGED = s_stg.v

    if ph < 3 or (30 <= ph < 33):
        kb.wait("sync", s_stg, s_stg.v)
        d_e2t = dbg("e2in_t", (2050, 2048))
        if d_e2t is not None:
            kb.dma("sync", d_e2t, e2_in.ap(), s_st)
            kb.wait("sync", s_st, s_st.v)
        kb.wait("sync", s_cv, s_cv.v)
        kb.wait("sync", s_sc, s_sc.v)
        kb.wait("sync", s_pe, s_pe.v)
        kb.wait("sync", s_x, s_x.v)
        for wo_ in range(4):
            kb.dma("sync", out_mine.rearrange("(c p) t -> p c t", p=128)[:, :, wo_ * 512:(wo_ + 1) * 512],
                   big1[:].rearrange("p (c t) -> p c t", c=8), s_out)
        kb.wait("sync", s_out, s_out.v)
        e3.close()
        with nc.Block() as block:
            @block.tensor
            def _(e):
                for f in kb.ops["tensor"]:
                    f(e)
            @block.vector
            def _(e):
                for f in kb.ops["vector"]:
                    f(e)
            @block.scalar
            def _(e):
                for f in kb.ops["scalar"]:
                    f(e)
            @block.gpsimd
            def _(e):
                for f in kb.ops["gpsimd"]:
                    f(e)
            @block.sync
            def _(e):
                for f in kb.ops["sync"]:
                    f(e)
        es.close()
        return nc, dbg_aps
    kb.wait("gpsimd", s_stg, E2_STAGED)
    d_e2pre = dbg("e2in_pre", (2050, 2048))
    if d_e2pre is not None:
        kb.wait("sync", s_stg, E2_STAGED)
        kb.dma("sync", d_e2pre, e2_in.ap(), s_st)
        kb.wait("gpsimd", s_st, s_st.v)
    s_cc2.bump(1)
    kb.emit("gpsimd", lambda e: e.collective_compute(
        "AllGather", ALU.bypass, replica_groups=GROUPS,
        ins=[e2_in.ap()], outs=[e2_out.ap()]).then_inc(s_cc2.h))
    E2_DONE = s_cc2.v

    e3.close()
    ef4 = ExitStack()
    E4 = ef4.enter_context
    x1w = E4(nc.sbuf_tensor("x1w4", [128, 4096], F32))
    xb2 = E4(nc.sbuf_tensor("xb24", [128, 4096], BF16))
    xq2 = E4(nc.sbuf_tensor("xq24", [128, 4096], BF16))
    xtmp = E4(nc.sbuf_tensor("xtmp4", [128, 4096], F32))
    xint = E4(nc.sbuf_tensor("xint4", [128, 4096], BF16))
    mmx = E4(nc.sbuf_tensor("mmx4", [128, 1040], BF16))
    soutb = E4(nc.sbuf_tensor("soutb4", [128, 512], F32))
    x2w = E4(nc.sbuf_tensor("x2w4", [128, 4096], F32))
    rows3 = E4(nc.sbuf_tensor("rows34", [128, 2048], F32))
    rsw = E4(nc.sbuf_tensor("rsw", [128, 4096], F32))
    rstdr = E4(nc.sbuf_tensor("rstdr", [1, 2048], F32))
    sb_rmask = E4(nc.sbuf_tensor("sb_rmask", [128, 2], F32))

    g4 = dict(pe=s_pe.v, cv=s_cv.v, sc=s_sc.v)
    for engx in ("tensor", "vector", "scalar", "sync", "gpsimd"):
        kb.wait(engx, s_pe, g4["pe"])
        kb.wait(engx, s_cv, g4["cv"])
        kb.wait(engx, s_sc, g4["sc"])
        kb.wait(engx, s_cc2, E2_DONE)
    kb.dma("sync", sb_rmask[:], c_rmask, s_x)
    kb.dma("sync", rstdr[0:1, :], e2_out.ap()[1024:1025, :], s_x)
    kb.dma("sync", rsw[0:1, 0:2048], e2_out.ap()[3074:3075, :], s_x)
    kb.dma("sync", rsw[0:1, 2048:4096], e2_out.ap()[2049:2050, :], s_x)
    kb.dma("sync", rows3[0:1, 0:2048], e2_out.ap()[4099:4100, :], s_x)
    RSR = s_x.v
    kb.wait("vector", s_x, RSR)
    kb.v("tensor_add", rstdr[0:1, :], rstdr[0:1, :], rsw[0:1, 0:2048], sem=s_cv)
    kb.wait("vector", s_cv, s_cv.v)
    kb.v("tensor_add", rows3[0:1, 0:2048], rows3[0:1, 0:2048], rsw[0:1, 2048:4096], sem=s_cv)
    kb.wait("vector", s_cv, s_cv.v)
    kb.v("tensor_scalar", rstdr[0:1, :], rstdr[0:1, :], sb_rmask[0:1, 0:1], None, op0=ALU.mult, sem=s_cv)
    kb.wait("vector", s_cv, s_cv.v)
    kb.v("scalar_tensor_tensor", rstdr[0:1, :], rows3[0:1, 0:2048], sb_rmask[0:1, 1:2], rstdr[0:1, :],
         op0=ALU.mult, op1=ALU.add, sem=s_cv)
    kb.wait("vector", s_cv, s_cv.v)
    kb.v("tensor_scalar_mul", rstdr[0:1, :], rstdr[0:1, :], 1.0 / 1024.0, sem=s_cv)
    kb.wait("scalar", s_cv, s_cv.v)
    kb.s("activation", out=rstdr[0:1, :], in_=rstdr[0:1, :], func=AF.Ln, bias=sb_epsb[0:1, 0:1])
    kb.s("activation", out=rstdr[0:1, :], in_=rstdr[0:1, :], func=AF.Exp, scale=-0.5, sem=s_sc)
    RSTD = s_sc.v
    d_rst = dbg("p4_rstd", (1, 2048))
    if d_rst is not None:
        kb.wait("sync", s_sc, RSTD)
        kb.dma("sync", d_rst, rstdr[0:1, :], s_st)
    d_e2o = dbg("p4_e2out", (4100, 2048))
    if d_e2o is not None:
        kb.wait("sync", s_cc2, E2_DONE)
        kb.dma("sync", d_e2o, e2_out.ap(), s_st)
    d_e2i = dbg("p4_e2in", (2050, 2048))
    if d_e2i is not None:
        kb.wait("sync", s_cc2, E2_DONE)
        kb.dma("sync", d_e2i, e2_in.ap(), s_st)

    def ld_x3(w_):
        kb.soft_barrier((s_pe, s_cv, s_sc, s_out))
        kb.wait("sync", s_cv, s_cv.v)
        kb.wait("sync", s_out, s_out.v)
        kb.dma("sync", x2w[:].rearrange("p (c t) -> p c t", c=8),
               e1_in.ap().rearrange("(c p) t -> p c t", p=128)[:, :, w_ * 512:(w_ + 1) * 512], s_x)
        for cch in range(8):
            kb.dma("sync", rsw[:, cch * 512:(cch + 1) * 512],
                   e2_out.ap()[1025 + cch * 128: 1025 + (cch + 1) * 128, w_ * 512:(w_ + 1) * 512], s_x)
            kb.dma("sync", xtmp[:, cch * 512:(cch + 1) * 512],
                   e2_out.ap()[3075 + cch * 128: 3075 + (cch + 1) * 128, w_ * 512:(w_ + 1) * 512], s_x)
        kb.wait("vector", s_x, s_x.v)
        kb.v("tensor_add", rsw[:], rsw[:], xtmp[:])
        kb.v("tensor_scalar", rsw[:], rsw[:], sb_rmask[:, 1:2], None, op0=ALU.mult, sem=s_cv)
        RSB = s_cv.v
        kb.wait("sync", s_cv, RSB)
        for cch in range(8):
            kb.dma("sync", xtmp[:, cch * 512:(cch + 1) * 512],
                   e2_out.ap()[cch * 128:(cch + 1) * 128, w_ * 512:(w_ + 1) * 512], s_x)
        kb.wait("vector", s_x, s_x.v)
        kb.v("scalar_tensor_tensor", rsw[:], xtmp[:], sb_rmask[:, 0:1], rsw[:],
             op0=ALU.mult, op1=ALU.add, sem=s_cv)
        kb.wait("sync", s_cv, s_cv.v)
        for cch in range(8):
            kb.dma("sync", xtmp[:, cch * 512:(cch + 1) * 512],
                   e2_out.ap()[2050 + cch * 128: 2050 + (cch + 1) * 128, w_ * 512:(w_ + 1) * 512], s_x)
        kb.wait("vector", s_x, s_x.v)
        kb.v("scalar_tensor_tensor", rsw[:], xtmp[:], sb_rmask[:, 0:1], rsw[:],
             op0=ALU.mult, op1=ALU.add, sem=s_cv)
        XL = s_x.v
        kb.wait("tensor", s_sc, RSTD)
        kb.wait("tensor", s_cv, s_cv.v)
        kb.t("matmul", P2[:, 0:512], sb_onesf[0:1, 0:128], rstdr[0:1, w_ * 512:(w_ + 1) * 512],
             start=True, stop=True, sem=s_pe)
        RB = s_pe.v
        d_p2 = dbg("p4_p2_%d" % w_, (128, 512))
        if d_p2 is not None:
            kb.wait("scalar", s_pe, RB)
            kb.s("copy", out=soutb[:], in_=P2[:, 0:512], sem=s_sc)
            kb.wait("sync", s_sc, s_sc.v)
            kb.dma("sync", d_p2, soutb[:], s_st)
            kb.wait("scalar", s_st, s_st.v)
        d_rpre = dbg("p4_rswpre_%d" % w_, (128, 4096))
        if d_rpre is not None:
            kb.wait("sync", s_x, XL)
            kb.dma("sync", d_rpre, rsw[:], s_st)
        if d_p2 is not None or d_rpre is not None:
            kb.wait("vector", s_st, s_st.v)
        kb.wait("vector", s_pe, RB)
        kb.wait("vector", s_x, XL)
        kb.v("tensor_tensor", out=rsw[:].rearrange("p (c t) -> p c t", c=8),
             in0=rsw[:].rearrange("p (c t) -> p c t", c=8),
             in1=P2[:, 0:512].rearrange("p (o t) -> p o t", o=1).broadcast_to([128, 8, 512]), op=ALU.mult)
        kb.v("tensor_add", x1w[:], x2w[:], rsw[:], sem=s_cv)
        X3D = s_cv.v
        d_x3 = dbg("p4_x3_%d" % w_, (128, 4096))
        if d_x3 is not None:
            kb.wait("sync", s_cv, X3D)
            kb.dma("sync", d_x3, x1w[:], s_st)
        d_rs = dbg("p4_rs_%d" % w_, (128, 4096))
        if d_rs is not None:
            kb.wait("sync", s_cv, X3D)
            kb.dma("sync", d_rs, rsw[:], s_st)
        kb.wait("scalar", s_cv, X3D)
        kb.wait("vector", s_cv, X3D)
        return s_x.v

    def st_out(w_):
        kb.wait("sync", s_cv, s_cv.v)
        kb.dma("sync", out_mine.rearrange("(c p) t -> p c t", p=128)[:, :, w_ * 512:(w_ + 1) * 512],
               x2w[:].rearrange("p (c t) -> p c t", c=8), s_out)

    if ph >= 4 and ph != 33:
        emit_ffn.st_sem = s_out
        emit_ffn({35: 1, 36: 2, 37: 3}.get(ph, 4), ld_x3, st_out)
    else:
        kb.wait("sync", s_cc, s_cc.v)
        kb.wait("sync", s_cc1, s_cc1.v)
        kb.wait("sync", s_cc2, s_cc2.v)
        for wo_ in range(4):
            kb.dma("sync", out_mine.rearrange("(c p) t -> p c t", p=128)[:, :, wo_ * 512:(wo_ + 1) * 512],
                   x1w[:].rearrange("p (c t) -> p c t", c=8), s_out)
    OUT_ALL = s_out.v

    ef4.close()

    kb.wait("sync", s_st, s_st.v)
    kb.wait("sync", s_out, OUT_ALL)
    d_e2e = dbg("e2in_end", (2050, 2048))
    if d_e2e is not None:
        kb.dma("sync", d_e2e, e2_in.ap(), s_st)
        kb.wait("sync", s_st, s_st.v)
    d_e2oe = dbg("e2out_end", (4100, 2048))
    if d_e2oe is not None:
        kb.dma("sync", d_e2oe, e2_out.ap(), s_st)
        kb.wait("sync", s_st, s_st.v)

    with nc.Block() as block:
        @block.tensor
        def _(e):
            for f in kb.ops["tensor"]:
                f(e)

        @block.vector
        def _(e):
            for f in kb.ops["vector"]:
                f(e)

        @block.scalar
        def _(e):
            for f in kb.ops["scalar"]:
                f(e)

        @block.gpsimd
        def _(e):
            for f in kb.ops["gpsimd"]:
                f(e)

        @block.sync
        def _(e):
            for f in kb.ops["sync"]:
                f(e)

    es.close()
    return nc, dbg_aps


def prep_inputs(inputs):
    f32 = np.float32
    x = np.ascontiguousarray(inputs["L0_attn_a_x"], dtype=f32)
    wq = np.ascontiguousarray(inputs["L0_attn_a_Wq"], dtype=f32)
    wkv = inputs["L0_attn_a_Wkv"]
    wk = np.ascontiguousarray(wkv[:, :1024], dtype=f32)
    wv = np.ascontiguousarray(wkv[:, 1024:], dtype=f32)
    wo = np.ascontiguousarray(inputs["L0_attn_a_Wo"], dtype=f32)
    w_inp_T = np.ascontiguousarray(inputs["L0_attn_b_in_proj_w"].T, dtype=f32)
    w_hp_T = np.ascontiguousarray(inputs["L0_attn_b_h_proj_w"].T, dtype=f32)
    w_outp_T = np.ascontiguousarray(inputs["L0_attn_b_out_proj_w"].T, dtype=f32)
    pT = np.ascontiguousarray(inputs["L0_ffn_packed_weight"].reshape(256, 1024).T)
    bias = np.ascontiguousarray(inputs["L0_ffn_bias"].reshape(8, 128).T, dtype=f32)
    conv_w = inputs["L0_attn_b_conv_w"]
    conv_b = inputs["L0_attn_b_conv_b"]
    gamma = inputs["L0_attn_b_gamma"]
    alog = inputs["L0_attn_b_A_log"]
    dskip = inputs["L0_attn_b_D"]
    normw = inputs["L0_attn_b_norm_w"]
    dtb = inputs["L0_attn_b_dt_bias"]
    wsc = inputs["L0_ffn_weight_scale"]

    bf = ml_dtypes.bfloat16
    ident = np.eye(128, dtype=bf)
    onesm = np.ones((128, 128), dtype=bf)
    tri = np.tril(np.ones((128, 128), dtype=np.float32)).T.astype(bf)
    id16 = np.zeros((17, 16), np.float32)
    id16[:16, :16] = np.eye(16)
    invden = np.broadcast_to(1.0 / np.arange(1, S + 1, dtype=np.float64).astype(np.float32), (40, S)).copy()

    kl = np.arange(128)[:, None]
    ql = np.arange(256)[None, :]
    tri_rect = (kl <= ql).astype(np.float32)
    stri_rect = (kl + 128 <= ql).astype(np.float32)
    zeros = np.zeros((128, 256), np.float32)
    onesr = np.ones((128, 256), np.float32)
    dm_A = np.concatenate([tri_rect, stri_rect, zeros, zeros], axis=1).astype(bf)
    dm_B = np.concatenate([onesr, onesr, tri_rect, stri_rect], axis=1).astype(bf)

    in_maps = []
    for c in range(8):
        p, r = c // 2, c % 2
        blocks = [2 * i + r for i in range(NBLK)]
        xb = x[p]
        xt_mine = np.ascontiguousarray(
            np.concatenate([xb[j * 256:(j + 1) * 256, :] for j in blocks], axis=0).T)
        heads = list(range(8 * r, 8 * r + 8))
        zrows = np.concatenate([np.arange(64 * h, 64 * h + 64) for h in heads])
        xrows = 1024 + zrows
        bc_rows = np.arange(2048, 2304)
        dt_rows = np.arange(2304, 2336)
        sel = np.concatenate([zrows, xrows, bc_rows, dt_rows])
        w_inp_mine = np.ascontiguousarray(w_inp_T[:, sel])
        w_outp_mine = np.ascontiguousarray(w_outp_T[zrows, :])
        convsel = np.concatenate([xrows - 1024, np.arange(1024, 1280)])
        cwm = conv_w[convsel]
        cbm = conv_b[convsel]
        conv_wv = np.ascontiguousarray(cwm.reshape(6, 128, 4).transpose(1, 0, 2).reshape(128, 24), dtype=f32)
        conv_bv = np.ascontiguousarray(cbm.reshape(6, 128).T, dtype=f32)
        d_mine = np.repeat(dskip[heads], 64).reshape(4, 128).T.astype(f32)
        normw_mine = normw[zrows].reshape(4, 128).T.astype(f32)
        wo_hm = np.ascontiguousarray(wo.reshape(16, 64, 1024).transpose(1, 0, 2).reshape(64, 16 * 1024))
        selm = np.zeros((16, 8), np.float32)
        for jj, hh in enumerate(heads):
            selm[hh, jj] = 1.0
        m = dict(
            c_sel=selm, c_id8=np.eye(8, dtype=np.float32),
            c_identf=np.eye(128, dtype=np.float32),
            c_onesrow=np.ones((1, 4096), np.float32),
            c_negones=np.full((1, 4096), -1.0, np.float32),
            xt=xt_mine, wq=wq, wk=wk, wv=wv, wo=wo_hm,
            w_inp=w_inp_mine, w_hp=w_hp_T, w_outp=w_outp_mine,
            ffn_pT=pT, ffn_bias=bias,
            conv_wv=conv_wv, conv_bv=np.ascontiguousarray(conv_bv),
            dtbias_rep=np.full((128, 1), dtb[0], f32),
            gamma_rep=np.broadcast_to(gamma[:12].astype(f32), (128, 12)).copy(),
            alog_my=np.ascontiguousarray(alog[heads].reshape(8, 1), dtype=f32),
            d_my=np.ascontiguousarray(d_mine), normw_my=np.ascontiguousarray(normw_mine),
            wscale=wsc.reshape(1, 1).astype(f32),
            c_ident=ident, c_ones=onesm, c_tri=tri,
            c_diagmask=(dm_A if r == 0 else dm_B),
            c_id16=id16, c_invden=invden,
            c_rmask=np.broadcast_to(
                np.array([1.0, 0.0] if r == 0 else [0.0, 1.0], np.float32)[None, :], (128, 2)).copy(),
            c_iota12=np.broadcast_to(
                np.broadcast_to(np.arange(12, dtype=bf)[:, None], (12, 28)).reshape(1, 336),
                (128, 336)).copy(),
        )
        in_maps.append(m)
    return in_maps


def kernel(**inputs):
    in_maps = prep_inputs(inputs)
    nc, _ = build(None, ph=4)
    res = run_bass_kernel_spmd(nc, in_maps, core_ids=list(range(8)))
    out = np.zeros((4, 4096, 1024), np.float32)
    for c in range(8):
        p, r = c // 2, c % 2
        om = res.results[c]["out_mine"]
        for i in range(NBLK):
            j = 2 * i + r
            out[p, j * 256:(j + 1) * 256, :] = om[:, i * 256:(i + 1) * 256].T
    return out
